# version 05
# baseline (speedup 1.0000x reference)
import math
from contextlib import ExitStack

import numpy as np

import concourse.bass as bass
import concourse.mybir as mybir
from concourse.bass_utils import run_bass_kernel_spmd

F32 = mybir.dt.float32
BF16 = mybir.dt.bfloat16
AF = mybir.ActivationFunctionType
ALU = mybir.AluOpType
AX = mybir.AxisListType.X

B, T, D, H, HD, NE, FF = 2, 8192, 1024, 8, 128, 8, 2048
NJ = 16
NT = 64
DC = D // 128
FT = FF // 128
SCALE = 1.0 / math.sqrt(HD)
EPS = 1e-5
NEG = -1.0e30
ENGINES = ("sync", "tensor", "vector", "scalar", "gpsimd")


class Builder:

    def __init__(self, nc, ctx):
        self.nc = nc
        self.ctx = ctx
        self.ops = {e: [] for e in ENGINES}
        self.h = {}
        self.cnt = {}

    sem_name = {}

    def sem(self, name):
        if name not in self.h:
            self.h[name] = self.ctx.enter_context(self.nc.semaphore(name))
            self.cnt[name] = 0
            self.sem_name[id(self.h[name])] = name
        return self.h[name]

    def cur(self, name):
        self.sem(name)
        return self.cnt[name]

    def add(self, engine, make, waits=(), incs=()):
        wl = [(self.sem(n), v) for n, v in waits if v > 0]
        il = []
        for n, k in incs:
            self.sem(n)
            self.cnt[n] += k
            il.append((self.h[n], k))
        self.ops[engine].append((wl, make, il))

    def wall(self, *names):
        return [(n, self.cur(n)) for n in names]

    marks = {}

    def mark(self, key, name):
        self.marks[key] = (name, self.cur(name))

    def mwait(self, key):
        return [self.marks[key]] if key in self.marks else []

    def emit_block(self):
        ops = self.ops
        self.ops = {e: [] for e in ENGINES}

        def replay(engine, eng):
            via_drain = engine in ("vector", "scalar")
            for wl, make, il in ops[engine]:
                for s, v in wl:
                    eng.wait_ge(s, v)
                inst = make(eng)
                if engine == "tensor":
                    assert len(il) <= 1, "tensor ops carry at most one inc"
                for i, (s, k) in enumerate(il):
                    if i == 0 and not via_drain:
                        inst.then_inc(s, k)
                    elif i == 0 and via_drain:
                        eng.drain().then_inc(s, k)
                    else:
                        eng.nop().then_inc(s, k)

        with self.nc.Block() as block:
            @block.sync
            def _(eng):
                replay("sync", eng)

            @block.tensor
            def _(eng):
                replay("tensor", eng)

            @block.vector
            def _(eng):
                replay("vector", eng)

            @block.scalar
            def _(eng):
                replay("scalar", eng)

            @block.gpsimd
            def _(eng):
                replay("gpsimd", eng)


def build_nc(stop=None):
    nc = bass.Bass("TRN2", target_bir_lowering=False,
                   detect_race_conditions=False)
    dt = nc.dram_tensor
    if stop is not None:
        dbgK = dt("dbgK", [128, T], F32, kind="ExternalOutput").ap()
        dbgV = dt("dbgV", [128, NT * 128], F32, kind="ExternalOutput").ap()
        dbgL = dt("dbgL", [128, 2048], F32, kind="ExternalOutput").ap()
        dbgI = dt("dbgI", [128, 256], F32, kind="ExternalOutput").ap()
        dbgF = dt("dbgF", [128, 8192], F32, kind="ExternalOutput").ap()
        dbgY = dt("dbgY", [128, 8192], F32, kind="ExternalOutput").ap()
        dbgP = dt("dbgP", [128, 4096], F32, kind="ExternalOutput").ap()
        dbgH = dt("dbgH", [128, FT * 1024], F32, kind="ExternalOutput").ap()

    xq = dt("xq", [NJ, 128, D], F32, kind="ExternalInput").ap()
    xk = dt("xk", [NT, 128, D], F32, kind="ExternalInput").ap()
    wq = dt("wq", [128, H, DC, 128], F32, kind="ExternalInput").ap()
    wkv = dt("wkv", [128, 2, DC, 128], F32, kind="ExternalInput").ap()
    wo = dt("wo", [128, H, DC, 128], F32, kind="ExternalInput").ap()
    wg = dt("wg", [NE, 128, FT, DC, 128], F32, kind="ExternalInput").ap()
    wu = dt("wu", [NE, 128, FT, DC, 128], F32, kind="ExternalInput").ap()
    wd = dt("wd", [NE, 128, FT, DC, 128], F32, kind="ExternalInput").ap()
    rw = dt("rw", [128, DC, NE], F32, kind="ExternalInput").ap()
    bqr = dt("bqr", [1, H, 128], F32, kind="ExternalInput").ap()
    bkvr = dt("bkvr", [1, 2, 128], F32, kind="ExternalInput").ap()
    bgT = dt("bgT", [128, NE, FT], F32, kind="ExternalInput").ap()
    buT = dt("buT", [128, NE, FT], F32, kind="ExternalInput").ap()
    bdr = dt("bdr", [8, DC, 128], F32, kind="ExternalInput").ap()
    pek = dt("pek", [128, T], F32, kind="ExternalInput").ap()
    peo = dt("peo", [128, NJ, 128], F32, kind="ExternalInput").ap()
    maskin = dt("maskin", [128, 512], F32, kind="ExternalInput").ap()
    idbin = dt("idbin", [128, 128], F32, kind="ExternalInput").ap()
    idfin = dt("idfin", [128, 128], F32, kind="ExternalInput").ap()
    onesin = dt("onesin", [1, 128], F32, kind="ExternalInput").ap()
    eselin = dt("eselin", [8, NE, 128], F32, kind="ExternalInput").ap()
    outp = dt("out", [NJ, 128, D], F32, kind="ExternalOutput").ap()
    yspill = dt("yspill", [128, NJ, D], F32, kind="Internal").ap()

    ctx = ExitStack()
    bld = Builder(nc, ctx)
    add, cur, wall = bld.add, bld.cur, bld.wall

    def sbt(stack, name, shape, dtp):
        return stack.enter_context(nc.sbuf_tensor(name, shape, dtp)).ap()

    def pst(stack, name):
        return stack.enter_context(nc.psum_tensor(name, [128, 512], F32)).ap()

    idb = sbt(ctx, "idb", [128, 128], BF16)
    idf = sbt(ctx, "idf", [128, 128], F32)
    ones_b = sbt(ctx, "ones_b", [1, 128], BF16)
    mask_sb = sbt(ctx, "mask_sb", [128, 512], F32)
    sqscr = sbt(ctx, "sqscr", [128, 1024], BF16)
    lnsum = sbt(ctx, "lnsum", [128, 1], F32)
    lnssq = sbt(ctx, "lnssq", [128, 1], F32)
    lnmean = sbt(ctx, "lnmean", [128, 1], F32)
    lnmsq = sbt(ctx, "lnmsq", [128, 1], F32)
    veps = sbt(ctx, "veps", [128, 1], F32)
    lnt = sbt(ctx, "lnt", [128, 1], F32)
    rstd = sbt(ctx, "rstd", [128, 1], F32)
    nmr = sbt(ctx, "nmr", [128, 1], F32)
    nm = sbt(ctx, "nm", [128, 1], F32)

    lncount = [0]

    def vch(make, extra=(), incs=()):
        add("vector", make, waits=list(extra) + wall("s_vch"),
            incs=[("s_vch", 1)] + list(incs))

    def sch(make, extra=(), incs=()):
        add("scalar", make, waits=list(extra) + wall("s_sch"),
            incs=[("s_sch", 1)] + list(incs))

    def emit_ln(x_ap, out_ap, dep_waits, apply_waits=(), apply_incs=()):
        lncount[0] += 1
        w0 = list(dep_waits) + wall("s_lnapply")
        vch(lambda v, x=x_ap: v.reduce_sum(
            lnsum[:, :], x[:, :], axis=AX), extra=w0)
        sch(lambda s, x=x_ap: s.activation(
            sqscr[:, :], x[:, :], AF.Square, accum_out=lnssq[:, 0:1]),
            extra=list(dep_waits) + wall("s_lnnmr"),
            incs=[("s_lnsq", 1)])
        vch(lambda v: v.tensor_scalar(
            lnmean[:, :], lnsum[:, :], 1.0 / D, None, ALU.mult))
        vch(lambda v: v.tensor_mul(lnmsq[:, :], lnmean[:, :], lnmean[:, :]))
        vch(lambda v: v.scalar_tensor_tensor(
            veps[:, :], lnssq[:, :], 1.0 / D, lnmsq[:, :],
            ALU.mult, ALU.subtract),
            extra=wall("s_lnsq"))
        vch(lambda v: v.tensor_scalar(
            veps[:, :], veps[:, :], EPS, None, ALU.add),
            incs=[("s_lnstat", 1)])
        sch(lambda s: s.activation(lnt[:, :], veps[:, :], AF.Ln),
            extra=wall("s_lnstat"))
        sch(lambda s: s.activation(rstd[:, :], lnt[:, :], AF.Exp, scale=-0.5),
            incs=[("s_lnrstd", 1)])
        vch(lambda v: v.scalar_tensor_tensor(
            nmr[:, :], lnmean[:, :], -1.0, rstd[:, :], ALU.mult, ALU.mult),
            extra=wall("s_lnrstd"), incs=[("s_lnnmr", 1)])
        sch(lambda s, x=x_ap, o=out_ap: s.activation(
            o, x, AF.Identity, bias=nmr[:, 0:1], scale=rstd[:, 0:1]),
            extra=list(apply_waits) + wall("s_lnnmr"),
            incs=[("s_lnapply", 1)] + list(apply_incs))

    p1c = ExitStack()
    x_own = sbt(p1c, "x_own", [128, NJ, D], F32)
    wq_sb = sbt(p1c, "wq_sb", [128, H, DC, 128], BF16)
    wo_sb = sbt(p1c, "wo_sb", [128, H, DC, 128], BF16)
    wkv_sb = sbt(p1c, "wkv_sb", [128, 2, DC, 128], BF16)
    bq_sb = sbt(p1c, "bq_sb", [1, H, 128], BF16)
    bkv_sb = sbt(p1c, "bkv_sb", [1, 2, 128], BF16)
    K_sb = sbt(p1c, "K_sb", [128, T], BF16)
    V_sb = sbt(p1c, "V_sb", [128, NT, 128], BF16)

    p1a = ExitStack()
    x_kv = sbt(p1a, "x_kv", [128, 2, D], F32)
    pek_sb = sbt(p1a, "pek_sb", [128, 2, 512], F32)
    ln_kv = sbt(p1a, "ln_kv", [128, 2, D], BF16)
    lnT_kv = sbt(p1a, "lnT_kv", [128, 2, DC, 128], BF16)
    vt_tmp = sbt(p1a, "vt_tmp", [128, 2, 128], BF16)
    ptr = p1a.enter_context(nc.psum_tensor("ptr", [128, 2048], F32)).ap()
    pkv = p1a.enter_context(nc.psum_tensor("pkv", [128, 1024], F32)).ap()

    add("sync", lambda s: s.dma_start(idf[:, :], idfin), incs=[("s_c", 16)])
    add("sync", lambda s: s.dma_start(mask_sb[:, :], maskin), incs=[("s_c", 16)])
    add("gpsimd", lambda g: g.dma_start(idb[:, :], idbin), incs=[("s_cg", 16)])
    add("gpsimd", lambda g: g.dma_start(ones_b[:, :], onesin), incs=[("s_cg", 16)])
    add("gpsimd", lambda g: g.dma_start(wq_sb[:, :, :, :], wq), incs=[("s_w", 16)])
    add("gpsimd", lambda g: g.dma_start(wkv_sb[:, :, :, :], wkv), incs=[("s_w", 16)])
    add("gpsimd", lambda g: g.dma_start(wo_sb[:, :, :, :], wo), incs=[("s_w", 16)])
    add("gpsimd", lambda g: g.dma_start(bq_sb[:, :, :], bqr), incs=[("s_w", 16)])
    add("gpsimd", lambda g: g.dma_start(bkv_sb[:, :, :], bkvr), incs=[("s_w", 16)])
    for j in range(NJ):
        add("sync", lambda s, j=j: s.dma_start(x_own[:, j, :], xq[j]),
            incs=[("s_xq", 16)])

    for j in range(NJ):
        jp = j % 2
        add("sync", lambda s, j=j, jp=jp: s.dma_start(
            pek_sb[:, jp, :], pek[:, 512 * j:512 * (j + 1)]),
            waits=[("s_pek_r", j - 1)] if j >= 2 else (),
            incs=[("s_pek%d" % jp, 16)])
        for ti in range(4):
            tt = 4 * j + ti
            tp = tt % 2
            add("sync", lambda s, tt=tt, tp=tp: s.dma_start(
                x_kv[:, tp, :], xk[tt]),
                waits=[("s_xkv_r", tt - 1)] if tt >= 2 else (),
                incs=[("s_xkv%d" % tp, 16)])
            emit_ln(x_kv[:, tp, :], ln_kv[:, tp, :],
                    wall("s_xkv%d" % tp),
                    apply_waits=bld.mwait(("lnkv", tt - 2)),
                    apply_incs=[("s_xkv_r", 1)])
            for half in range(2):
                for c4 in range(4):
                    c = 4 * half + c4
                    w = []
                    if c4 == 0:
                        w = wall("s_ptr_r")
                        if tt == 0 and half == 0:
                            w += wall("s_c", "s_cg")
                        if half == 0:
                            w += wall("s_lnapply")
                    add("tensor", lambda t, tp=tp, c=c, c4=c4: t.matmul(
                        ptr[:, c4 * 512:c4 * 512 + 128],
                        ln_kv[:, tp, c * 128:(c + 1) * 128], idb[:, :],
                        start=True, stop=True),
                        waits=w, incs=[("s_ptr_w", 1)])
                    if half == 1 and c4 == 3:
                        bld.mark(("lnkv", tt), "s_ptr_w")
                add("scalar", lambda s, tp=tp, half=half: s.copy(
                    lnT_kv[:, tp, 4 * half:4 * half + 4, :],
                    ptr.rearrange("p (b w) -> p b w", w=512)[:, :, 0:128]),
                    waits=wall("s_ptr_w"), incs=[("s_ptr_r", 1)])
            for f in range(2):
                for c in range(DC):
                    w = []
                    if f == 0 and c == 0:
                        w = wall("s_ptr_r", "s_w") + \
                            ([("s_kvk_r", tt)] if tt >= 1 else [])
                    if f == 1 and c == 0:
                        w = [("s_kvv_r", tt)] if tt >= 1 else []
                    add("tensor", lambda t, f=f, c=c, tp=tp: t.matmul(
                        pkv[:, f * 512:f * 512 + 128],
                        wkv_sb[:, f, c, :], lnT_kv[:, tp, c, :],
                        start=(c == 0), stop=False), waits=w)
                add("tensor", lambda t, f=f: t.matmul(
                    pkv[:, f * 512:f * 512 + 128],
                    bkv_sb[:, f, :], ones_b[:, :], start=False, stop=True),
                    incs=[("s_kvk_w" if f == 0 else "s_kvv_w", 1)])
            add("vector", lambda v, tt=tt, jp=jp, ti=ti: v.scalar_tensor_tensor(
                K_sb[:, tt * 128:(tt + 1) * 128], pkv[:, 0:128], 1.0,
                pek_sb[:, jp, ti * 128:(ti + 1) * 128], ALU.mult, ALU.add),
                waits=wall("s_kvk_w", "s_pek%d" % jp),
                incs=[("s_kv", 1), ("s_kvk_r", 1)] +
                     ([("s_pek_r", 1)] if ti == 3 else []))
            add("scalar", lambda s, tp=tp: s.copy(
                vt_tmp[:, tp, :], pkv[:, 512:640]),
                waits=wall("s_kvv_w"),
                incs=[("s_vt", 1), ("s_kvv_r", 1)])
            add("tensor", lambda t, tp=tp: t.matmul(
                ptr[:, 0:128], vt_tmp[:, tp, :], idb[:, :],
                start=True, stop=True),
                waits=wall("s_vt", "s_ptr_r"),
                incs=[("s_ptr_w", 1)])
            add("vector", lambda v, tt=tt: v.tensor_copy(
                V_sb[:, tt, :], ptr[:, 0:128]),
                waits=wall("s_ptr_w"),
                incs=[("s_kv", 1), ("s_ptr_r", 1)])

    for e in ENGINES:
        add(e, lambda eng: eng.nop(), incs=[("s_p1a", 1)])

    if stop is not None:
        add("gpsimd", lambda g: g.dma_start(dbgK, K_sb[:, :]),
            waits=[("s_p1a", 5)], incs=[("s_dbg_st", 16)])
        add("gpsimd", lambda g: g.dma_start(
            dbgV, V_sb.rearrange("p a b -> p (a b)")),
            incs=[("s_dbg_st", 16)])
        add("gpsimd", lambda g: g.dma_start(
            dbgL[:, 0:1024], ln_kv[:, 1, :]), incs=[("s_dbg_st", 16)])
        add("gpsimd", lambda g: g.dma_start(
            dbgL[:, 1024:2048],
            lnT_kv[:, 1, :, :].rearrange("p c f -> p (c f)")),
            incs=[("s_dbg_st", 16)])
        add("gpsimd", lambda g: g.dma_start(dbgI[:, 0:128], idb[:, :]),
            incs=[("s_dbg_st", 16)])
        add("gpsimd", lambda g: g.dma_start(dbgI[:, 128:256], vt_tmp[:, 1, :]),
            incs=[("s_dbg_st", 16)])
        if stop == "kv":
            for j in range(NJ):
                add("sync", lambda s, j=j: s.dma_start(
                    outp[j], x_own[:, j, :]),
                    waits=[("s_xq", 16 * NJ)], incs=[("s_out", 16)])
            add("sync", lambda s: s.wait_ge(bld.sem("s_out"), cur("s_out")))
            add("sync", lambda s: s.wait_ge(bld.sem("s_dbg_st"), cur("s_dbg_st")))
            bld.emit_block()
            p1a.close()
            p1c.close()
            ctx.close()
            return nc

    bld.emit_block()
    p1a.close()

    bar1 = wall("s_p1a")
    p1b = ExitStack()
    peo_sb = sbt(p1b, "peo_sb", [128, NJ, 128], F32)
    scores = sbt(p1b, "scores", [128, T], BF16)
    probs = sbt(p1b, "probs", [128, T], BF16)
    ln_own = sbt(p1b, "ln_own", [128, D], BF16)
    ln1T = sbt(p1b, "ln1T", [128, 2, DC, 128], BF16)
    qT_sb = sbt(p1b, "qT_sb", [128, H, 128], BF16)
    pT_sb = sbt(p1b, "pT_sb", [128, 2, 512], BF16)
    attnT = sbt(p1b, "attnT", [128, 2, H, 128], BF16)
    nmb = sbt(p1b, "nmb", [128, 16], F32)
    ssum = sbt(p1b, "ssum", [128, 1], F32)
    rs = sbt(p1b, "rs", [128, 1], F32)
    diag = sbt(p1b, "diag", [128, 128], BF16)
    tmpf = sbt(p1b, "tmpf", [128, 512], F32)

    pqk = p1b.enter_context(nc.psum_tensor("pqk", [128, 1024], F32)).ap()
    pat = [p1b.enter_context(nc.psum_tensor("pat0", [128, 128], F32)).ap(),
           p1b.enter_context(nc.psum_tensor("pat1", [128, 128], F32)).ap()]
    ptq = p1b.enter_context(nc.psum_tensor("ptq", [128, 2048], F32)).ap()

    add("sync", lambda s: s.dma_start(peo_sb[:, :, :], peo),
        waits=bar1, incs=[("s_peo", 16)])

    first1 = {"tensor": True, "vector": True}

    def bw1(eng, extra=()):
        w = list(extra)
        if first1.get(eng):
            w += bar1
            first1[eng] = False
        return w

    import kernel as _selfmod
    for j in range(getattr(_selfmod, "ATTN_J", NJ)):
        jp = j % 2
        kext = 512 * (j + 1)
        nkb = j + 1
        emit_ln(x_own[:, j, :], ln_own[:, :], bw1("vector", wall("s_xq")),
                apply_waits=bld.mwait(("lnown", j - 1)))
        for half in range(2):
            for c4 in range(4):
                c = 4 * half + c4
                w = []
                if c4 == 0:
                    w = bw1("tensor", wall("s_ptq_r"))
                    if half == 0:
                        w += wall("s_lnapply")
                add("tensor", lambda t, c=c, c4=c4: t.matmul(
                    ptq[:, c4 * 512:c4 * 512 + 128],
                    ln_own[:, c * 128:(c + 1) * 128], idb[:, :],
                    start=True, stop=True),
                    waits=w, incs=[("s_ptq_w", 1)])
                if half == 1 and c4 == 3:
                    bld.mark(("lnown", j), "s_ptq_w")
            add("scalar", lambda s, jp=jp, half=half: s.copy(
                ln1T[:, jp, 4 * half:4 * half + 4, :],
                ptq.rearrange("p (b w) -> p b w", w=512)[:, :, 0:128]),
                waits=wall("s_ptq_w"), incs=[("s_ptq_r", 1), ("s_ln1T", 1)])

        STG = getattr(_selfmod, "ATTN_STAGE", 5)
        def emit_qproj(h):
            seq = 8 * j + h
            for c in range(DC):
                w = wall("s_ptq_r", "s_ln1T", "s_w") if c == 0 else []
                add("tensor", lambda t, h=h, c=c, jp=jp: t.matmul(
                    ptq[:, 0:128], wq_sb[:, h, c, :], ln1T[:, jp, c, :],
                    start=(c == 0), stop=False), waits=w)
            add("tensor", lambda t, h=h: t.matmul(
                ptq[:, 0:128], bq_sb[:, h, :], ones_b[:, :],
                start=False, stop=True), incs=[("s_ptq_w", 1)])
            add("vector", lambda v, h=h, j=j: v.scalar_tensor_tensor(
                qT_sb[:, h, :], ptq[:, 0:128], SCALE, peo_sb[:, j, :],
                ALU.mult, ALU.add),
                waits=wall("s_ptq_w", "s_peo") + bld.mwait(("qTr", seq - 8)),
                incs=[("s_qT", 1), ("s_ptq_r", 1)])

        def emit_front(h):
            hp = h % 2
            seq = 8 * j + h
            if STG < 3:
                return
            nfull = nkb - 1
            for kb in range(nkb):
                bp = kb % 2
                w = [("s_qT", 8 * j + h + 1),
                     ("s_qkr", cur("s_qkr"))]
                if seq == 0 and kb == 0:
                    w += [("s_kv", 2 * NT)]
                add("tensor", lambda t, h=h, kb=kb, bp=bp: t.matmul(
                    pqk[:, bp * 512:(bp + 1) * 512], qT_sb[:, h, :],
                    K_sb[:, kb * 512:(kb + 1) * 512],
                    start=True, stop=True),
                    waits=w, incs=[("s_qkw", 1)])
                if kb == nkb - 1:
                    bld.mark(("qTr", seq), "s_qkw")
                if kb < nfull and (kb % 2 == 1 or kb == nfull - 1):
                    lo = kb - (kb % 2)
                    width = (kb % 2 + 1) * 512
                    add("vector", lambda v, lo=lo, width=width: v.tensor_scalar(
                        scores[:, lo * 512:lo * 512 + width],
                        pqk[:, 0:width],
                        1.0, None, ALU.mult, op1=ALU.max,
                        accum_out=nmb[:, lo // 2:lo // 2 + 1]),
                        waits=wall("s_qkw", "s_exp"),
                        incs=[("s_qkr", 1)])
                elif kb == nkb - 1:
                    bp2 = kb % 2
                    add("vector", lambda v, kb=kb, bp2=bp2: v.scalar_tensor_tensor(
                        scores[:, kb * 512:(kb + 1) * 512],
                        pqk[:, bp2 * 512:(bp2 + 1) * 512],
                        1.0, mask_sb[:, :], ALU.mult, ALU.add),
                        waits=wall("s_qkw", "s_exp", "s_c"),
                        incs=[("s_qkr", 1)])
                    add("vector", lambda v, kb=kb: v.reduce_max(
                        nmb[:, 15:16],
                        scores[:, kb * 512:(kb + 1) * 512], axis=AX),
                        waits=wall("s_qkr"))
            nm_n = (nfull + 1) // 2
            add("vector", lambda v, nm_n=nm_n: v.reduce_max(
                nm[:, :],
                nmb[:, 0:nm_n] if nm_n > 0 else nmb[:, 15:16],
                axis=AX, negate=True) if nm_n == 0 else v.tensor_tensor(
                nm[:, :], nmb[:, 15:16],
                nmb[:, 15:16], ALU.max),
                waits=wall("s_qkr")) if False else None
            def mk_combine(v, nm_n=nm_n):
                if nm_n > 0:
                    v.tensor_copy(nmb[:, 14:15], nmb[:, 15:16])
                    return v.reduce_max(nm[:, :], nmb[:, 0:nm_n], axis=AX,
                                        negate=True) if False else None
            add("vector", lambda v, nm_n=nm_n: v.tensor_copy(
                nmb[:, nm_n:nm_n + 1], nmb[:, 15:16]),
                waits=wall("s_qkr")) if nm_n < 15 else None
            add("vector", lambda v, nm_n=nm_n: v.reduce_max(
                nm[:, :], nmb[:, 0:nm_n + 1], axis=AX, negate=True),
                incs=[("s_nm", 1)])

        def emit_exp(h):
            seq = 8 * j + h
            add("scalar", lambda s, kext=kext: s.activation(
                probs[:, 0:kext], scores[:, 0:kext], AF.Exp,
                bias=nm[:, 0:1], accum_out=ssum[:, 0:1]),
                waits=[("s_nm", seq + 1)] + bld.mwait(("T", seq - 1)),
                incs=[("s_exp", 1)])

        def emit_tail(h):
            hp = h % 2
            seq = 8 * j + h
            add("vector", lambda v: v.reciprocal(rs[:, :], ssum[:, :]),
                waits=[("s_exp", seq + 1)], incs=[("s_rs", 1)])
            add("vector", lambda v: v.tensor_scalar(
                diag[:, :], idb[:, :], rs[:, 0:1], None, ALU.mult),
                waits=[("s_rs", seq + 1)] + bld.mwait(("T", seq - 1)) +
                      (wall("s_cg") if seq == 0 else []),
                incs=[("s_diag", 1)])
            if STG < 4:
                return
            SUB = getattr(_selfmod, "ATTN_SUB", 3)
            for kb in range(nkb):
                tp = kb % 2
                for c4 in range(4):
                    w = []
                    if c4 == 0:
                        w = [("s_diag", seq + 1), ("s_exp", seq + 1)] + \
                            wall("s_ptq_r")
                    add("tensor", lambda t, kb=kb, c4=c4: t.matmul(
                        ptq[:, c4 * 512:c4 * 512 + 128],
                        probs[:, kb * 512 + c4 * 128:kb * 512 + (c4 + 1) * 128],
                        diag[:, :], start=True, stop=True),
                        waits=w, incs=[("s_ptq_w", 1)])
                    if kb == nkb - 1 and c4 == 3:
                        bld.mark(("T", seq), "s_ptq_w")
                add("scalar", lambda s, tp=tp: s.copy(
                    pT_sb[:, tp, :],
                    ptq.rearrange("p (b w) -> p b w", w=512)[:, :, 0:128]),
                    waits=wall("s_ptq_w") +
                          [("s_pta%d" % tp, cur("s_pta%d" % tp))],
                    incs=[("s_ptq_r", 1), ("s_pte%d" % tp, 1)])
                if SUB < 2:
                    continue
                for c4 in range(4):
                    w = []
                    if c4 == 0:
                        w = wall("s_pte%d" % tp)
                        if kb == 0:
                            w += [("s_atr%d" % hp, cur("s_atr%d" % hp))]
                    add("tensor", lambda t, kb=kb, c4=c4, tp=tp, hp=hp,
                        last=(kb == nkb - 1): t.matmul(
                        pat[hp][:, :], V_sb[:, 4 * kb + c4, :],
                        pT_sb[:, tp, c4 * 128:(c4 + 1) * 128],
                        start=(kb == 0 and c4 == 0),
                        stop=(last and c4 == 3)),
                        waits=w,
                        incs=[("s_pta%d" % tp, 1)] if c4 == 3 else [])
            if SUB >= 3:
                add("vector", lambda v, jp=jp, h=h, hp=hp: v.tensor_copy(
                    attnT[:, jp, h, :], pat[hp][:, :]),
                    waits=wall("s_pta0", "s_pta1"),
                    incs=[("s_atr%d" % hp, 1), ("s_at", 1)])

        STG = getattr(_selfmod, "ATTN_STAGE", 5)
        for h in range(H if STG >= 2 else 0):
            emit_qproj(h)
        for h in range(H if STG >= 2 else 0):
            emit_front(h)
            if STG >= 3 and h >= 1:
                emit_tail(h - 1)
            if STG >= 3:
                emit_exp(h)
        if STG >= 3:
            emit_tail(H - 1)

        for half in range(2 if STG >= 5 else 0):
            for d4 in range(4):
                dtile = 4 * half + d4
                for h in range(H):
                    w = []
                    if d4 == 0 and h == 0:
                        w = wall("s_at", "s_ptq_r")
                    add("tensor", lambda t, h=h, dtile=dtile, d4=d4, jp=jp: t.matmul(
                        ptq[:, d4 * 512:d4 * 512 + 128],
                        wo_sb[:, h, dtile, :], attnT[:, jp, h, :],
                        start=(h == 0), stop=(h == H - 1)),
                        waits=w,
                        incs=[("s_ptq_w", 1)] if (d4 == 3 and h == H - 1) else ())
            add("scalar", lambda s: s.copy(
                tmpf[:, :],
                ptq.rearrange("p (b w) -> p b w", w=512)[:, :, 0:128]),
                waits=wall("s_ptq_w"),
                incs=[("s_ptq_r", 1), ("s_tmpf", 1)])
            for d4 in range(4):
                w = []
                if d4 == 0:
                    w = wall("s_tmpf", "s_ptq_r")
                add("tensor", lambda t, d4=d4: t.matmul(
                    ptq[:, d4 * 512:d4 * 512 + 128],
                    tmpf[:, d4 * 128:(d4 + 1) * 128], idf[:, :],
                    start=True, stop=True),
                    waits=w, incs=[("s_ptq_w", 1)])
            add("vector", lambda v, j=j, half=half: v.scalar_tensor_tensor(
                x_own[:, j, half * 512:(half + 1) * 512].rearrange(
                    "p (c f) -> p c f", f=128),
                ptq.rearrange("p (b w) -> p b w", w=512)[:, :, 0:128], 1.0,
                x_own[:, j, half * 512:(half + 1) * 512].rearrange(
                    "p (c f) -> p c f", f=128),
                ALU.mult, ALU.add),
                waits=wall("s_ptq_w"),
                incs=[("s_ptq_r", 1), ("s_res", 1)])

    if stop == "attn":
        for j in range(NJ):
            add("sync", lambda s, j=j: s.dma_start(outp[j], x_own[:, j, :]),
                waits=wall("s_res") + [("s_xq", 16 * NJ)], incs=[("s_out", 16)])
        add("sync", lambda s: s.wait_ge(bld.sem("s_out"), cur("s_out")))
        add("sync", lambda s: s.wait_ge(bld.sem("s_dbg_st"), cur("s_dbg_st")))
        bld.emit_block()
        p1b.close()
        p1c.close()
        ctx.close()
        return nc

    add("sync", lambda s: s.dma_start(yspill, x_own[:, :, :]),
        waits=wall("s_res"), incs=[("s_yspill", 16)])
    for e in ENGINES:
        add(e, lambda eng: eng.nop(), incs=[("s_p1", 1)])

    bld.emit_block()
    p1b.close()
    p1c.close()

    bar = wall("s_p1", "s_yspill")
    p2 = ExitStack()
    ln2T = sbt(p2, "ln2T", [128, DC, 1024], BF16)
    h_sb = sbt(p2, "h_sb", [128, FT, 1024], BF16)
    yffn = sbt(p2, "yffn", [128, DC, 1024], F32)
    wg_sb = sbt(p2, "wg_sb", [128, 2, 4, DC, 128], BF16)
    wu_sb = sbt(p2, "wu_sb", [128, 2, 4, DC, 128], BF16)
    wd_sb = sbt(p2, "wd_sb", [128, FT, DC, 128], BF16)
    pb_sb = sbt(p2, "pb_sb", [128, 2, 1024], BF16)
    pf_sb = sbt(p2, "pf_sb", [128, 8, NE], BF16)
    prT = sbt(p2, "prT", [8, 1024], BF16)
    y_rot = sbt(p2, "y_rot", [128, 2, D], F32)
    ln2_tmp = sbt(p2, "ln2_tmp", [128, D], BF16)
    silu_sb = sbt(p2, "silu_sb", [128, 512], BF16)
    u_sb = sbt(p2, "u_sb", [128, 512], BF16)
    h1_sb = sbt(p2, "h1_sb", [128, 512], BF16)
    fin_sb = sbt(p2, "fin_sb", [128, 2, D], F32)
    rw_sb = sbt(p2, "rw_sb", [128, DC, NE], BF16)
    esel_sb = sbt(p2, "esel_sb", [8, NE, 128], BF16)
    bd_sb = sbt(p2, "bd_sb", [8, DC, 128], BF16)
    bgT_sb = sbt(p2, "bgT_sb", [128, NE, FT], F32)
    buT_sb = sbt(p2, "buT_sb", [128, NE, FT], F32)
    l_sb = sbt(p2, "l_sb", [128, NE], F32)
    E_sb = sbt(p2, "E_sb", [128, NE], F32)
    t8_sb = sbt(p2, "t8_sb", [128, 8], F32)
    keep_sb = sbt(p2, "keep_sb", [128, NE], F32)
    pm_sb = sbt(p2, "pm_sb", [128, NE], F32)
    Z_sb = sbt(p2, "Z_sb", [128, 1], F32)
    S2_sb = sbt(p2, "S2_sb", [128, 1], F32)
    den_sb = sbt(p2, "den_sb", [128, 1], F32)
    rden_sb = sbt(p2, "rden_sb", [128, 1], F32)

    pg = [pst(p2, "pg0"), pst(p2, "pg1")]
    pu = [pst(p2, "pu0"), pst(p2, "pu1")]
    py = [pst(p2, "py0"), pst(p2, "py1")]
    pxq = p2.enter_context(nc.psum_tensor("pxq", [128, 1024], F32)).ap()
    pxv = pxq.rearrange("p (b w) -> p b w", w=512)

    add("sync", lambda s: s.dma_start(bgT_sb[:, :, :], bgT),
        waits=bar, incs=[("s_c2", 16)])
    add("sync", lambda s: s.dma_start(buT_sb[:, :, :], buT), incs=[("s_c2", 16)])
    add("gpsimd", lambda g: g.dma_start(rw_sb[:, :, :], rw),
        waits=bar, incs=[("s_c2g", 16)])
    add("gpsimd", lambda g: g.dma_start(esel_sb[:, :, :], eselin),
        incs=[("s_c2g", 16)])
    add("gpsimd", lambda g: g.dma_start(bd_sb[:, :, :], bdr),
        incs=[("s_c2g", 16)])

    first = {"tensor": True, "vector": True}

    def bw(eng, extra=()):
        w = list(extra)
        if first.get(eng):
            w += bar
            first[eng] = False
        return w

    for g in range(2):
        for jl in range(8):
            jt = g * 8 + jl
            tp = jt % 2
            yn = (cur("s_y0") + cur("s_y1")) // 16
            w = ([("s_y_r", yn - 1)] if yn >= 2 else []) + \
                (wall("s_yspill") if (g == 0 and jl == 0) else [])
            add("sync", lambda s, jt=jt, tp=tp: s.dma_start(
                y_rot[:, tp, :], yspill[:, jt, :]), waits=w,
                incs=[("s_y%d" % tp, 16)])
            emit_ln(y_rot[:, tp, :], ln2_tmp[:, :],
                    bw("vector", wall("s_y%d" % tp)),
                    apply_waits=bld.mwait(("ln2tmp", jt - 1)),
                    apply_incs=[("s_y_r", 1)])
            for pr in range(4):
                for c2 in range(2):
                    c = 2 * pr + c2
                    w = []
                    if c2 == 0:
                        w = bw("tensor", wall("s_pxq_r"))
                        if pr == 0:
                            w += wall("s_lnapply")
                    add("tensor", lambda t, c=c, c2=c2: t.matmul(
                        pxq[:, c2 * 512:c2 * 512 + 128],
                        ln2_tmp[:, c * 128:(c + 1) * 128], idb[:, :],
                        start=True, stop=True),
                        waits=w, incs=[("s_pxq_w", 1)])
                    if pr == 3 and c2 == 1:
                        bld.mark(("ln2tmp", jt), "s_pxq_w")
                add("scalar", lambda s, jl=jl, pr=pr: s.copy(
                    ln2T[:, 2 * pr:2 * pr + 2, jl * 128:(jl + 1) * 128],
                    pxv[:, :, 0:128]),
                    waits=wall("s_pxq_w"),
                    incs=[("s_pxq_r", 1), ("s_ln2T", 1)])

        for jl in range(8):
            for c in range(DC):
                w = wall("s_pxq_r", "s_c2g") if c == 0 else []
                add("tensor", lambda t, jl=jl, c=c: t.matmul(
                    pxq[:, 0:NE], ln2T[:, c, jl * 128:(jl + 1) * 128],
                    rw_sb[:, c, :], start=(c == 0), stop=(c == DC - 1)),
                    waits=w, incs=[("s_pxq_w", 1)] if c == DC - 1 else ())
            vch(lambda v: v.tensor_copy(l_sb[:, :], pxq[:, 0:NE]),
                extra=wall("s_pxq_w", "s_rexp"), incs=[("s_pxq_r", 1)])
            vch(lambda v: v.reduce_max(nm[:, :], l_sb[:, :],
                                       axis=AX, negate=True),
                incs=[("s_rnm", 1)])
            sch(lambda s: s.activation(
                E_sb[:, :], l_sb[:, :], AF.Exp, bias=nm[:, 0:1],
                accum_out=Z_sb[:, 0:1]),
                extra=wall("s_rnm"), incs=[("s_rexp", 1)])
            vch(lambda v: v.max(t8_sb[:, :], E_sb[:, :]),
                extra=wall("s_rexp"))
            vch(lambda v: v.tensor_scalar(
                keep_sb[:, :], E_sb[:, :], t8_sb[:, 1:2], None, ALU.is_ge))
            vch(lambda v: v.tensor_mul(pm_sb[:, :], E_sb[:, :], keep_sb[:, :]))
            vch(lambda v: v.reduce_sum(S2_sb[:, :], pm_sb[:, :], axis=AX))
            vch(lambda v: v.scalar_tensor_tensor(
                den_sb[:, :], Z_sb[:, :], 1e-9, S2_sb[:, :],
                ALU.mult, ALU.add))
            vch(lambda v: v.reciprocal(rden_sb[:, :], den_sb[:, :]))
            vch(lambda v, jl=jl: v.tensor_scalar(
                pf_sb[:, jl, :], pm_sb[:, :], rden_sb[:, 0:1], None, ALU.mult),
                incs=[("s_rout", 1)])
        for st in range(4):
            for q2 in range(2):
                jl = 2 * st + q2
                w = wall("s_rout", "s_pxq_r") if q2 == 0 else []
                add("tensor", lambda t, jl=jl, q2=q2: t.matmul(
                    pxq[0:8, q2 * 512:q2 * 512 + 128],
                    pf_sb[:, jl, :], idb[:, :], start=True, stop=True),
                    waits=w, incs=[("s_pxq_w", 1)])
            add("scalar", lambda s, st=st: s.copy(
                prT[:, st * 256:(st + 1) * 256],
                pxv[0:8, :, 0:128]),
                waits=wall("s_pxq_w"),
                incs=[("s_pxq_r", 1), ("s_prT", 1)])

        for e in range(NE):
            ep = e % 2
            seq = g * NE + e
            def emit_wgu_dma(q, e=e, seq=seq):
                sG = 4 * seq + q
                sp = sG % 2
                assert sG < 2 or ("wgr", sG - 2) in bld.marks
                add("gpsimd", lambda gg, e=e, q=q, sp=sp: gg.dma_start(
                    wg_sb[:, sp, :, :, :], wg[e][:, 4 * q:4 * q + 4, :, :]),
                    waits=bld.mwait(("wgr", sG - 2)) +
                          (bar if sG == 0 else []),
                    incs=[("s_wg%d" % sp, 16)])
                add("gpsimd", lambda gg, e=e, q=q, sp=sp: gg.dma_start(
                    wu_sb[:, sp, :, :, :], wu[e][:, 4 * q:4 * q + 4, :, :]),
                    waits=bld.mwait(("wur", sG - 2)),
                    incs=[("s_wu%d" % sp, 16)])
            emit_wgu_dma(0)
            emit_wgu_dma(1)
            add("gpsimd", lambda gg, e=e: gg.dma_start(
                wd_sb[:, :, :, :], wd[e]),
                waits=bld.mwait(("wdr", seq - 1)),
                incs=[("s_wd", 16)])
            for half in range(2):
                w = wall("s_prT", "s_c2g")
                if half == 0:
                    w += wall("s_pxq_r")
                add("tensor", lambda t, e=e, half=half: t.matmul(
                    pxq[:, half * 512:(half + 1) * 512], esel_sb[:, e, :],
                    prT[:, half * 512:(half + 1) * 512],
                    start=True, stop=True),
                    waits=w,
                    incs=[("s_pxq_w", 1)])
                add("scalar", lambda s, ep=ep, half=half: s.copy(
                    pb_sb[:, ep, half * 512:(half + 1) * 512],
                    pxq[:, half * 512:(half + 1) * 512]),
                    waits=wall("s_pxq_w"),
                    incs=[("s_pxq_r", 1), ("s_pb", 1)])
            for ft in range(FT):
                q = ft // 4
                sG = 4 * seq + q
                sp = sG % 2
                for nh in range(2):
                    pgb, pub = pg[nh], pu[nh]
                    ng, nu = "s_pg%d" % nh, "s_pu%d" % nh
                    for c in range(DC):
                        w = []
                        if c == 0:
                            w = wall(ng + "_r")
                            if ft % 4 == 0 and nh == 0:
                                w += [("s_wg%d" % sp, 16 * (sG // 2 + 1))]
                            if ft == 0 and nh == 0:
                                w += wall("s_ln2T")
                        add("tensor", lambda t, sp=sp, ft=ft, c=c, nh=nh,
                            pgb=pgb: t.matmul(
                            pgb[:, :], wg_sb[:, sp, ft % 4, c, :],
                            ln2T[:, c, nh * 512:(nh + 1) * 512],
                            start=(c == 0), stop=(c == DC - 1)),
                            waits=w,
                            incs=[(ng + "_w", 1)] if c == DC - 1 else [])
                        if ft % 4 == 3 and nh == 1 and c == DC - 1:
                            bld.mark(("wgr", sG), ng + "_w")
                    for c in range(DC):
                        w = []
                        if c == 0:
                            w = wall(nu + "_r")
                            if ft % 4 == 0 and nh == 0:
                                w += [("s_wu%d" % sp, 16 * (sG // 2 + 1))]
                        add("tensor", lambda t, sp=sp, ft=ft, c=c, nh=nh,
                            pub=pub: t.matmul(
                            pub[:, :], wu_sb[:, sp, ft % 4, c, :],
                            ln2T[:, c, nh * 512:(nh + 1) * 512],
                            start=(c == 0), stop=(c == DC - 1)),
                            waits=w,
                            incs=[(nu + "_w", 1)] if c == DC - 1 else [])
                        if ft % 4 == 3 and nh == 1 and c == DC - 1:
                            bld.mark(("wur", sG), nu + "_w")
                            if ft < 8:
                                emit_wgu_dma(ft // 4 + 2)
                    add("scalar", lambda s, e=e, ft=ft, pgb=pgb: s.activation(
                        silu_sb[:, :], pgb[:, :], AF.Silu,
                        bias=bgT_sb[:, e, ft:ft + 1]),
                        waits=wall(ng + "_w", "s_h1", "s_c2"),
                        incs=[("s_silu", 1), (ng + "_r", 1)])
                    add("scalar", lambda s, e=e, ft=ft, pub=pub: s.activation(
                        u_sb[:, :], pub[:, :], AF.Identity,
                        bias=buT_sb[:, e, ft:ft + 1]),
                        waits=wall(nu + "_w", "s_h1"),
                        incs=[("s_usb", 1), (nu + "_r", 1)])
                    add("vector", lambda v: v.tensor_mul(
                        h1_sb[:, :], silu_sb[:, :], u_sb[:, :]),
                        waits=bw("vector", wall("s_silu", "s_usb")),
                        incs=[("s_h1", 1)])
                    add("vector", lambda v, ep=ep, ft=ft, nh=nh: v.tensor_mul(
                        h_sb[:, ft, nh * 512:(nh + 1) * 512], h1_sb[:, :],
                        pb_sb[:, ep, nh * 512:(nh + 1) * 512]),
                        waits=wall("s_pb", "s_h1") +
                              bld.mwait(("wdr", seq - 1)),
                        incs=[("s_h", 1)])
            for dtile in range(DC):
                for nh in range(2):
                    pyp = nh
                    pyb = py[pyp]
                    nyp = "s_py%d" % pyp
                    if e == 0:
                        add("tensor", lambda t, dtile=dtile, nh=nh, pyb=pyb: t.matmul(
                            pyb[:, :], bd_sb[:, dtile, :],
                            prT[:, nh * 512:(nh + 1) * 512],
                            start=True, stop=False),
                            waits=wall(nyp + "_r", "s_prT", "s_c2g"))
                    for fc in range(FT):
                        w = []
                        if fc == 0:
                            if e != 0:
                                w = wall(nyp + "_r")
                            if dtile == 0 and nh == 0:
                                w += wall("s_h") + [("s_wd", 16 * (seq + 1))]
                        add("tensor", lambda t, fc=fc, dtile=dtile,
                            nh=nh, pyb=pyb, e=e: t.matmul(
                            pyb[:, :], wd_sb[:, fc, dtile, :],
                            h_sb[:, fc, nh * 512:(nh + 1) * 512],
                            start=(fc == 0 and e != 0), stop=(fc == FT - 1)),
                            waits=w,
                            incs=[(nyp + "_w", 1)] if fc == FT - 1 else [])
                        if fc == FT - 1 and dtile == DC - 1 and nh == 1:
                            bld.mark(("wdr", seq), nyp + "_w")
                    if e == 0:
                        add("vector", lambda v, dtile=dtile, nh=nh, pyb=pyb:
                            v.tensor_copy(
                            yffn[:, dtile, nh * 512:(nh + 1) * 512], pyb[:, :]),
                            waits=wall(nyp + "_w") +
                                  bld.mwait(("finT", g - 1)),
                            incs=[(nyp + "_r", 1), ("s_yf", 1)])
                    else:
                        prevyf = g * 128 + (e - 1) * 16 + dtile * 2 + nh + 1
                        add("vector", lambda v, dtile=dtile, nh=nh, pyb=pyb:
                            v.scalar_tensor_tensor(
                            yffn[:, dtile, nh * 512:(nh + 1) * 512],
                            pyb[:, :], 1.0,
                            yffn[:, dtile, nh * 512:(nh + 1) * 512],
                            ALU.mult, ALU.add),
                            waits=wall(nyp + "_w") + [("s_yf", prevyf)],
                            incs=[(nyp + "_r", 1), ("s_yf", 1)])

        for jl in range(8):
            jt = g * 8 + jl
            tp = jt % 2
            yn = (cur("s_y0") + cur("s_y1")) // 16
            add("sync", lambda s, jt=jt, tp=tp: s.dma_start(
                y_rot[:, tp, :], yspill[:, jt, :]),
                waits=[("s_y_r", yn - 1)], incs=[("s_y%d" % tp, 16)])
            for pr in range(4):
                for d2 in range(2):
                    dtile = 2 * pr + d2
                    w = []
                    if d2 == 0:
                        w = wall("s_pxq_r")
                        if jl == 0 and pr == 0:
                            w += wall("s_yf")
                    add("tensor", lambda t, dtile=dtile, d2=d2, jl=jl: t.matmul(
                        pxq[:, d2 * 512:d2 * 512 + 128],
                        yffn[:, dtile, jl * 128:(jl + 1) * 128], idf[:, :],
                        start=True, stop=True),
                        waits=w, incs=[("s_pxq_w", 1)])
                    if jl == 7 and pr == 3 and d2 == 1:
                        bld.mark(("finT", g), "s_pxq_w")
                add("vector", lambda v, tp=tp, pr=pr:
                    v.scalar_tensor_tensor(
                    fin_sb[:, tp, pr * 256:(pr + 1) * 256].rearrange(
                        "p (c f) -> p c f", f=128),
                    pxv[:, :, 0:128], 1.0,
                    y_rot[:, tp, pr * 256:(pr + 1) * 256].rearrange(
                        "p (c f) -> p c f", f=128),
                    ALU.mult, ALU.add),
                    waits=wall("s_pxq_w", "s_y%d" % tp) +
                          ([("s_out%d" % tp, 16 * (jt // 2))] if (jt >= 2 and pr == 0) else []),
                    incs=[("s_pxq_r", 1), ("s_fin", 1)] +
                         ([("s_y_r", 1)] if pr == 3 else []))
            add("sync", lambda s, jt=jt, tp=tp: s.dma_start(
                outp[jt], fin_sb[:, tp, :]),
                waits=wall("s_fin"), incs=[("s_out%d" % tp, 16)])

    if stop is not None:
        add("gpsimd", lambda g: g.dma_start(
            dbgF, ln2T.rearrange("p a b -> p (a b)")),
            waits=wall("s_yf"), incs=[("s_dbg2", 16)])
        add("gpsimd", lambda g: g.dma_start(
            dbgY, yffn.rearrange("p a b -> p (a b)")),
            incs=[("s_dbg2", 16)])
        add("gpsimd", lambda g: g.dma_start(
            dbgP[:, 0:2048], pb_sb.rearrange("p a b -> p (a b)")),
            incs=[("s_dbg2", 16)])
        add("gpsimd", lambda g: g.dma_start(
            dbgP[0:8, 2048:3072], prT[:, :]), incs=[("s_dbg2", 16)])
        add("gpsimd", lambda g: g.dma_start(
            dbgP[:, 3072:3136], pf_sb.rearrange("p a b -> p (a b)")),
            incs=[("s_dbg2", 16)])
        add("gpsimd", lambda g: g.dma_start(
            dbgH, h_sb.rearrange("p a b -> p (a b)")),
            incs=[("s_dbg2", 16)])
        add("sync", lambda s: s.wait_ge(bld.sem("s_dbg2"), cur("s_dbg2")))
    add("sync", lambda s: s.wait_ge(bld.sem("s_out0"), cur("s_out0")))
    add("sync", lambda s: s.wait_ge(bld.sem("s_out1"), cur("s_out1")))

    bld.emit_block()
    p2.close()
    ctx.close()
    return nc



def _pe_table():
    pos = np.arange(T, dtype=np.float32)[:, None]
    inv = np.exp(-np.float32(math.log(10000.0)) *
                 np.arange(0, HD, 2, dtype=np.float32) / np.float32(HD))
    ang = pos * inv[None, :].astype(np.float32)
    pe = np.zeros((T, HD), dtype=np.float32)
    pe[:, 0::2] = np.sin(ang)
    pe[:, 1::2] = np.cos(ang)
    return pe


def stage_inputs(L0_attn_x, L0_attn_Wq, L0_attn_bq, L0_attn_Wkv, L0_attn_bkv,
                 L0_attn_Wo, L0_ffn_router_w, L0_ffn_gate_w, L0_ffn_gate_b,
                 L0_ffn_up_w, L0_ffn_up_b, L0_ffn_down_w, L0_ffn_down_b):
    f32 = lambda a: np.ascontiguousarray(np.asarray(a), dtype=np.float32)
    x = f32(L0_attn_x)
    Wq, bq, Wkv, bkv, Wo = map(f32, (L0_attn_Wq, L0_attn_bq, L0_attn_Wkv,
                                     L0_attn_bkv, L0_attn_Wo))
    rwv, gw, gb, uw, ub, dw, db = map(f32, (
        L0_ffn_router_w, L0_ffn_gate_w, L0_ffn_gate_b, L0_ffn_up_w,
        L0_ffn_up_b, L0_ffn_down_w, L0_ffn_down_b))

    C = np.ascontiguousarray
    shared = {
        "wq": C(Wq.reshape(DC, 128, H, 128).transpose(1, 2, 0, 3)),
        "wkv": C(Wkv.reshape(DC, 128, 2, 128).transpose(1, 2, 0, 3)),
        "wo": C(Wo.reshape(H, 128, DC, 128).transpose(1, 0, 2, 3)),
        "wg": C(gw.reshape(NE, DC, 128, FT, 128).transpose(0, 2, 3, 1, 4)),
        "wu": C(uw.reshape(NE, DC, 128, FT, 128).transpose(0, 2, 3, 1, 4)),
        "wd": C(dw.reshape(NE, FT, 128, DC, 128).transpose(0, 2, 1, 3, 4)),
        "rw": C(rwv.reshape(NE, DC, 128).transpose(2, 1, 0)),
        "bqr": C(bq.reshape(1, H, 128)),
        "bkvr": C(bkv.reshape(1, 2, 128)),
        "bgT": C(gb.reshape(NE, FT, 128).transpose(2, 0, 1)),
        "buT": C(ub.reshape(NE, FT, 128).transpose(2, 0, 1)),
        "bdr": C(db.reshape(NE, DC, 128)),
        "idbin": np.eye(128, dtype=np.float32),
        "idfin": np.eye(128, dtype=np.float32),
        "onesin": np.ones((1, 128), dtype=np.float32),
        "eselin": C(np.eye(NE, dtype=np.float32)[:, :, None]
                    * np.ones((1, 1, 128), np.float32)),
    }
    pe = _pe_table()
    shared["pek"] = C(pe.T)
    pes = (pe * np.float32(SCALE)).astype(np.float32)

    in_maps = []
    for c in range(8):
        g, r = c // 4, c % 4
        xg = x[g].reshape(NT, 128, D)
        qi = np.arange(128, dtype=np.float32)[:, None]
        kb = np.arange(512, dtype=np.float32)[None, :]
        maskv = np.where(kb <= r * 128 + qi, 0.0, NEG).astype(np.float32)
        own = [4 * j + r for j in range(NJ)]
        peov = C(np.stack([pes[t * 128:(t + 1) * 128, :].T for t in own]
                          ).transpose(1, 0, 2))
        m = dict(shared)
        m["xk"] = C(xg)
        m["xq"] = C(xg[own])
        m["peo"] = peov
        m["maskin"] = maskv
        in_maps.append(m)
    return in_maps


def kernel(**inputs):
    in_maps = stage_inputs(**inputs)
    nc = build_nc()
    res = run_bass_kernel_spmd(nc, in_maps, core_ids=list(range(8)))
    out = np.empty((B, T, D), dtype=np.float32)
    for c in range(8):
        g, r = c // 4, c % 4
        out[g].reshape(NT, 128, D)[r::4] = res.results[c]["out"]
    return out


if __name__ == "__main__":
    import reference
    inputs = reference.setup_inputs()
    got = kernel(**{k: np.asarray(v) for k, v in inputs.items()})
    want = np.asarray(reference.reference(**inputs))
    err = np.sqrt(((got - want) ** 2).mean() / (want ** 2).mean())
    print("resvar-rel-l2", err)


# version 07
# speedup vs baseline: 1.0920x; 1.0920x over previous
import math
from contextlib import ExitStack

import numpy as np

import concourse.bass as bass
import concourse.mybir as mybir
from concourse.bass_utils import run_bass_kernel_spmd

F32 = mybir.dt.float32
BF16 = mybir.dt.bfloat16
AF = mybir.ActivationFunctionType
ALU = mybir.AluOpType
AX = mybir.AxisListType.X

B, T, D, H, HD, NE, FF = 2, 8192, 1024, 8, 128, 8, 2048
NJ = 16
NT = 64
DC = D // 128
FT = FF // 128
SCALE = 1.0 / math.sqrt(HD)
EPS = 1e-5
NEG = -1.0e30
ENGINES = ("sync", "tensor", "vector", "scalar", "gpsimd")


class Builder:

    def __init__(self, nc, ctx):
        self.nc = nc
        self.ctx = ctx
        self.ops = {e: [] for e in ENGINES}
        self.h = {}
        self.cnt = {}

    sem_name = {}

    def sem(self, name):
        if name not in self.h:
            self.h[name] = self.ctx.enter_context(self.nc.semaphore(name))
            self.cnt[name] = 0
            self.sem_name[id(self.h[name])] = name
        return self.h[name]

    def cur(self, name):
        self.sem(name)
        return self.cnt[name]

    def add(self, engine, make, waits=(), incs=()):
        wl = [(self.sem(n), v) for n, v in waits if v > 0]
        il = []
        for n, k in incs:
            self.sem(n)
            self.cnt[n] += k
            il.append((self.h[n], k))
        self.ops[engine].append((wl, make, il))

    def wall(self, *names):
        return [(n, self.cur(n)) for n in names]

    marks = {}

    def mark(self, key, name):
        self.marks[key] = (name, self.cur(name))

    def mwait(self, key):
        return [self.marks[key]] if key in self.marks else []

    def emit_block(self):
        ops = self.ops
        self.ops = {e: [] for e in ENGINES}

        def replay(engine, eng):
            RELEASE = {"s_ptq_r", "s_pxq_r", "s_qkr", "s_qkr0",
                       "s_qkr1", "s_atr0", "s_atr1", "s_kvk_r", "s_kvv_r",
                       "s_xkv_r", "s_pek_r", "s_y_r", "s_pg0_r", "s_pg1_r",
                       "s_pu0_r", "s_pu1_r", "s_py0_r", "s_py1_r"}
            via_drain = engine in ("vector", "scalar")
            for wl, make, il in ops[engine]:
                for s, v in wl:
                    eng.wait_ge(s, v)
                inst = make(eng)
                if engine == "tensor":
                    assert len(il) <= 1, "tensor ops carry at most one inc"
                if not via_drain:
                    for i, (s, k) in enumerate(il):
                        if i == 0:
                            inst.then_inc(s, k)
                        else:
                            eng.nop().then_inc(s, k)
                    continue
                rel = [x for x in il
                       if self.sem_name.get(id(x[0]), "") in RELEASE]
                dat = [x for x in il
                       if self.sem_name.get(id(x[0]), "") not in RELEASE]
                for i, (s, k) in enumerate(rel):
                    if i == 0:
                        inst.then_inc(s, k)
                    else:
                        eng.nop().then_inc(s, k)
                for i, (s, k) in enumerate(dat):
                    if i == 0:
                        eng.drain().then_inc(s, k)
                    else:
                        eng.nop().then_inc(s, k)

        with self.nc.Block() as block:
            @block.sync
            def _(eng):
                replay("sync", eng)

            @block.tensor
            def _(eng):
                replay("tensor", eng)

            @block.vector
            def _(eng):
                replay("vector", eng)

            @block.scalar
            def _(eng):
                replay("scalar", eng)

            @block.gpsimd
            def _(eng):
                replay("gpsimd", eng)


def build_nc(stop=None):
    nc = bass.Bass("TRN2", target_bir_lowering=False,
                   detect_race_conditions=False)
    dt = nc.dram_tensor
    if stop is not None:
        dbgK = dt("dbgK", [128, T], F32, kind="ExternalOutput").ap()
        dbgV = dt("dbgV", [128, NT * 128], F32, kind="ExternalOutput").ap()
        dbgL = dt("dbgL", [128, 2048], F32, kind="ExternalOutput").ap()
        dbgI = dt("dbgI", [128, 256], F32, kind="ExternalOutput").ap()
        dbgF = dt("dbgF", [128, 8192], F32, kind="ExternalOutput").ap()
        dbgY = dt("dbgY", [128, 8192], F32, kind="ExternalOutput").ap()
        dbgP = dt("dbgP", [128, 4096], F32, kind="ExternalOutput").ap()
        dbgH = dt("dbgH", [128, FT * 1024], F32, kind="ExternalOutput").ap()

    xq = dt("xq", [NJ, 128, D], F32, kind="ExternalInput").ap()
    xk = dt("xk", [NT, 128, D], F32, kind="ExternalInput").ap()
    wq = dt("wq", [128, H, DC, 128], F32, kind="ExternalInput").ap()
    wkv = dt("wkv", [128, 2, DC, 128], F32, kind="ExternalInput").ap()
    wo = dt("wo", [128, H, DC, 128], F32, kind="ExternalInput").ap()
    wg = dt("wg", [NE, 128, FT, DC, 128], F32, kind="ExternalInput").ap()
    wu = dt("wu", [NE, 128, FT, DC, 128], F32, kind="ExternalInput").ap()
    wd = dt("wd", [NE, 128, FT, DC, 128], F32, kind="ExternalInput").ap()
    rw = dt("rw", [128, DC, NE], F32, kind="ExternalInput").ap()
    bqr = dt("bqr", [1, H, 128], F32, kind="ExternalInput").ap()
    bkvr = dt("bkvr", [1, 2, 128], F32, kind="ExternalInput").ap()
    bgT = dt("bgT", [128, NE, FT], F32, kind="ExternalInput").ap()
    buT = dt("buT", [128, NE, FT], F32, kind="ExternalInput").ap()
    bdr = dt("bdr", [8, DC, 128], F32, kind="ExternalInput").ap()
    pek = dt("pek", [128, T], F32, kind="ExternalInput").ap()
    peo = dt("peo", [128, NJ, 128], F32, kind="ExternalInput").ap()
    maskin = dt("maskin", [128, 512], F32, kind="ExternalInput").ap()
    idbin = dt("idbin", [128, 128], F32, kind="ExternalInput").ap()
    idfin = dt("idfin", [128, 128], F32, kind="ExternalInput").ap()
    onesin = dt("onesin", [1, 128], F32, kind="ExternalInput").ap()
    eselin = dt("eselin", [8, NE, 128], F32, kind="ExternalInput").ap()
    outp = dt("out", [NJ, 128, D], F32, kind="ExternalOutput").ap()
    yspill = dt("yspill", [128, NJ, D], F32, kind="Internal").ap()

    ctx = ExitStack()
    bld = Builder(nc, ctx)
    add, cur, wall = bld.add, bld.cur, bld.wall

    def sbt(stack, name, shape, dtp):
        return stack.enter_context(nc.sbuf_tensor(name, shape, dtp)).ap()

    def pst(stack, name):
        return stack.enter_context(nc.psum_tensor(name, [128, 512], F32)).ap()

    idb = sbt(ctx, "idb", [128, 128], BF16)
    idf = sbt(ctx, "idf", [128, 128], F32)
    ones_b = sbt(ctx, "ones_b", [1, 128], BF16)
    mask_sb = sbt(ctx, "mask_sb", [128, 512], F32)
    sqscr = sbt(ctx, "sqscr", [128, 1024], BF16)
    lnsum = sbt(ctx, "lnsum", [128, 1], F32)
    lnssq = sbt(ctx, "lnssq", [128, 1], F32)
    lnmean = sbt(ctx, "lnmean", [128, 1], F32)
    lnmsq = sbt(ctx, "lnmsq", [128, 1], F32)
    veps = sbt(ctx, "veps", [128, 1], F32)
    lnt = sbt(ctx, "lnt", [128, 1], F32)
    rstd = sbt(ctx, "rstd", [128, 1], F32)
    nmr = sbt(ctx, "nmr", [128, 1], F32)
    nm = sbt(ctx, "nm", [128, 1], F32)

    lncount = [0]

    def vch(make, extra=(), incs=()):
        add("vector", make, waits=list(extra) + wall("s_vch"),
            incs=[("s_vch", 1)] + list(incs))

    def sch(make, extra=(), incs=()):
        add("scalar", make, waits=list(extra) + wall("s_sch"),
            incs=[("s_sch", 1)] + list(incs))

    def emit_ln(x_ap, out_ap, dep_waits, apply_waits=(), apply_incs=()):
        lncount[0] += 1
        w0 = list(dep_waits) + wall("s_lnapply")
        vch(lambda v, x=x_ap: v.reduce_sum(
            lnsum[:, :], x[:, :], axis=AX), extra=w0)
        sch(lambda s, x=x_ap: s.activation(
            sqscr[:, :], x[:, :], AF.Square, accum_out=lnssq[:, 0:1]),
            extra=list(dep_waits) + wall("s_lnnmr"),
            incs=[("s_lnsq", 1)])
        vch(lambda v: v.tensor_scalar(
            lnmean[:, :], lnsum[:, :], 1.0 / D, None, ALU.mult))
        vch(lambda v: v.tensor_mul(lnmsq[:, :], lnmean[:, :], lnmean[:, :]))
        vch(lambda v: v.scalar_tensor_tensor(
            veps[:, :], lnssq[:, :], 1.0 / D, lnmsq[:, :],
            ALU.mult, ALU.subtract),
            extra=wall("s_lnsq"))
        vch(lambda v: v.tensor_scalar(
            veps[:, :], veps[:, :], EPS, None, ALU.add),
            incs=[("s_lnstat", 1)])
        sch(lambda s: s.activation(lnt[:, :], veps[:, :], AF.Ln),
            extra=wall("s_lnstat"))
        sch(lambda s: s.activation(rstd[:, :], lnt[:, :], AF.Exp, scale=-0.5),
            incs=[("s_lnrstd", 1)])
        vch(lambda v: v.scalar_tensor_tensor(
            nmr[:, :], lnmean[:, :], -1.0, rstd[:, :], ALU.mult, ALU.mult),
            extra=wall("s_lnrstd"), incs=[("s_lnnmr", 1)])
        sch(lambda s, x=x_ap, o=out_ap: s.activation(
            o, x, AF.Identity, bias=nmr[:, 0:1], scale=rstd[:, 0:1]),
            extra=list(apply_waits) + wall("s_lnnmr"),
            incs=[("s_lnapply", 1)] + list(apply_incs))

    p1c = ExitStack()
    x_own = sbt(p1c, "x_own", [128, NJ, D], F32)
    wq_sb = sbt(p1c, "wq_sb", [128, H, DC, 128], BF16)
    wo_sb = sbt(p1c, "wo_sb", [128, H, DC, 128], BF16)
    wkv_sb = sbt(p1c, "wkv_sb", [128, 2, DC, 128], BF16)
    bq_sb = sbt(p1c, "bq_sb", [1, H, 128], BF16)
    bkv_sb = sbt(p1c, "bkv_sb", [1, 2, 128], BF16)
    K_sb = sbt(p1c, "K_sb", [128, T], BF16)
    V_sb = sbt(p1c, "V_sb", [128, NT, 128], BF16)

    p1a = ExitStack()
    x_kv = sbt(p1a, "x_kv", [128, 2, D], F32)
    pek_sb = sbt(p1a, "pek_sb", [128, 2, 512], F32)
    ln_kv = sbt(p1a, "ln_kv", [128, 2, D], BF16)
    lnT_kv = sbt(p1a, "lnT_kv", [128, 2, DC, 128], BF16)
    vt_tmp = sbt(p1a, "vt_tmp", [128, 2, 128], BF16)
    ptr = p1a.enter_context(nc.psum_tensor("ptr", [128, 2048], F32)).ap()
    pkv = p1a.enter_context(nc.psum_tensor("pkv", [128, 1024], F32)).ap()

    add("sync", lambda s: s.dma_start(idf[:, :], idfin), incs=[("s_c", 16)])
    add("sync", lambda s: s.dma_start(mask_sb[:, :], maskin), incs=[("s_c", 16)])
    add("gpsimd", lambda g: g.dma_start(idb[:, :], idbin), incs=[("s_cg", 16)])
    add("gpsimd", lambda g: g.dma_start(ones_b[:, :], onesin), incs=[("s_cg", 16)])
    add("gpsimd", lambda g: g.dma_start(wq_sb[:, :, :, :], wq), incs=[("s_w", 16)])
    add("gpsimd", lambda g: g.dma_start(wkv_sb[:, :, :, :], wkv), incs=[("s_w", 16)])
    add("gpsimd", lambda g: g.dma_start(wo_sb[:, :, :, :], wo), incs=[("s_w", 16)])
    add("gpsimd", lambda g: g.dma_start(bq_sb[:, :, :], bqr), incs=[("s_w", 16)])
    add("gpsimd", lambda g: g.dma_start(bkv_sb[:, :, :], bkvr), incs=[("s_w", 16)])
    for j in range(NJ):
        add("sync", lambda s, j=j: s.dma_start(x_own[:, j, :], xq[j]),
            incs=[("s_xq", 16)])

    for j in range(NJ):
        jp = j % 2
        add("sync", lambda s, j=j, jp=jp: s.dma_start(
            pek_sb[:, jp, :], pek[:, 512 * j:512 * (j + 1)]),
            waits=[("s_pek_r", j - 1)] if j >= 2 else (),
            incs=[("s_pek%d" % jp, 16)])
        for ti in range(4):
            tt = 4 * j + ti
            tp = tt % 2
            add("sync", lambda s, tt=tt, tp=tp: s.dma_start(
                x_kv[:, tp, :], xk[tt]),
                waits=[("s_xkv_r", tt - 1)] if tt >= 2 else (),
                incs=[("s_xkv%d" % tp, 16)])
            emit_ln(x_kv[:, tp, :], ln_kv[:, tp, :],
                    wall("s_xkv%d" % tp),
                    apply_waits=bld.mwait(("lnkv", tt - 2)),
                    apply_incs=[("s_xkv_r", 1)])
            for half in range(2):
                for c4 in range(4):
                    c = 4 * half + c4
                    w = []
                    if c4 == 0:
                        w = wall("s_ptr_r")
                        if tt == 0 and half == 0:
                            w += wall("s_c", "s_cg")
                        if half == 0:
                            w += wall("s_lnapply")
                    add("tensor", lambda t, tp=tp, c=c, c4=c4: t.matmul(
                        ptr[:, c4 * 512:c4 * 512 + 128],
                        ln_kv[:, tp, c * 128:(c + 1) * 128], idb[:, :],
                        start=True, stop=True),
                        waits=w, incs=[("s_ptr_w", 1)])
                    if half == 1 and c4 == 3:
                        bld.mark(("lnkv", tt), "s_ptr_w")
                add("scalar", lambda s, tp=tp, half=half: s.copy(
                    lnT_kv[:, tp, 4 * half:4 * half + 4, :],
                    ptr.rearrange("p (b w) -> p b w", w=512)[:, :, 0:128]),
                    waits=wall("s_ptr_w"), incs=[("s_ptr_r", 1)])
            for f in range(2):
                for c in range(DC):
                    w = []
                    if f == 0 and c == 0:
                        w = wall("s_ptr_r", "s_w") + \
                            ([("s_kvk_r", tt)] if tt >= 1 else [])
                    if f == 1 and c == 0:
                        w = [("s_kvv_r", tt)] if tt >= 1 else []
                    add("tensor", lambda t, f=f, c=c, tp=tp: t.matmul(
                        pkv[:, f * 512:f * 512 + 128],
                        wkv_sb[:, f, c, :], lnT_kv[:, tp, c, :],
                        start=(c == 0), stop=False), waits=w)
                add("tensor", lambda t, f=f: t.matmul(
                    pkv[:, f * 512:f * 512 + 128],
                    bkv_sb[:, f, :], ones_b[:, :], start=False, stop=True),
                    incs=[("s_kvk_w" if f == 0 else "s_kvv_w", 1)])
            add("vector", lambda v, tt=tt, jp=jp, ti=ti: v.scalar_tensor_tensor(
                K_sb[:, tt * 128:(tt + 1) * 128], pkv[:, 0:128], 1.0,
                pek_sb[:, jp, ti * 128:(ti + 1) * 128], ALU.mult, ALU.add),
                waits=wall("s_kvk_w", "s_pek%d" % jp),
                incs=[("s_kv", 1), ("s_kvk_r", 1)] +
                     ([("s_pek_r", 1)] if ti == 3 else []))
            add("scalar", lambda s, tp=tp: s.copy(
                vt_tmp[:, tp, :], pkv[:, 512:640]),
                waits=wall("s_kvv_w"),
                incs=[("s_vt", 1), ("s_kvv_r", 1)])
            add("tensor", lambda t, tp=tp: t.matmul(
                ptr[:, 0:128], vt_tmp[:, tp, :], idb[:, :],
                start=True, stop=True),
                waits=wall("s_vt", "s_ptr_r"),
                incs=[("s_ptr_w", 1)])
            add("vector", lambda v, tt=tt: v.tensor_copy(
                V_sb[:, tt, :], ptr[:, 0:128]),
                waits=wall("s_ptr_w"),
                incs=[("s_kv", 1), ("s_ptr_r", 1)])

    for e in ENGINES:
        add(e, lambda eng: eng.nop(), incs=[("s_p1a", 1)])

    if stop is not None:
        add("gpsimd", lambda g: g.dma_start(dbgK, K_sb[:, :]),
            waits=[("s_p1a", 5)], incs=[("s_dbg_st", 16)])
        add("gpsimd", lambda g: g.dma_start(
            dbgV, V_sb.rearrange("p a b -> p (a b)")),
            incs=[("s_dbg_st", 16)])
        add("gpsimd", lambda g: g.dma_start(
            dbgL[:, 0:1024], ln_kv[:, 1, :]), incs=[("s_dbg_st", 16)])
        add("gpsimd", lambda g: g.dma_start(
            dbgL[:, 1024:2048],
            lnT_kv[:, 1, :, :].rearrange("p c f -> p (c f)")),
            incs=[("s_dbg_st", 16)])
        add("gpsimd", lambda g: g.dma_start(dbgI[:, 0:128], idb[:, :]),
            incs=[("s_dbg_st", 16)])
        add("gpsimd", lambda g: g.dma_start(dbgI[:, 128:256], vt_tmp[:, 1, :]),
            incs=[("s_dbg_st", 16)])
        if stop == "kv":
            for j in range(NJ):
                add("sync", lambda s, j=j: s.dma_start(
                    outp[j], x_own[:, j, :]),
                    waits=[("s_xq", 16 * NJ)], incs=[("s_out", 16)])
            add("sync", lambda s: s.wait_ge(bld.sem("s_out"), cur("s_out")))
            add("sync", lambda s: s.wait_ge(bld.sem("s_dbg_st"), cur("s_dbg_st")))
            bld.emit_block()
            p1a.close()
            p1c.close()
            ctx.close()
            return nc

    bld.emit_block()
    p1a.close()

    bar1 = wall("s_p1a")
    p1b = ExitStack()
    peo_sb = sbt(p1b, "peo_sb", [128, NJ, 128], F32)
    scores = sbt(p1b, "scores", [128, T], BF16)
    probs = sbt(p1b, "probs", [128, T], BF16)
    ln_own = sbt(p1b, "ln_own", [128, D], BF16)
    ln1T = sbt(p1b, "ln1T", [128, 2, DC, 128], BF16)
    qT_sb = sbt(p1b, "qT_sb", [128, 2, 128], BF16)
    pT_sb = sbt(p1b, "pT_sb", [128, 2, 512], BF16)
    attnT = sbt(p1b, "attnT", [128, 2, H, 128], BF16)
    nmb = sbt(p1b, "nmb", [128, 16], F32)
    ssum = sbt(p1b, "ssum", [128, 1], F32)
    rs = sbt(p1b, "rs", [128, 1], F32)
    diag = sbt(p1b, "diag", [128, 128], BF16)
    tmpf = sbt(p1b, "tmpf", [128, 512], F32)

    pqk = [pst(p1b, "pqk0"), pst(p1b, "pqk1")]
    pat = [p1b.enter_context(nc.psum_tensor("pat0", [128, 128], F32)).ap(),
           p1b.enter_context(nc.psum_tensor("pat1", [128, 128], F32)).ap()]
    ptq = p1b.enter_context(nc.psum_tensor("ptq", [128, 2048], F32)).ap()

    add("sync", lambda s: s.dma_start(peo_sb[:, :, :], peo),
        waits=bar1, incs=[("s_peo", 16)])

    first1 = {"tensor": True, "vector": True}

    def bw1(eng, extra=()):
        w = list(extra)
        if first1.get(eng):
            w += bar1
            first1[eng] = False
        return w

    import kernel as _selfmod
    for j in range(getattr(_selfmod, "ATTN_J", NJ)):
        jp = j % 2
        kext = 512 * (j + 1)
        nkb = j + 1
        emit_ln(x_own[:, j, :], ln_own[:, :], bw1("vector", wall("s_xq")),
                apply_waits=bld.mwait(("lnown", j - 1)))
        for half in range(2):
            for c4 in range(4):
                c = 4 * half + c4
                w = []
                if c4 == 0:
                    w = bw1("tensor", wall("s_ptq_r"))
                    if half == 0:
                        w += wall("s_lnapply")
                add("tensor", lambda t, c=c, c4=c4: t.matmul(
                    ptq[:, c4 * 512:c4 * 512 + 128],
                    ln_own[:, c * 128:(c + 1) * 128], idb[:, :],
                    start=True, stop=True),
                    waits=w, incs=[("s_ptq_w", 1)])
                if half == 1 and c4 == 3:
                    bld.mark(("lnown", j), "s_ptq_w")
            add("scalar", lambda s, jp=jp, half=half: s.copy(
                ln1T[:, jp, 4 * half:4 * half + 4, :],
                ptq.rearrange("p (b w) -> p b w", w=512)[:, :, 0:128]),
                waits=wall("s_ptq_w"), incs=[("s_ptq_r", 1), ("s_ln1T", 1)])

        STG = getattr(_selfmod, "ATTN_STAGE", 5)
        def emit_front(h):
            hp = h % 2
            seq = 8 * j + h
            for c in range(DC):
                w = wall("s_ptq_r", "s_ln1T", "s_w") if c == 0 else []
                add("tensor", lambda t, h=h, c=c, jp=jp: t.matmul(
                    ptq[:, 0:128], wq_sb[:, h, c, :], ln1T[:, jp, c, :],
                    start=(c == 0), stop=False), waits=w)
            add("tensor", lambda t, h=h: t.matmul(
                ptq[:, 0:128], bq_sb[:, h, :], ones_b[:, :],
                start=False, stop=True), incs=[("s_ptq_w", 1)])
            add("vector", lambda v, hp=hp, j=j: v.scalar_tensor_tensor(
                qT_sb[:, hp, :], ptq[:, 0:128], SCALE, peo_sb[:, j, :],
                ALU.mult, ALU.add),
                waits=wall("s_ptq_w", "s_peo") + bld.mwait(("qTr", seq - 2)),
                incs=[("s_qT", 1), ("s_ptq_r", 1)])
            if STG < 3:
                return
            for kb in range(nkb):
                bp = kb % 2
                w = wall("s_qT") + [("s_qkr%d" % bp, cur("s_qkr%d" % bp))]
                if seq == 0 and kb == 0:
                    w += [("s_kv", 2 * NT)]
                add("tensor", lambda t, hp=hp, kb=kb, bp=bp: t.matmul(
                    pqk[bp][:, :], qT_sb[:, hp, :],
                    K_sb[:, kb * 512:(kb + 1) * 512],
                    start=True, stop=True),
                    waits=w, incs=[("s_qkw%d" % bp, 1)])
                if kb == nkb - 1:
                    bld.mark(("qTr", seq), "s_qkw%d" % bp)
                if kb < nkb - 1:
                    add("vector", lambda v, kb=kb, bp=bp: v.tensor_scalar(
                        scores[:, kb * 512:(kb + 1) * 512], pqk[bp][:, :],
                        1.0, None, ALU.mult, op1=ALU.max,
                        accum_out=nmb[:, kb:kb + 1]),
                        waits=wall("s_qkw%d" % bp, "s_exp"),
                        incs=[("s_qkr%d" % bp, 1)])
                else:
                    add("vector", lambda v, kb=kb, bp=bp: v.scalar_tensor_tensor(
                        scores[:, kb * 512:(kb + 1) * 512], pqk[bp][:, :],
                        1.0, mask_sb[:, :], ALU.mult, ALU.add),
                        waits=wall("s_qkw%d" % bp, "s_exp", "s_c"),
                        incs=[("s_qkr%d" % bp, 1)])
                    add("vector", lambda v, kb=kb: v.reduce_max(
                        nmb[:, kb:kb + 1],
                        scores[:, kb * 512:(kb + 1) * 512], axis=AX),
                        waits=[("s_qkr%d" % bp, cur("s_qkr%d" % bp))])
            add("vector", lambda v, nkb=nkb: v.reduce_max(
                nm[:, :], nmb[:, 0:nkb], axis=AX, negate=True),
                waits=wall("s_qkr0", "s_qkr1"),
                incs=[("s_nm", 1)])

        def emit_exp(h):
            seq = 8 * j + h
            add("scalar", lambda s, kext=kext: s.activation(
                probs[:, 0:kext], scores[:, 0:kext], AF.Exp,
                bias=nm[:, 0:1], accum_out=ssum[:, 0:1]),
                waits=[("s_nm", seq + 1)] + bld.mwait(("T", seq - 1)),
                incs=[("s_exp", 1)])

        def emit_tail(h):
            hp = h % 2
            seq = 8 * j + h
            add("vector", lambda v: v.reciprocal(rs[:, :], ssum[:, :]),
                waits=[("s_exp", seq + 1)], incs=[("s_rs", 1)])
            add("vector", lambda v: v.tensor_scalar(
                diag[:, :], idb[:, :], rs[:, 0:1], None, ALU.mult),
                waits=[("s_rs", seq + 1)] + bld.mwait(("T", seq - 1)) +
                      (wall("s_cg") if seq == 0 else []),
                incs=[("s_diag", 1)])
            if STG < 4:
                return
            SUB = getattr(_selfmod, "ATTN_SUB", 3)
            for kb in range(nkb):
                tp = kb % 2
                for c4 in range(4):
                    w = []
                    if c4 == 0:
                        w = [("s_diag", seq + 1), ("s_exp", seq + 1)] + \
                            wall("s_ptq_r")
                    add("tensor", lambda t, kb=kb, c4=c4: t.matmul(
                        ptq[:, c4 * 512:c4 * 512 + 128],
                        probs[:, kb * 512 + c4 * 128:kb * 512 + (c4 + 1) * 128],
                        diag[:, :], start=True, stop=True),
                        waits=w, incs=[("s_ptq_w", 1)])
                    if kb == nkb - 1 and c4 == 3:
                        bld.mark(("T", seq), "s_ptq_w")
                if tp == 0:
                    add("scalar", lambda s, tp=tp: s.copy(
                        pT_sb[:, tp, :],
                        ptq.rearrange("p (b w) -> p b w", w=512)[:, :, 0:128]),
                        waits=wall("s_ptq_w") +
                              [("s_pta%d" % tp, cur("s_pta%d" % tp))],
                        incs=[("s_ptq_r", 1), ("s_pte%d" % tp, 1)])
                else:
                    add("vector", lambda v, tp=tp: v.tensor_copy(
                        pT_sb[:, tp, :],
                        ptq.rearrange("p (b w) -> p b w", w=512)[:, :, 0:128]),
                        waits=wall("s_ptq_w") +
                              [("s_pta%d" % tp, cur("s_pta%d" % tp))],
                        incs=[("s_ptq_r", 1), ("s_pte%d" % tp, 1)])
                if SUB < 2:
                    continue
                for c4 in range(4):
                    w = []
                    if c4 == 0:
                        w = wall("s_pte%d" % tp)
                        if kb == 0:
                            w += [("s_atr%d" % hp, cur("s_atr%d" % hp))]
                    add("tensor", lambda t, kb=kb, c4=c4, tp=tp, hp=hp,
                        last=(kb == nkb - 1): t.matmul(
                        pat[hp][:, :], V_sb[:, 4 * kb + c4, :],
                        pT_sb[:, tp, c4 * 128:(c4 + 1) * 128],
                        start=(kb == 0 and c4 == 0),
                        stop=(last and c4 == 3)),
                        waits=w,
                        incs=[("s_pta%d" % tp, 1)] if c4 == 3 else [])
            if SUB >= 3:
                add("vector", lambda v, jp=jp, h=h, hp=hp: v.tensor_copy(
                    attnT[:, jp, h, :], pat[hp][:, :]),
                    waits=wall("s_pta0", "s_pta1"),
                    incs=[("s_atr%d" % hp, 1), ("s_at", 1)])

        STG = getattr(_selfmod, "ATTN_STAGE", 5)
        for h in range(H if STG >= 2 else 0):
            emit_front(h)
            if STG >= 3 and h >= 1:
                emit_tail(h - 1)
            if STG >= 3:
                emit_exp(h)
        if STG >= 3:
            emit_tail(H - 1)

        for half in range(2 if STG >= 5 else 0):
            for d4 in range(4):
                dtile = 4 * half + d4
                for h in range(H):
                    w = []
                    if d4 == 0 and h == 0:
                        w = wall("s_at", "s_ptq_r")
                    add("tensor", lambda t, h=h, dtile=dtile, d4=d4, jp=jp: t.matmul(
                        ptq[:, d4 * 512:d4 * 512 + 128],
                        wo_sb[:, h, dtile, :], attnT[:, jp, h, :],
                        start=(h == 0), stop=(h == H - 1)),
                        waits=w,
                        incs=[("s_ptq_w", 1)] if (d4 == 3 and h == H - 1) else ())
            add("scalar", lambda s: s.copy(
                tmpf[:, :],
                ptq.rearrange("p (b w) -> p b w", w=512)[:, :, 0:128]),
                waits=wall("s_ptq_w"),
                incs=[("s_ptq_r", 1), ("s_tmpf", 1)])
            for d4 in range(4):
                w = []
                if d4 == 0:
                    w = wall("s_tmpf", "s_ptq_r")
                add("tensor", lambda t, d4=d4: t.matmul(
                    ptq[:, d4 * 512:d4 * 512 + 128],
                    tmpf[:, d4 * 128:(d4 + 1) * 128], idf[:, :],
                    start=True, stop=True),
                    waits=w, incs=[("s_ptq_w", 1)])
            add("vector", lambda v, j=j, half=half: v.scalar_tensor_tensor(
                x_own[:, j, half * 512:(half + 1) * 512].rearrange(
                    "p (c f) -> p c f", f=128),
                ptq.rearrange("p (b w) -> p b w", w=512)[:, :, 0:128], 1.0,
                x_own[:, j, half * 512:(half + 1) * 512].rearrange(
                    "p (c f) -> p c f", f=128),
                ALU.mult, ALU.add),
                waits=wall("s_ptq_w"),
                incs=[("s_ptq_r", 1), ("s_res", 1)])

    if stop == "attn":
        for j in range(NJ):
            add("sync", lambda s, j=j: s.dma_start(outp[j], x_own[:, j, :]),
                waits=wall("s_res") + [("s_xq", 16 * NJ)], incs=[("s_out", 16)])
        add("sync", lambda s: s.wait_ge(bld.sem("s_out"), cur("s_out")))
        add("sync", lambda s: s.wait_ge(bld.sem("s_dbg_st"), cur("s_dbg_st")))
        bld.emit_block()
        p1b.close()
        p1c.close()
        ctx.close()
        return nc

    add("sync", lambda s: s.dma_start(yspill, x_own[:, :, :]),
        waits=wall("s_res"), incs=[("s_yspill", 16)])
    for e in ENGINES:
        add(e, lambda eng: eng.nop(), incs=[("s_p1", 1)])

    bld.emit_block()
    p1b.close()
    p1c.close()

    bar = wall("s_p1", "s_yspill")
    p2 = ExitStack()
    ln2T = sbt(p2, "ln2T", [128, DC, 1024], BF16)
    h_sb = sbt(p2, "h_sb", [128, FT, 1024], BF16)
    yffn = sbt(p2, "yffn", [128, DC, 1024], F32)
    wg_sb = sbt(p2, "wg_sb", [128, 2, 4, DC, 128], BF16)
    wu_sb = sbt(p2, "wu_sb", [128, 2, 4, DC, 128], BF16)
    wd_sb = sbt(p2, "wd_sb", [128, FT, DC, 128], BF16)
    pb_sb = sbt(p2, "pb_sb", [128, 2, 1024], BF16)
    pf_sb = sbt(p2, "pf_sb", [128, 8, NE], BF16)
    prT = sbt(p2, "prT", [8, 1024], BF16)
    y_rot = sbt(p2, "y_rot", [128, 2, D], F32)
    ln2_tmp = sbt(p2, "ln2_tmp", [128, D], BF16)
    silu_sb = sbt(p2, "silu_sb", [128, 512], BF16)
    u_sb = sbt(p2, "u_sb", [128, 512], BF16)
    h1_sb = sbt(p2, "h1_sb", [128, 512], BF16)
    fin_sb = sbt(p2, "fin_sb", [128, 2, D], F32)
    rw_sb = sbt(p2, "rw_sb", [128, DC, NE], BF16)
    esel_sb = sbt(p2, "esel_sb", [8, NE, 128], BF16)
    bd_sb = sbt(p2, "bd_sb", [8, DC, 128], BF16)
    bgT_sb = sbt(p2, "bgT_sb", [128, NE, FT], F32)
    buT_sb = sbt(p2, "buT_sb", [128, NE, FT], F32)
    l_sb = sbt(p2, "l_sb", [128, NE], F32)
    E_sb = sbt(p2, "E_sb", [128, NE], F32)
    t8_sb = sbt(p2, "t8_sb", [128, 8], F32)
    keep_sb = sbt(p2, "keep_sb", [128, NE], F32)
    pm_sb = sbt(p2, "pm_sb", [128, NE], F32)
    Z_sb = sbt(p2, "Z_sb", [128, 1], F32)
    S2_sb = sbt(p2, "S2_sb", [128, 1], F32)
    den_sb = sbt(p2, "den_sb", [128, 1], F32)
    rden_sb = sbt(p2, "rden_sb", [128, 1], F32)

    pg = [pst(p2, "pg0"), pst(p2, "pg1")]
    pu = [pst(p2, "pu0"), pst(p2, "pu1")]
    py = [pst(p2, "py0"), pst(p2, "py1")]
    pxq = p2.enter_context(nc.psum_tensor("pxq", [128, 1024], F32)).ap()
    pxv = pxq.rearrange("p (b w) -> p b w", w=512)

    add("sync", lambda s: s.dma_start(bgT_sb[:, :, :], bgT),
        waits=bar, incs=[("s_c2", 16)])
    add("sync", lambda s: s.dma_start(buT_sb[:, :, :], buT), incs=[("s_c2", 16)])
    add("gpsimd", lambda g: g.dma_start(rw_sb[:, :, :], rw),
        waits=bar, incs=[("s_c2g", 16)])
    add("gpsimd", lambda g: g.dma_start(esel_sb[:, :, :], eselin),
        incs=[("s_c2g", 16)])
    add("gpsimd", lambda g: g.dma_start(bd_sb[:, :, :], bdr),
        incs=[("s_c2g", 16)])

    first = {"tensor": True, "vector": True}

    def bw(eng, extra=()):
        w = list(extra)
        if first.get(eng):
            w += bar
            first[eng] = False
        return w

    for g in range(2):
        for jl in range(8):
            jt = g * 8 + jl
            tp = jt % 2
            yn = (cur("s_y0") + cur("s_y1")) // 16
            w = ([("s_y_r", yn - 1)] if yn >= 2 else []) + \
                (wall("s_yspill") if (g == 0 and jl == 0) else [])
            add("sync", lambda s, jt=jt, tp=tp: s.dma_start(
                y_rot[:, tp, :], yspill[:, jt, :]), waits=w,
                incs=[("s_y%d" % tp, 16)])
            emit_ln(y_rot[:, tp, :], ln2_tmp[:, :],
                    bw("vector", wall("s_y%d" % tp)),
                    apply_waits=bld.mwait(("ln2tmp", jt - 1)),
                    apply_incs=[("s_y_r", 1)])
            for pr in range(4):
                for c2 in range(2):
                    c = 2 * pr + c2
                    w = []
                    if c2 == 0:
                        w = bw("tensor", wall("s_pxq_r"))
                        if pr == 0:
                            w += wall("s_lnapply")
                    add("tensor", lambda t, c=c, c2=c2: t.matmul(
                        pxq[:, c2 * 512:c2 * 512 + 128],
                        ln2_tmp[:, c * 128:(c + 1) * 128], idb[:, :],
                        start=True, stop=True),
                        waits=w, incs=[("s_pxq_w", 1)])
                    if pr == 3 and c2 == 1:
                        bld.mark(("ln2tmp", jt), "s_pxq_w")
                add("scalar", lambda s, jl=jl, pr=pr: s.copy(
                    ln2T[:, 2 * pr:2 * pr + 2, jl * 128:(jl + 1) * 128],
                    pxv[:, :, 0:128]),
                    waits=wall("s_pxq_w"),
                    incs=[("s_pxq_r", 1), ("s_ln2T", 1)])

        for jl in range(8):
            for c in range(DC):
                w = wall("s_pxq_r", "s_c2g") if c == 0 else []
                add("tensor", lambda t, jl=jl, c=c: t.matmul(
                    pxq[:, 0:NE], ln2T[:, c, jl * 128:(jl + 1) * 128],
                    rw_sb[:, c, :], start=(c == 0), stop=(c == DC - 1)),
                    waits=w, incs=[("s_pxq_w", 1)] if c == DC - 1 else ())
            vch(lambda v: v.tensor_copy(l_sb[:, :], pxq[:, 0:NE]),
                extra=wall("s_pxq_w", "s_rexp"), incs=[("s_pxq_r", 1)])
            vch(lambda v: v.reduce_max(nm[:, :], l_sb[:, :],
                                       axis=AX, negate=True),
                incs=[("s_rnm", 1)])
            sch(lambda s: s.activation(
                E_sb[:, :], l_sb[:, :], AF.Exp, bias=nm[:, 0:1],
                accum_out=Z_sb[:, 0:1]),
                extra=wall("s_rnm"), incs=[("s_rexp", 1)])
            vch(lambda v: v.max(t8_sb[:, :], E_sb[:, :]),
                extra=wall("s_rexp"))
            vch(lambda v: v.tensor_scalar(
                keep_sb[:, :], E_sb[:, :], t8_sb[:, 1:2], None, ALU.is_ge))
            vch(lambda v: v.tensor_mul(pm_sb[:, :], E_sb[:, :], keep_sb[:, :]))
            vch(lambda v: v.reduce_sum(S2_sb[:, :], pm_sb[:, :], axis=AX))
            vch(lambda v: v.scalar_tensor_tensor(
                den_sb[:, :], Z_sb[:, :], 1e-9, S2_sb[:, :],
                ALU.mult, ALU.add))
            vch(lambda v: v.reciprocal(rden_sb[:, :], den_sb[:, :]))
            vch(lambda v, jl=jl: v.tensor_scalar(
                pf_sb[:, jl, :], pm_sb[:, :], rden_sb[:, 0:1], None, ALU.mult),
                incs=[("s_rout", 1)])
        for st in range(4):
            for q2 in range(2):
                jl = 2 * st + q2
                w = wall("s_rout", "s_pxq_r") if q2 == 0 else []
                add("tensor", lambda t, jl=jl, q2=q2: t.matmul(
                    pxq[0:8, q2 * 512:q2 * 512 + 128],
                    pf_sb[:, jl, :], idb[:, :], start=True, stop=True),
                    waits=w, incs=[("s_pxq_w", 1)])
            add("scalar", lambda s, st=st: s.copy(
                prT[:, st * 256:(st + 1) * 256],
                pxv[0:8, :, 0:128]),
                waits=wall("s_pxq_w"),
                incs=[("s_pxq_r", 1), ("s_prT", 1)])

        for e in range(NE):
            ep = e % 2
            seq = g * NE + e
            def emit_wgu_dma(q, e=e, seq=seq):
                sG = 4 * seq + q
                sp = sG % 2
                assert sG < 2 or ("wgr", sG - 2) in bld.marks
                add("gpsimd", lambda gg, e=e, q=q, sp=sp: gg.dma_start(
                    wg_sb[:, sp, :, :, :], wg[e][:, 4 * q:4 * q + 4, :, :]),
                    waits=bld.mwait(("wgr", sG - 2)) +
                          (bar if sG == 0 else []),
                    incs=[("s_wg%d" % sp, 16)])
                add("gpsimd", lambda gg, e=e, q=q, sp=sp: gg.dma_start(
                    wu_sb[:, sp, :, :, :], wu[e][:, 4 * q:4 * q + 4, :, :]),
                    waits=bld.mwait(("wur", sG - 2)),
                    incs=[("s_wu%d" % sp, 16)])
            emit_wgu_dma(0)
            emit_wgu_dma(1)
            add("gpsimd", lambda gg, e=e: gg.dma_start(
                wd_sb[:, :, :, :], wd[e]),
                waits=bld.mwait(("wdr", seq - 1)),
                incs=[("s_wd", 16)])
            for half in range(2):
                w = wall("s_prT", "s_c2g")
                if half == 0:
                    w += wall("s_pxq_r")
                add("tensor", lambda t, e=e, half=half: t.matmul(
                    pxq[:, half * 512:(half + 1) * 512], esel_sb[:, e, :],
                    prT[:, half * 512:(half + 1) * 512],
                    start=True, stop=True),
                    waits=w,
                    incs=[("s_pxq_w", 1)])
                add("scalar", lambda s, ep=ep, half=half: s.copy(
                    pb_sb[:, ep, half * 512:(half + 1) * 512],
                    pxq[:, half * 512:(half + 1) * 512]),
                    waits=wall("s_pxq_w"),
                    incs=[("s_pxq_r", 1), ("s_pb", 1)])
            for ft in range(FT):
                q = ft // 4
                sG = 4 * seq + q
                sp = sG % 2
                for nh in range(2):
                    pgb, pub = pg[nh], pu[nh]
                    ng, nu = "s_pg%d" % nh, "s_pu%d" % nh
                    for c in range(DC):
                        w = []
                        if c == 0:
                            w = wall(ng + "_r")
                            if ft % 4 == 0 and nh == 0:
                                w += [("s_wg%d" % sp, 16 * (sG // 2 + 1))]
                            if ft == 0 and nh == 0:
                                w += wall("s_ln2T")
                        add("tensor", lambda t, sp=sp, ft=ft, c=c, nh=nh,
                            pgb=pgb: t.matmul(
                            pgb[:, :], wg_sb[:, sp, ft % 4, c, :],
                            ln2T[:, c, nh * 512:(nh + 1) * 512],
                            start=(c == 0), stop=(c == DC - 1)),
                            waits=w,
                            incs=[(ng + "_w", 1)] if c == DC - 1 else [])
                        if ft % 4 == 3 and nh == 1 and c == DC - 1:
                            bld.mark(("wgr", sG), ng + "_w")
                    for c in range(DC):
                        w = []
                        if c == 0:
                            w = wall(nu + "_r")
                            if ft % 4 == 0 and nh == 0:
                                w += [("s_wu%d" % sp, 16 * (sG // 2 + 1))]
                        add("tensor", lambda t, sp=sp, ft=ft, c=c, nh=nh,
                            pub=pub: t.matmul(
                            pub[:, :], wu_sb[:, sp, ft % 4, c, :],
                            ln2T[:, c, nh * 512:(nh + 1) * 512],
                            start=(c == 0), stop=(c == DC - 1)),
                            waits=w,
                            incs=[(nu + "_w", 1)] if c == DC - 1 else [])
                        if ft % 4 == 3 and nh == 1 and c == DC - 1:
                            bld.mark(("wur", sG), nu + "_w")
                            if ft < 8:
                                emit_wgu_dma(ft // 4 + 2)
                    add("scalar", lambda s, e=e, ft=ft, pgb=pgb: s.activation(
                        silu_sb[:, :], pgb[:, :], AF.Silu,
                        bias=bgT_sb[:, e, ft:ft + 1]),
                        waits=wall(ng + "_w", "s_h1", "s_c2"),
                        incs=[("s_silu", 1), (ng + "_r", 1)])
                    add("scalar", lambda s, e=e, ft=ft, pub=pub: s.activation(
                        u_sb[:, :], pub[:, :], AF.Identity,
                        bias=buT_sb[:, e, ft:ft + 1]),
                        waits=wall(nu + "_w", "s_h1"),
                        incs=[("s_usb", 1), (nu + "_r", 1)])
                    add("vector", lambda v: v.tensor_mul(
                        h1_sb[:, :], silu_sb[:, :], u_sb[:, :]),
                        waits=bw("vector", wall("s_silu", "s_usb")),
                        incs=[("s_h1", 1)])
                    add("vector", lambda v, ep=ep, ft=ft, nh=nh: v.tensor_mul(
                        h_sb[:, ft, nh * 512:(nh + 1) * 512], h1_sb[:, :],
                        pb_sb[:, ep, nh * 512:(nh + 1) * 512]),
                        waits=wall("s_pb", "s_h1") +
                              bld.mwait(("wdr", seq - 1)),
                        incs=[("s_h", 1)])
            for dtile in range(DC):
                for nh in range(2):
                    pyp = nh
                    pyb = py[pyp]
                    nyp = "s_py%d" % pyp
                    if e == 0:
                        add("tensor", lambda t, dtile=dtile, nh=nh, pyb=pyb: t.matmul(
                            pyb[:, :], bd_sb[:, dtile, :],
                            prT[:, nh * 512:(nh + 1) * 512],
                            start=True, stop=False),
                            waits=wall(nyp + "_r", "s_prT", "s_c2g"))
                    for fc in range(FT):
                        w = []
                        if fc == 0:
                            if e != 0:
                                w = wall(nyp + "_r")
                            if dtile == 0 and nh == 0:
                                w += wall("s_h") + [("s_wd", 16 * (seq + 1))]
                        add("tensor", lambda t, fc=fc, dtile=dtile,
                            nh=nh, pyb=pyb, e=e: t.matmul(
                            pyb[:, :], wd_sb[:, fc, dtile, :],
                            h_sb[:, fc, nh * 512:(nh + 1) * 512],
                            start=(fc == 0 and e != 0), stop=(fc == FT - 1)),
                            waits=w,
                            incs=[(nyp + "_w", 1)] if fc == FT - 1 else [])
                        if fc == FT - 1 and dtile == DC - 1 and nh == 1:
                            bld.mark(("wdr", seq), nyp + "_w")
                    if e == 0:
                        add("vector", lambda v, dtile=dtile, nh=nh, pyb=pyb:
                            v.tensor_copy(
                            yffn[:, dtile, nh * 512:(nh + 1) * 512], pyb[:, :]),
                            waits=wall(nyp + "_w") +
                                  bld.mwait(("finT", g - 1)),
                            incs=[(nyp + "_r", 1), ("s_yf", 1)])
                    else:
                        prevyf = g * 128 + (e - 1) * 16 + dtile * 2 + nh + 1
                        add("vector", lambda v, dtile=dtile, nh=nh, pyb=pyb:
                            v.scalar_tensor_tensor(
                            yffn[:, dtile, nh * 512:(nh + 1) * 512],
                            pyb[:, :], 1.0,
                            yffn[:, dtile, nh * 512:(nh + 1) * 512],
                            ALU.mult, ALU.add),
                            waits=wall(nyp + "_w") + [("s_yf", prevyf)],
                            incs=[(nyp + "_r", 1), ("s_yf", 1)])

        for jl in range(8):
            jt = g * 8 + jl
            tp = jt % 2
            yn = (cur("s_y0") + cur("s_y1")) // 16
            add("sync", lambda s, jt=jt, tp=tp: s.dma_start(
                y_rot[:, tp, :], yspill[:, jt, :]),
                waits=[("s_y_r", yn - 1)], incs=[("s_y%d" % tp, 16)])
            for pr in range(4):
                for d2 in range(2):
                    dtile = 2 * pr + d2
                    w = []
                    if d2 == 0:
                        w = wall("s_pxq_r")
                        if jl == 0 and pr == 0:
                            w += wall("s_yf")
                    add("tensor", lambda t, dtile=dtile, d2=d2, jl=jl: t.matmul(
                        pxq[:, d2 * 512:d2 * 512 + 128],
                        yffn[:, dtile, jl * 128:(jl + 1) * 128], idf[:, :],
                        start=True, stop=True),
                        waits=w, incs=[("s_pxq_w", 1)])
                    if jl == 7 and pr == 3 and d2 == 1:
                        bld.mark(("finT", g), "s_pxq_w")
                add("vector", lambda v, tp=tp, pr=pr:
                    v.scalar_tensor_tensor(
                    fin_sb[:, tp, pr * 256:(pr + 1) * 256].rearrange(
                        "p (c f) -> p c f", f=128),
                    pxv[:, :, 0:128], 1.0,
                    y_rot[:, tp, pr * 256:(pr + 1) * 256].rearrange(
                        "p (c f) -> p c f", f=128),
                    ALU.mult, ALU.add),
                    waits=wall("s_pxq_w", "s_y%d" % tp) +
                          ([("s_out%d" % tp, 16 * (jt // 2))] if (jt >= 2 and pr == 0) else []),
                    incs=[("s_pxq_r", 1), ("s_fin", 1)] +
                         ([("s_y_r", 1)] if pr == 3 else []))
            add("sync", lambda s, jt=jt, tp=tp: s.dma_start(
                outp[jt], fin_sb[:, tp, :]),
                waits=wall("s_fin"), incs=[("s_out%d" % tp, 16)])

    if stop is not None:
        add("gpsimd", lambda g: g.dma_start(
            dbgF, ln2T.rearrange("p a b -> p (a b)")),
            waits=wall("s_yf"), incs=[("s_dbg2", 16)])
        add("gpsimd", lambda g: g.dma_start(
            dbgY, yffn.rearrange("p a b -> p (a b)")),
            incs=[("s_dbg2", 16)])
        add("gpsimd", lambda g: g.dma_start(
            dbgP[:, 0:2048], pb_sb.rearrange("p a b -> p (a b)")),
            incs=[("s_dbg2", 16)])
        add("gpsimd", lambda g: g.dma_start(
            dbgP[0:8, 2048:3072], prT[:, :]), incs=[("s_dbg2", 16)])
        add("gpsimd", lambda g: g.dma_start(
            dbgP[:, 3072:3136], pf_sb.rearrange("p a b -> p (a b)")),
            incs=[("s_dbg2", 16)])
        add("gpsimd", lambda g: g.dma_start(
            dbgH, h_sb.rearrange("p a b -> p (a b)")),
            incs=[("s_dbg2", 16)])
        add("sync", lambda s: s.wait_ge(bld.sem("s_dbg2"), cur("s_dbg2")))
    add("sync", lambda s: s.wait_ge(bld.sem("s_out0"), cur("s_out0")))
    add("sync", lambda s: s.wait_ge(bld.sem("s_out1"), cur("s_out1")))

    bld.emit_block()
    p2.close()
    ctx.close()
    return nc



def _pe_table():
    pos = np.arange(T, dtype=np.float32)[:, None]
    inv = np.exp(-np.float32(math.log(10000.0)) *
                 np.arange(0, HD, 2, dtype=np.float32) / np.float32(HD))
    ang = pos * inv[None, :].astype(np.float32)
    pe = np.zeros((T, HD), dtype=np.float32)
    pe[:, 0::2] = np.sin(ang)
    pe[:, 1::2] = np.cos(ang)
    return pe


def stage_inputs(L0_attn_x, L0_attn_Wq, L0_attn_bq, L0_attn_Wkv, L0_attn_bkv,
                 L0_attn_Wo, L0_ffn_router_w, L0_ffn_gate_w, L0_ffn_gate_b,
                 L0_ffn_up_w, L0_ffn_up_b, L0_ffn_down_w, L0_ffn_down_b):
    f32 = lambda a: np.ascontiguousarray(np.asarray(a), dtype=np.float32)
    x = f32(L0_attn_x)
    Wq, bq, Wkv, bkv, Wo = map(f32, (L0_attn_Wq, L0_attn_bq, L0_attn_Wkv,
                                     L0_attn_bkv, L0_attn_Wo))
    rwv, gw, gb, uw, ub, dw, db = map(f32, (
        L0_ffn_router_w, L0_ffn_gate_w, L0_ffn_gate_b, L0_ffn_up_w,
        L0_ffn_up_b, L0_ffn_down_w, L0_ffn_down_b))

    C = np.ascontiguousarray
    shared = {
        "wq": C(Wq.reshape(DC, 128, H, 128).transpose(1, 2, 0, 3)),
        "wkv": C(Wkv.reshape(DC, 128, 2, 128).transpose(1, 2, 0, 3)),
        "wo": C(Wo.reshape(H, 128, DC, 128).transpose(1, 0, 2, 3)),
        "wg": C(gw.reshape(NE, DC, 128, FT, 128).transpose(0, 2, 3, 1, 4)),
        "wu": C(uw.reshape(NE, DC, 128, FT, 128).transpose(0, 2, 3, 1, 4)),
        "wd": C(dw.reshape(NE, FT, 128, DC, 128).transpose(0, 2, 1, 3, 4)),
        "rw": C(rwv.reshape(NE, DC, 128).transpose(2, 1, 0)),
        "bqr": C(bq.reshape(1, H, 128)),
        "bkvr": C(bkv.reshape(1, 2, 128)),
        "bgT": C(gb.reshape(NE, FT, 128).transpose(2, 0, 1)),
        "buT": C(ub.reshape(NE, FT, 128).transpose(2, 0, 1)),
        "bdr": C(db.reshape(NE, DC, 128)),
        "idbin": np.eye(128, dtype=np.float32),
        "idfin": np.eye(128, dtype=np.float32),
        "onesin": np.ones((1, 128), dtype=np.float32),
        "eselin": C(np.eye(NE, dtype=np.float32)[:, :, None]
                    * np.ones((1, 1, 128), np.float32)),
    }
    pe = _pe_table()
    shared["pek"] = C(pe.T)
    pes = (pe * np.float32(SCALE)).astype(np.float32)

    in_maps = []
    for c in range(8):
        g, r = c // 4, c % 4
        xg = x[g].reshape(NT, 128, D)
        qi = np.arange(128, dtype=np.float32)[:, None]
        kb = np.arange(512, dtype=np.float32)[None, :]
        maskv = np.where(kb <= r * 128 + qi, 0.0, NEG).astype(np.float32)
        own = [4 * j + r for j in range(NJ)]
        peov = C(np.stack([pes[t * 128:(t + 1) * 128, :].T for t in own]
                          ).transpose(1, 0, 2))
        m = dict(shared)
        m["xk"] = C(xg)
        m["xq"] = C(xg[own])
        m["peo"] = peov
        m["maskin"] = maskv
        in_maps.append(m)
    return in_maps


def kernel(**inputs):
    in_maps = stage_inputs(**inputs)
    nc = build_nc()
    res = run_bass_kernel_spmd(nc, in_maps, core_ids=list(range(8)))
    out = np.empty((B, T, D), dtype=np.float32)
    for c in range(8):
        g, r = c // 4, c % 4
        out[g].reshape(NT, 128, D)[r::4] = res.results[c]["out"]
    return out


if __name__ == "__main__":
    import reference
    inputs = reference.setup_inputs()
    got = kernel(**{k: np.asarray(v) for k, v in inputs.items()})
    want = np.asarray(reference.reference(**inputs))
    err = np.sqrt(((got - want) ** 2).mean() / (want ** 2).mean())
    print("resvar-rel-l2", err)


# version 08
# speedup vs baseline: 1.0937x; 1.0016x over previous
import math
from contextlib import ExitStack

import numpy as np

import concourse.bass as bass
import concourse.mybir as mybir
from concourse.bass_utils import run_bass_kernel_spmd

F32 = mybir.dt.float32
BF16 = mybir.dt.bfloat16
AF = mybir.ActivationFunctionType
ALU = mybir.AluOpType
AX = mybir.AxisListType.X

B, T, D, H, HD, NE, FF = 2, 8192, 1024, 8, 128, 8, 2048
NJ = 16
NT = 64
DC = D // 128
FT = FF // 128
SCALE = 1.0 / math.sqrt(HD)
EPS = 1e-5
NEG = -1.0e30
ENGINES = ("sync", "tensor", "vector", "scalar", "gpsimd")


class Builder:

    def __init__(self, nc, ctx):
        self.nc = nc
        self.ctx = ctx
        self.ops = {e: [] for e in ENGINES}
        self.h = {}
        self.cnt = {}

    sem_name = {}

    def sem(self, name):
        if name not in self.h:
            self.h[name] = self.ctx.enter_context(self.nc.semaphore(name))
            self.cnt[name] = 0
            self.sem_name[id(self.h[name])] = name
        return self.h[name]

    def cur(self, name):
        self.sem(name)
        return self.cnt[name]

    def add(self, engine, make, waits=(), incs=()):
        wl = [(self.sem(n), v) for n, v in waits if v > 0]
        il = []
        for n, k in incs:
            self.sem(n)
            self.cnt[n] += k
            il.append((self.h[n], k))
        self.ops[engine].append((wl, make, il))

    def wall(self, *names):
        return [(n, self.cur(n)) for n in names]

    marks = {}

    def mark(self, key, name):
        self.marks[key] = (name, self.cur(name))

    def mwait(self, key):
        return [self.marks[key]] if key in self.marks else []

    def emit_block(self):
        ops = self.ops
        self.ops = {e: [] for e in ENGINES}

        def replay(engine, eng):
            RELEASE = {"s_ptq_r", "s_pxq_r", "s_qkr", "s_qkr0",
                       "s_qkr1", "s_atr0", "s_atr1", "s_kvk_r", "s_kvv_r",
                       "s_xkv_r", "s_pek_r", "s_y_r", "s_pg0_r", "s_pg1_r",
                       "s_pu0_r", "s_pu1_r", "s_py0_r", "s_py1_r"}
            via_drain = engine in ("vector", "scalar")
            for wl, make, il in ops[engine]:
                for s, v in wl:
                    eng.wait_ge(s, v)
                inst = make(eng)
                if engine == "tensor":
                    assert len(il) <= 1, "tensor ops carry at most one inc"
                if not via_drain:
                    for i, (s, k) in enumerate(il):
                        if i == 0:
                            inst.then_inc(s, k)
                        else:
                            eng.nop().then_inc(s, k)
                    continue
                rel = [x for x in il
                       if self.sem_name.get(id(x[0]), "") in RELEASE]
                dat = [x for x in il
                       if self.sem_name.get(id(x[0]), "") not in RELEASE]
                for i, (s, k) in enumerate(rel):
                    if i == 0:
                        inst.then_inc(s, k)
                    else:
                        eng.nop().then_inc(s, k)
                for i, (s, k) in enumerate(dat):
                    if i == 0:
                        eng.drain().then_inc(s, k)
                    else:
                        eng.nop().then_inc(s, k)

        with self.nc.Block() as block:
            @block.sync
            def _(eng):
                replay("sync", eng)

            @block.tensor
            def _(eng):
                replay("tensor", eng)

            @block.vector
            def _(eng):
                replay("vector", eng)

            @block.scalar
            def _(eng):
                replay("scalar", eng)

            @block.gpsimd
            def _(eng):
                replay("gpsimd", eng)


def build_nc(stop=None):
    nc = bass.Bass("TRN2", target_bir_lowering=False,
                   detect_race_conditions=False)
    dt = nc.dram_tensor
    if stop is not None:
        dbgK = dt("dbgK", [128, T], F32, kind="ExternalOutput").ap()
        dbgV = dt("dbgV", [128, NT * 128], F32, kind="ExternalOutput").ap()
        dbgL = dt("dbgL", [128, 2048], F32, kind="ExternalOutput").ap()
        dbgI = dt("dbgI", [128, 256], F32, kind="ExternalOutput").ap()
        dbgF = dt("dbgF", [128, 8192], F32, kind="ExternalOutput").ap()
        dbgY = dt("dbgY", [128, 8192], F32, kind="ExternalOutput").ap()
        dbgP = dt("dbgP", [128, 4096], F32, kind="ExternalOutput").ap()
        dbgH = dt("dbgH", [128, FT * 1024], F32, kind="ExternalOutput").ap()

    xq = dt("xq", [NJ, 128, D], F32, kind="ExternalInput").ap()
    xk = dt("xk", [NT, 128, D], F32, kind="ExternalInput").ap()
    wq = dt("wq", [128, H, DC, 128], F32, kind="ExternalInput").ap()
    wkv = dt("wkv", [128, 2, DC, 128], F32, kind="ExternalInput").ap()
    wo = dt("wo", [128, H, DC, 128], F32, kind="ExternalInput").ap()
    wg = dt("wg", [NE, 128, FT, DC, 128], F32, kind="ExternalInput").ap()
    wu = dt("wu", [NE, 128, FT, DC, 128], F32, kind="ExternalInput").ap()
    wd = dt("wd", [NE, 128, FT, DC, 128], F32, kind="ExternalInput").ap()
    rw = dt("rw", [128, DC, NE], F32, kind="ExternalInput").ap()
    bqr = dt("bqr", [1, H, 128], F32, kind="ExternalInput").ap()
    bkvr = dt("bkvr", [1, 2, 128], F32, kind="ExternalInput").ap()
    bgT = dt("bgT", [128, NE, FT], F32, kind="ExternalInput").ap()
    buT = dt("buT", [128, NE, FT], F32, kind="ExternalInput").ap()
    bdr = dt("bdr", [8, DC, 128], F32, kind="ExternalInput").ap()
    pek = dt("pek", [128, T], F32, kind="ExternalInput").ap()
    peo = dt("peo", [128, NJ, 128], F32, kind="ExternalInput").ap()
    maskin = dt("maskin", [128, 512], F32, kind="ExternalInput").ap()
    idbin = dt("idbin", [128, 128], F32, kind="ExternalInput").ap()
    idfin = dt("idfin", [128, 128], F32, kind="ExternalInput").ap()
    onesin = dt("onesin", [1, 128], F32, kind="ExternalInput").ap()
    eselin = dt("eselin", [8, NE, 128], F32, kind="ExternalInput").ap()
    outp = dt("out", [NJ, 128, D], F32, kind="ExternalOutput").ap()
    yspill = dt("yspill", [128, NJ, D], F32, kind="Internal").ap()

    ctx = ExitStack()
    bld = Builder(nc, ctx)
    add, cur, wall = bld.add, bld.cur, bld.wall

    def sbt(stack, name, shape, dtp):
        return stack.enter_context(nc.sbuf_tensor(name, shape, dtp)).ap()

    def pst(stack, name):
        return stack.enter_context(nc.psum_tensor(name, [128, 512], F32)).ap()

    idb = sbt(ctx, "idb", [128, 128], BF16)
    idf = sbt(ctx, "idf", [128, 128], F32)
    ones_b = sbt(ctx, "ones_b", [1, 128], BF16)
    mask_sb = sbt(ctx, "mask_sb", [128, 512], F32)
    sqscr = sbt(ctx, "sqscr", [128, 1024], BF16)
    lnsum = sbt(ctx, "lnsum", [128, 1], F32)
    lnssq = sbt(ctx, "lnssq", [128, 1], F32)
    lnmean = sbt(ctx, "lnmean", [128, 1], F32)
    lnmsq = sbt(ctx, "lnmsq", [128, 1], F32)
    veps = sbt(ctx, "veps", [128, 1], F32)
    lnt = sbt(ctx, "lnt", [128, 1], F32)
    rstd = sbt(ctx, "rstd", [128, 1], F32)
    nmr = sbt(ctx, "nmr", [128, 1], F32)
    nm = sbt(ctx, "nm", [128, 1], F32)

    lncount = [0]

    def vch(make, extra=(), incs=()):
        add("vector", make, waits=list(extra) + wall("s_vch"),
            incs=[("s_vch", 1)] + list(incs))

    def sch(make, extra=(), incs=()):
        add("scalar", make, waits=list(extra) + wall("s_sch"),
            incs=[("s_sch", 1)] + list(incs))

    def emit_ln(x_ap, out_ap, dep_waits, apply_waits=(), apply_incs=()):
        lncount[0] += 1
        w0 = list(dep_waits) + wall("s_lnapply")
        vch(lambda v, x=x_ap: v.reduce_sum(
            lnsum[:, :], x[:, :], axis=AX), extra=w0)
        sch(lambda s, x=x_ap: s.activation(
            sqscr[:, :], x[:, :], AF.Square, accum_out=lnssq[:, 0:1]),
            extra=list(dep_waits) + wall("s_lnnmr"),
            incs=[("s_lnsq", 1)])
        vch(lambda v: v.tensor_scalar(
            lnmean[:, :], lnsum[:, :], 1.0 / D, None, ALU.mult))
        vch(lambda v: v.tensor_mul(lnmsq[:, :], lnmean[:, :], lnmean[:, :]))
        vch(lambda v: v.scalar_tensor_tensor(
            veps[:, :], lnssq[:, :], 1.0 / D, lnmsq[:, :],
            ALU.mult, ALU.subtract),
            extra=wall("s_lnsq"))
        vch(lambda v: v.tensor_scalar(
            veps[:, :], veps[:, :], EPS, None, ALU.add),
            incs=[("s_lnstat", 1)])
        sch(lambda s: s.activation(lnt[:, :], veps[:, :], AF.Ln),
            extra=wall("s_lnstat"))
        sch(lambda s: s.activation(rstd[:, :], lnt[:, :], AF.Exp, scale=-0.5),
            incs=[("s_lnrstd", 1)])
        vch(lambda v: v.scalar_tensor_tensor(
            nmr[:, :], lnmean[:, :], -1.0, rstd[:, :], ALU.mult, ALU.mult),
            extra=wall("s_lnrstd"), incs=[("s_lnnmr", 1)])
        sch(lambda s, x=x_ap, o=out_ap: s.activation(
            o, x, AF.Identity, bias=nmr[:, 0:1], scale=rstd[:, 0:1]),
            extra=list(apply_waits) + wall("s_lnnmr"),
            incs=[("s_lnapply", 1)] + list(apply_incs))

    p1c = ExitStack()
    x_own = sbt(p1c, "x_own", [128, NJ, D], F32)
    wq_sb = sbt(p1c, "wq_sb", [128, H, DC, 128], BF16)
    wo_sb = sbt(p1c, "wo_sb", [128, H, DC, 128], BF16)
    wkv_sb = sbt(p1c, "wkv_sb", [128, 2, DC, 128], BF16)
    bq_sb = sbt(p1c, "bq_sb", [1, H, 128], BF16)
    bkv_sb = sbt(p1c, "bkv_sb", [1, 2, 128], BF16)
    K_sb = sbt(p1c, "K_sb", [128, T], BF16)
    V_sb = sbt(p1c, "V_sb", [128, NT, 128], BF16)

    p1a = ExitStack()
    x_kv = sbt(p1a, "x_kv", [128, 2, D], F32)
    pek_sb = sbt(p1a, "pek_sb", [128, 2, 512], F32)
    ln_kv = sbt(p1a, "ln_kv", [128, 2, D], BF16)
    lnT_kv = sbt(p1a, "lnT_kv", [128, 2, DC, 128], BF16)
    vt_tmp = sbt(p1a, "vt_tmp", [128, 2, 128], BF16)
    ptr = p1a.enter_context(nc.psum_tensor("ptr", [128, 2048], F32)).ap()
    pkv = p1a.enter_context(nc.psum_tensor("pkv", [128, 1024], F32)).ap()

    add("sync", lambda s: s.dma_start(idf[:, :], idfin), incs=[("s_c", 16)])
    add("sync", lambda s: s.dma_start(mask_sb[:, :], maskin), incs=[("s_c", 16)])
    add("gpsimd", lambda g: g.dma_start(idb[:, :], idbin), incs=[("s_cg", 16)])
    add("gpsimd", lambda g: g.dma_start(ones_b[:, :], onesin), incs=[("s_cg", 16)])
    add("gpsimd", lambda g: g.dma_start(wq_sb[:, :, :, :], wq), incs=[("s_w", 16)])
    add("gpsimd", lambda g: g.dma_start(wkv_sb[:, :, :, :], wkv), incs=[("s_w", 16)])
    add("gpsimd", lambda g: g.dma_start(wo_sb[:, :, :, :], wo), incs=[("s_w", 16)])
    add("gpsimd", lambda g: g.dma_start(bq_sb[:, :, :], bqr), incs=[("s_w", 16)])
    add("gpsimd", lambda g: g.dma_start(bkv_sb[:, :, :], bkvr), incs=[("s_w", 16)])
    for j in range(NJ):
        add("sync", lambda s, j=j: s.dma_start(x_own[:, j, :], xq[j]),
            incs=[("s_xq", 16)])

    for j in range(NJ):
        jp = j % 2
        add("sync", lambda s, j=j, jp=jp: s.dma_start(
            pek_sb[:, jp, :], pek[:, 512 * j:512 * (j + 1)]),
            waits=[("s_pek_r", j - 1)] if j >= 2 else (),
            incs=[("s_pek%d" % jp, 16)])
        for ti in range(4):
            tt = 4 * j + ti
            tp = tt % 2
            add("sync", lambda s, tt=tt, tp=tp: s.dma_start(
                x_kv[:, tp, :], xk[tt]),
                waits=[("s_xkv_r", tt - 1)] if tt >= 2 else (),
                incs=[("s_xkv%d" % tp, 16)])
            emit_ln(x_kv[:, tp, :], ln_kv[:, tp, :],
                    wall("s_xkv%d" % tp),
                    apply_waits=bld.mwait(("lnkv", tt - 2)),
                    apply_incs=[("s_xkv_r", 1)])
            for half in range(2):
                for c4 in range(4):
                    c = 4 * half + c4
                    w = []
                    if c4 == 0:
                        w = wall("s_ptr_r")
                        if tt == 0 and half == 0:
                            w += wall("s_c", "s_cg")
                        if half == 0:
                            w += wall("s_lnapply")
                    add("tensor", lambda t, tp=tp, c=c, c4=c4: t.matmul(
                        ptr[:, c4 * 512:c4 * 512 + 128],
                        ln_kv[:, tp, c * 128:(c + 1) * 128], idb[:, :],
                        start=True, stop=True),
                        waits=w, incs=[("s_ptr_w", 1)])
                    if half == 1 and c4 == 3:
                        bld.mark(("lnkv", tt), "s_ptr_w")
                add("scalar", lambda s, tp=tp, half=half: s.copy(
                    lnT_kv[:, tp, 4 * half:4 * half + 4, :],
                    ptr.rearrange("p (b w) -> p b w", w=512)[:, :, 0:128]),
                    waits=wall("s_ptr_w"), incs=[("s_ptr_r", 1)])
            for f in range(2):
                for c in range(DC):
                    w = []
                    if f == 0 and c == 0:
                        w = wall("s_ptr_r", "s_w") + \
                            ([("s_kvk_r", tt)] if tt >= 1 else [])
                    if f == 1 and c == 0:
                        w = [("s_kvv_r", tt)] if tt >= 1 else []
                    add("tensor", lambda t, f=f, c=c, tp=tp: t.matmul(
                        pkv[:, f * 512:f * 512 + 128],
                        wkv_sb[:, f, c, :], lnT_kv[:, tp, c, :],
                        start=(c == 0), stop=False), waits=w)
                add("tensor", lambda t, f=f: t.matmul(
                    pkv[:, f * 512:f * 512 + 128],
                    bkv_sb[:, f, :], ones_b[:, :], start=False, stop=True),
                    incs=[("s_kvk_w" if f == 0 else "s_kvv_w", 1)])
            add("vector", lambda v, tt=tt, jp=jp, ti=ti: v.scalar_tensor_tensor(
                K_sb[:, tt * 128:(tt + 1) * 128], pkv[:, 0:128], 1.0,
                pek_sb[:, jp, ti * 128:(ti + 1) * 128], ALU.mult, ALU.add),
                waits=wall("s_kvk_w", "s_pek%d" % jp),
                incs=[("s_kv", 1), ("s_kvk_r", 1)] +
                     ([("s_pek_r", 1)] if ti == 3 else []))
            add("scalar", lambda s, tp=tp: s.copy(
                vt_tmp[:, tp, :], pkv[:, 512:640]),
                waits=wall("s_kvv_w"),
                incs=[("s_vt", 1), ("s_kvv_r", 1)])
            add("tensor", lambda t, tp=tp: t.matmul(
                ptr[:, 0:128], vt_tmp[:, tp, :], idb[:, :],
                start=True, stop=True),
                waits=wall("s_vt", "s_ptr_r"),
                incs=[("s_ptr_w", 1)])
            add("vector", lambda v, tt=tt: v.tensor_copy(
                V_sb[:, tt, :], ptr[:, 0:128]),
                waits=wall("s_ptr_w"),
                incs=[("s_kv", 1), ("s_ptr_r", 1)])

    for e in ENGINES:
        add(e, lambda eng: eng.nop(), incs=[("s_p1a", 1)])

    if stop is not None:
        add("gpsimd", lambda g: g.dma_start(dbgK, K_sb[:, :]),
            waits=[("s_p1a", 5)], incs=[("s_dbg_st", 16)])
        add("gpsimd", lambda g: g.dma_start(
            dbgV, V_sb.rearrange("p a b -> p (a b)")),
            incs=[("s_dbg_st", 16)])
        add("gpsimd", lambda g: g.dma_start(
            dbgL[:, 0:1024], ln_kv[:, 1, :]), incs=[("s_dbg_st", 16)])
        add("gpsimd", lambda g: g.dma_start(
            dbgL[:, 1024:2048],
            lnT_kv[:, 1, :, :].rearrange("p c f -> p (c f)")),
            incs=[("s_dbg_st", 16)])
        add("gpsimd", lambda g: g.dma_start(dbgI[:, 0:128], idb[:, :]),
            incs=[("s_dbg_st", 16)])
        add("gpsimd", lambda g: g.dma_start(dbgI[:, 128:256], vt_tmp[:, 1, :]),
            incs=[("s_dbg_st", 16)])
        if stop == "kv":
            for j in range(NJ):
                add("sync", lambda s, j=j: s.dma_start(
                    outp[j], x_own[:, j, :]),
                    waits=[("s_xq", 16 * NJ)], incs=[("s_out", 16)])
            add("sync", lambda s: s.wait_ge(bld.sem("s_out"), cur("s_out")))
            add("sync", lambda s: s.wait_ge(bld.sem("s_dbg_st"), cur("s_dbg_st")))
            bld.emit_block()
            p1a.close()
            p1c.close()
            ctx.close()
            return nc

    bld.emit_block()
    p1a.close()

    bar1 = wall("s_p1a")
    p1b = ExitStack()
    peo_sb = sbt(p1b, "peo_sb", [128, NJ, 128], F32)
    scores = sbt(p1b, "scores", [128, T], BF16)
    probs = sbt(p1b, "probs", [128, T], BF16)
    ln_own = sbt(p1b, "ln_own", [128, D], BF16)
    ln1T = sbt(p1b, "ln1T", [128, 2, DC, 128], BF16)
    qT_sb = sbt(p1b, "qT_sb", [128, 2, 128], BF16)
    pT_sb = sbt(p1b, "pT_sb", [128, 2, 512], BF16)
    attnT = sbt(p1b, "attnT", [128, 2, H, 128], BF16)
    nmb = sbt(p1b, "nmb", [128, 16], F32)
    ssum = sbt(p1b, "ssum", [128, 1], F32)
    rs = sbt(p1b, "rs", [128, 1], F32)
    diag = sbt(p1b, "diag", [128, 128], BF16)
    tmpf = sbt(p1b, "tmpf", [128, 512], F32)

    pqk = [pst(p1b, "pqk0"), pst(p1b, "pqk1")]
    pat = [p1b.enter_context(nc.psum_tensor("pat0", [128, 128], F32)).ap(),
           p1b.enter_context(nc.psum_tensor("pat1", [128, 128], F32)).ap()]
    ptq = p1b.enter_context(nc.psum_tensor("ptq", [128, 2048], F32)).ap()

    add("sync", lambda s: s.dma_start(peo_sb[:, :, :], peo),
        waits=bar1, incs=[("s_peo", 16)])

    first1 = {"tensor": True, "vector": True}

    def bw1(eng, extra=()):
        w = list(extra)
        if first1.get(eng):
            w += bar1
            first1[eng] = False
        return w

    _G = globals()
    for j in range(_G.get("ATTN_J", NJ)):
        jp = j % 2
        kext = 512 * (j + 1)
        nkb = j + 1
        emit_ln(x_own[:, j, :], ln_own[:, :], bw1("vector", wall("s_xq")),
                apply_waits=bld.mwait(("lnown", j - 1)))
        for half in range(2):
            for c4 in range(4):
                c = 4 * half + c4
                w = []
                if c4 == 0:
                    w = bw1("tensor", wall("s_ptq_r"))
                    if half == 0:
                        w += wall("s_lnapply")
                add("tensor", lambda t, c=c, c4=c4: t.matmul(
                    ptq[:, c4 * 512:c4 * 512 + 128],
                    ln_own[:, c * 128:(c + 1) * 128], idb[:, :],
                    start=True, stop=True),
                    waits=w, incs=[("s_ptq_w", 1)])
                if half == 1 and c4 == 3:
                    bld.mark(("lnown", j), "s_ptq_w")
            add("scalar", lambda s, jp=jp, half=half: s.copy(
                ln1T[:, jp, 4 * half:4 * half + 4, :],
                ptq.rearrange("p (b w) -> p b w", w=512)[:, :, 0:128]),
                waits=wall("s_ptq_w"), incs=[("s_ptq_r", 1), ("s_ln1T", 1)])

        STG = _G.get("ATTN_STAGE", 5)
        def emit_front(h):
            hp = h % 2
            seq = 8 * j + h
            for c in range(DC):
                w = wall("s_ptq_r", "s_ln1T", "s_w") if c == 0 else []
                add("tensor", lambda t, h=h, c=c, jp=jp: t.matmul(
                    ptq[:, 0:128], wq_sb[:, h, c, :], ln1T[:, jp, c, :],
                    start=(c == 0), stop=False), waits=w)
            add("tensor", lambda t, h=h: t.matmul(
                ptq[:, 0:128], bq_sb[:, h, :], ones_b[:, :],
                start=False, stop=True), incs=[("s_ptq_w", 1)])
            add("vector", lambda v, hp=hp, j=j: v.scalar_tensor_tensor(
                qT_sb[:, hp, :], ptq[:, 0:128], SCALE, peo_sb[:, j, :],
                ALU.mult, ALU.add),
                waits=wall("s_ptq_w", "s_peo") + bld.mwait(("qTr", seq - 2)),
                incs=[("s_qT", 1), ("s_ptq_r", 1)])
            if STG < 3:
                return
            for kb in range(nkb):
                bp = kb % 2
                w = wall("s_qT") + [("s_qkr%d" % bp, cur("s_qkr%d" % bp))]
                if seq == 0 and kb == 0:
                    w += [("s_kv", 2 * NT)]
                add("tensor", lambda t, hp=hp, kb=kb, bp=bp: t.matmul(
                    pqk[bp][:, :], qT_sb[:, hp, :],
                    K_sb[:, kb * 512:(kb + 1) * 512],
                    start=True, stop=True),
                    waits=w, incs=[("s_qkw%d" % bp, 1)])
                if kb == nkb - 1:
                    bld.mark(("qTr", seq), "s_qkw%d" % bp)
                if kb < nkb - 1:
                    add("vector", lambda v, kb=kb, bp=bp: v.tensor_scalar(
                        scores[:, kb * 512:(kb + 1) * 512], pqk[bp][:, :],
                        1.0, None, ALU.mult, op1=ALU.max,
                        accum_out=nmb[:, kb:kb + 1]),
                        waits=wall("s_qkw%d" % bp, "s_exp"),
                        incs=[("s_qkr%d" % bp, 1)])
                else:
                    add("vector", lambda v, kb=kb, bp=bp: v.scalar_tensor_tensor(
                        scores[:, kb * 512:(kb + 1) * 512], pqk[bp][:, :],
                        1.0, mask_sb[:, :], ALU.mult, ALU.add),
                        waits=wall("s_qkw%d" % bp, "s_exp", "s_c"),
                        incs=[("s_qkr%d" % bp, 1)])
                    add("vector", lambda v, kb=kb: v.reduce_max(
                        nmb[:, kb:kb + 1],
                        scores[:, kb * 512:(kb + 1) * 512], axis=AX),
                        waits=[("s_qkr%d" % bp, cur("s_qkr%d" % bp))])
            add("vector", lambda v, nkb=nkb: v.reduce_max(
                nm[:, :], nmb[:, 0:nkb], axis=AX, negate=True),
                waits=wall("s_qkr0", "s_qkr1"),
                incs=[("s_nm", 1)])

        def emit_exp(h):
            seq = 8 * j + h
            add("scalar", lambda s, kext=kext: s.activation(
                probs[:, 0:kext], scores[:, 0:kext], AF.Exp,
                bias=nm[:, 0:1], accum_out=ssum[:, 0:1]),
                waits=[("s_nm", seq + 1)] + bld.mwait(("T", seq - 1)),
                incs=[("s_exp", 1)])

        def emit_tail(h):
            hp = h % 2
            seq = 8 * j + h
            add("vector", lambda v: v.reciprocal(rs[:, :], ssum[:, :]),
                waits=[("s_exp", seq + 1)], incs=[("s_rs", 1)])
            add("vector", lambda v: v.tensor_scalar(
                diag[:, :], idb[:, :], rs[:, 0:1], None, ALU.mult),
                waits=[("s_rs", seq + 1)] + bld.mwait(("T", seq - 1)) +
                      (wall("s_cg") if seq == 0 else []),
                incs=[("s_diag", 1)])
            if STG < 4:
                return
            SUB = _G.get("ATTN_SUB", 3)
            for kb in range(nkb):
                tp = kb % 2
                for c4 in range(4):
                    w = []
                    if c4 == 0:
                        w = [("s_diag", seq + 1), ("s_exp", seq + 1)] + \
                            wall("s_ptq_r")
                    add("tensor", lambda t, kb=kb, c4=c4: t.matmul(
                        ptq[:, c4 * 512:c4 * 512 + 128],
                        probs[:, kb * 512 + c4 * 128:kb * 512 + (c4 + 1) * 128],
                        diag[:, :], start=True, stop=True),
                        waits=w, incs=[("s_ptq_w", 1)])
                    if kb == nkb - 1 and c4 == 3:
                        bld.mark(("T", seq), "s_ptq_w")
                if tp == 0:
                    add("scalar", lambda s, tp=tp: s.copy(
                        pT_sb[:, tp, :],
                        ptq.rearrange("p (b w) -> p b w", w=512)[:, :, 0:128]),
                        waits=wall("s_ptq_w") +
                              [("s_pta%d" % tp, cur("s_pta%d" % tp))],
                        incs=[("s_ptq_r", 1), ("s_pte%d" % tp, 1)])
                else:
                    add("vector", lambda v, tp=tp: v.tensor_copy(
                        pT_sb[:, tp, :],
                        ptq.rearrange("p (b w) -> p b w", w=512)[:, :, 0:128]),
                        waits=wall("s_ptq_w") +
                              [("s_pta%d" % tp, cur("s_pta%d" % tp))],
                        incs=[("s_ptq_r", 1), ("s_pte%d" % tp, 1)])
                if SUB < 2:
                    continue
                for c4 in range(4):
                    w = []
                    if c4 == 0:
                        w = wall("s_pte%d" % tp)
                        if kb == 0:
                            w += [("s_atr%d" % hp, cur("s_atr%d" % hp))]
                    add("tensor", lambda t, kb=kb, c4=c4, tp=tp, hp=hp,
                        last=(kb == nkb - 1): t.matmul(
                        pat[hp][:, :], V_sb[:, 4 * kb + c4, :],
                        pT_sb[:, tp, c4 * 128:(c4 + 1) * 128],
                        start=(kb == 0 and c4 == 0),
                        stop=(last and c4 == 3)),
                        waits=w,
                        incs=[("s_pta%d" % tp, 1)] if c4 == 3 else [])
            if SUB >= 3:
                add("vector", lambda v, jp=jp, h=h, hp=hp: v.tensor_copy(
                    attnT[:, jp, h, :], pat[hp][:, :]),
                    waits=wall("s_pta0", "s_pta1"),
                    incs=[("s_atr%d" % hp, 1), ("s_at", 1)])

        STG = _G.get("ATTN_STAGE", 5)
        for h in range(H if STG >= 2 else 0):
            emit_front(h)
            if STG >= 3 and h >= 1:
                emit_tail(h - 1)
            if STG >= 3:
                emit_exp(h)
        if STG >= 3:
            emit_tail(H - 1)

        for half in range(2 if STG >= 5 else 0):
            for d4 in range(4):
                dtile = 4 * half + d4
                for h in range(H):
                    w = []
                    if d4 == 0 and h == 0:
                        w = wall("s_at", "s_ptq_r")
                    add("tensor", lambda t, h=h, dtile=dtile, d4=d4, jp=jp: t.matmul(
                        ptq[:, d4 * 512:d4 * 512 + 128],
                        wo_sb[:, h, dtile, :], attnT[:, jp, h, :],
                        start=(h == 0), stop=(h == H - 1)),
                        waits=w,
                        incs=[("s_ptq_w", 1)] if (d4 == 3 and h == H - 1) else ())
            add("scalar", lambda s: s.copy(
                tmpf[:, :],
                ptq.rearrange("p (b w) -> p b w", w=512)[:, :, 0:128]),
                waits=wall("s_ptq_w"),
                incs=[("s_ptq_r", 1), ("s_tmpf", 1)])
            for d4 in range(4):
                w = []
                if d4 == 0:
                    w = wall("s_tmpf", "s_ptq_r")
                add("tensor", lambda t, d4=d4: t.matmul(
                    ptq[:, d4 * 512:d4 * 512 + 128],
                    tmpf[:, d4 * 128:(d4 + 1) * 128], idf[:, :],
                    start=True, stop=True),
                    waits=w, incs=[("s_ptq_w", 1)])
            add("vector", lambda v, j=j, half=half: v.scalar_tensor_tensor(
                x_own[:, j, half * 512:(half + 1) * 512].rearrange(
                    "p (c f) -> p c f", f=128),
                ptq.rearrange("p (b w) -> p b w", w=512)[:, :, 0:128], 1.0,
                x_own[:, j, half * 512:(half + 1) * 512].rearrange(
                    "p (c f) -> p c f", f=128),
                ALU.mult, ALU.add),
                waits=wall("s_ptq_w"),
                incs=[("s_ptq_r", 1), ("s_res", 1)])

    if stop == "attn":
        for j in range(NJ):
            add("sync", lambda s, j=j: s.dma_start(outp[j], x_own[:, j, :]),
                waits=wall("s_res") + [("s_xq", 16 * NJ)], incs=[("s_out", 16)])
        add("sync", lambda s: s.wait_ge(bld.sem("s_out"), cur("s_out")))
        add("sync", lambda s: s.wait_ge(bld.sem("s_dbg_st"), cur("s_dbg_st")))
        bld.emit_block()
        p1b.close()
        p1c.close()
        ctx.close()
        return nc

    add("sync", lambda s: s.dma_start(yspill, x_own[:, :, :]),
        waits=wall("s_res"), incs=[("s_yspill", 16)])
    for e in ENGINES:
        add(e, lambda eng: eng.nop(), incs=[("s_p1", 1)])

    bld.emit_block()
    p1b.close()
    p1c.close()

    bar = wall("s_p1", "s_yspill")
    p2 = ExitStack()
    ln2T = sbt(p2, "ln2T", [128, DC, 1024], BF16)
    h_sb = sbt(p2, "h_sb", [128, FT, 1024], BF16)
    yffn = sbt(p2, "yffn", [128, DC, 1024], F32)
    wg_sb = sbt(p2, "wg_sb", [128, 2, 4, DC, 128], BF16)
    wu_sb = sbt(p2, "wu_sb", [128, 2, 4, DC, 128], BF16)
    wd_sb = sbt(p2, "wd_sb", [128, FT, DC, 128], BF16)
    pb_sb = sbt(p2, "pb_sb", [128, 2, 1024], BF16)
    pf_sb = sbt(p2, "pf_sb", [128, 8, NE], BF16)
    prT = sbt(p2, "prT", [8, 1024], BF16)
    y_rot = sbt(p2, "y_rot", [128, 2, D], F32)
    ln2_tmp = sbt(p2, "ln2_tmp", [128, D], BF16)
    silu_sb = sbt(p2, "silu_sb", [128, 512], BF16)
    u_sb = sbt(p2, "u_sb", [128, 512], BF16)
    h1_sb = sbt(p2, "h1_sb", [128, 512], BF16)
    fin_sb = sbt(p2, "fin_sb", [128, 2, D], F32)
    rw_sb = sbt(p2, "rw_sb", [128, DC, NE], BF16)
    esel_sb = sbt(p2, "esel_sb", [8, NE, 128], BF16)
    bd_sb = sbt(p2, "bd_sb", [8, DC, 128], BF16)
    bgT_sb = sbt(p2, "bgT_sb", [128, NE, FT], F32)
    buT_sb = sbt(p2, "buT_sb", [128, NE, FT], F32)
    l_sb = sbt(p2, "l_sb", [128, NE], F32)
    E_sb = sbt(p2, "E_sb", [128, NE], F32)
    t8_sb = sbt(p2, "t8_sb", [128, 8], F32)
    keep_sb = sbt(p2, "keep_sb", [128, NE], F32)
    pm_sb = sbt(p2, "pm_sb", [128, NE], F32)
    Z_sb = sbt(p2, "Z_sb", [128, 1], F32)
    S2_sb = sbt(p2, "S2_sb", [128, 1], F32)
    den_sb = sbt(p2, "den_sb", [128, 1], F32)
    rden_sb = sbt(p2, "rden_sb", [128, 1], F32)

    pg = [pst(p2, "pg0"), pst(p2, "pg1")]
    pu = [pst(p2, "pu0"), pst(p2, "pu1")]
    py = [pst(p2, "py0"), pst(p2, "py1")]
    pxq = p2.enter_context(nc.psum_tensor("pxq", [128, 1024], F32)).ap()
    pxv = pxq.rearrange("p (b w) -> p b w", w=512)

    add("sync", lambda s: s.dma_start(bgT_sb[:, :, :], bgT),
        waits=bar, incs=[("s_c2", 16)])
    add("sync", lambda s: s.dma_start(buT_sb[:, :, :], buT), incs=[("s_c2", 16)])
    add("gpsimd", lambda g: g.dma_start(rw_sb[:, :, :], rw),
        waits=bar, incs=[("s_c2g", 16)])
    add("gpsimd", lambda g: g.dma_start(esel_sb[:, :, :], eselin),
        incs=[("s_c2g", 16)])
    add("gpsimd", lambda g: g.dma_start(bd_sb[:, :, :], bdr),
        incs=[("s_c2g", 16)])

    first = {"tensor": True, "vector": True}

    def bw(eng, extra=()):
        w = list(extra)
        if first.get(eng):
            w += bar
            first[eng] = False
        return w

    for g in range(2):
        for jl in range(8):
            jt = g * 8 + jl
            tp = jt % 2
            yn = (cur("s_y0") + cur("s_y1")) // 16
            w = ([("s_y_r", yn - 1)] if yn >= 2 else []) + \
                (wall("s_yspill") if (g == 0 and jl == 0) else [])
            add("sync", lambda s, jt=jt, tp=tp: s.dma_start(
                y_rot[:, tp, :], yspill[:, jt, :]), waits=w,
                incs=[("s_y%d" % tp, 16)])
            emit_ln(y_rot[:, tp, :], ln2_tmp[:, :],
                    bw("vector", wall("s_y%d" % tp)),
                    apply_waits=bld.mwait(("ln2tmp", jt - 1)),
                    apply_incs=[("s_y_r", 1)])
            for pr in range(4):
                for c2 in range(2):
                    c = 2 * pr + c2
                    w = []
                    if c2 == 0:
                        w = bw("tensor", wall("s_pxq_r"))
                        if pr == 0:
                            w += wall("s_lnapply")
                    add("tensor", lambda t, c=c, c2=c2: t.matmul(
                        pxq[:, c2 * 512:c2 * 512 + 128],
                        ln2_tmp[:, c * 128:(c + 1) * 128], idb[:, :],
                        start=True, stop=True),
                        waits=w, incs=[("s_pxq_w", 1)])
                    if pr == 3 and c2 == 1:
                        bld.mark(("ln2tmp", jt), "s_pxq_w")
                add("scalar", lambda s, jl=jl, pr=pr: s.copy(
                    ln2T[:, 2 * pr:2 * pr + 2, jl * 128:(jl + 1) * 128],
                    pxv[:, :, 0:128]),
                    waits=wall("s_pxq_w"),
                    incs=[("s_pxq_r", 1), ("s_ln2T", 1)])

        for jl in range(8):
            for c in range(DC):
                w = wall("s_pxq_r", "s_c2g") if c == 0 else []
                add("tensor", lambda t, jl=jl, c=c: t.matmul(
                    pxq[:, 0:NE], ln2T[:, c, jl * 128:(jl + 1) * 128],
                    rw_sb[:, c, :], start=(c == 0), stop=(c == DC - 1)),
                    waits=w, incs=[("s_pxq_w", 1)] if c == DC - 1 else ())
            vch(lambda v: v.tensor_copy(l_sb[:, :], pxq[:, 0:NE]),
                extra=wall("s_pxq_w", "s_rexp"), incs=[("s_pxq_r", 1)])
            vch(lambda v: v.reduce_max(nm[:, :], l_sb[:, :],
                                       axis=AX, negate=True),
                incs=[("s_rnm", 1)])
            sch(lambda s: s.activation(
                E_sb[:, :], l_sb[:, :], AF.Exp, bias=nm[:, 0:1],
                accum_out=Z_sb[:, 0:1]),
                extra=wall("s_rnm"), incs=[("s_rexp", 1)])
            vch(lambda v: v.max(t8_sb[:, :], E_sb[:, :]),
                extra=wall("s_rexp"))
            vch(lambda v: v.tensor_scalar(
                keep_sb[:, :], E_sb[:, :], t8_sb[:, 1:2], None, ALU.is_ge))
            vch(lambda v: v.tensor_mul(pm_sb[:, :], E_sb[:, :], keep_sb[:, :]))
            vch(lambda v: v.reduce_sum(S2_sb[:, :], pm_sb[:, :], axis=AX))
            vch(lambda v: v.scalar_tensor_tensor(
                den_sb[:, :], Z_sb[:, :], 1e-9, S2_sb[:, :],
                ALU.mult, ALU.add))
            vch(lambda v: v.reciprocal(rden_sb[:, :], den_sb[:, :]))
            vch(lambda v, jl=jl: v.tensor_scalar(
                pf_sb[:, jl, :], pm_sb[:, :], rden_sb[:, 0:1], None, ALU.mult),
                incs=[("s_rout", 1)])
        for st in range(4):
            for q2 in range(2):
                jl = 2 * st + q2
                w = wall("s_rout", "s_pxq_r") if q2 == 0 else []
                add("tensor", lambda t, jl=jl, q2=q2: t.matmul(
                    pxq[0:8, q2 * 512:q2 * 512 + 128],
                    pf_sb[:, jl, :], idb[:, :], start=True, stop=True),
                    waits=w, incs=[("s_pxq_w", 1)])
            add("scalar", lambda s, st=st: s.copy(
                prT[:, st * 256:(st + 1) * 256],
                pxv[0:8, :, 0:128]),
                waits=wall("s_pxq_w"),
                incs=[("s_pxq_r", 1), ("s_prT", 1)])

        for e in range(NE):
            ep = e % 2
            seq = g * NE + e
            def emit_wgu_dma(q, e=e, seq=seq):
                sG = 4 * seq + q
                sp = sG % 2
                assert sG < 2 or ("wgr", sG - 2) in bld.marks
                add("gpsimd", lambda gg, e=e, q=q, sp=sp: gg.dma_start(
                    wg_sb[:, sp, :, :, :], wg[e][:, 4 * q:4 * q + 4, :, :]),
                    waits=bld.mwait(("wgr", sG - 2)) +
                          (bar if sG == 0 else []),
                    incs=[("s_wg%d" % sp, 16)])
                add("gpsimd", lambda gg, e=e, q=q, sp=sp: gg.dma_start(
                    wu_sb[:, sp, :, :, :], wu[e][:, 4 * q:4 * q + 4, :, :]),
                    waits=bld.mwait(("wur", sG - 2)),
                    incs=[("s_wu%d" % sp, 16)])
            emit_wgu_dma(0)
            emit_wgu_dma(1)
            add("gpsimd", lambda gg, e=e: gg.dma_start(
                wd_sb[:, :, :, :], wd[e]),
                waits=bld.mwait(("wdr", seq - 1)),
                incs=[("s_wd", 16)])
            for half in range(2):
                w = wall("s_prT", "s_c2g")
                if half == 0:
                    w += wall("s_pxq_r")
                add("tensor", lambda t, e=e, half=half: t.matmul(
                    pxq[:, half * 512:(half + 1) * 512], esel_sb[:, e, :],
                    prT[:, half * 512:(half + 1) * 512],
                    start=True, stop=True),
                    waits=w,
                    incs=[("s_pxq_w", 1)])
                add("scalar", lambda s, ep=ep, half=half: s.copy(
                    pb_sb[:, ep, half * 512:(half + 1) * 512],
                    pxq[:, half * 512:(half + 1) * 512]),
                    waits=wall("s_pxq_w"),
                    incs=[("s_pxq_r", 1), ("s_pb", 1)])
            for ft in range(FT):
                q = ft // 4
                sG = 4 * seq + q
                sp = sG % 2
                for nh in range(2):
                    pgb, pub = pg[nh], pu[nh]
                    ng, nu = "s_pg%d" % nh, "s_pu%d" % nh
                    for c in range(DC):
                        w = []
                        if c == 0:
                            w = wall(ng + "_r")
                            if ft % 4 == 0 and nh == 0:
                                w += [("s_wg%d" % sp, 16 * (sG // 2 + 1))]
                            if ft == 0 and nh == 0:
                                w += wall("s_ln2T")
                        add("tensor", lambda t, sp=sp, ft=ft, c=c, nh=nh,
                            pgb=pgb: t.matmul(
                            pgb[:, :], wg_sb[:, sp, ft % 4, c, :],
                            ln2T[:, c, nh * 512:(nh + 1) * 512],
                            start=(c == 0), stop=(c == DC - 1)),
                            waits=w,
                            incs=[(ng + "_w", 1)] if c == DC - 1 else [])
                        if ft % 4 == 3 and nh == 1 and c == DC - 1:
                            bld.mark(("wgr", sG), ng + "_w")
                    for c in range(DC):
                        w = []
                        if c == 0:
                            w = wall(nu + "_r")
                            if ft % 4 == 0 and nh == 0:
                                w += [("s_wu%d" % sp, 16 * (sG // 2 + 1))]
                        add("tensor", lambda t, sp=sp, ft=ft, c=c, nh=nh,
                            pub=pub: t.matmul(
                            pub[:, :], wu_sb[:, sp, ft % 4, c, :],
                            ln2T[:, c, nh * 512:(nh + 1) * 512],
                            start=(c == 0), stop=(c == DC - 1)),
                            waits=w,
                            incs=[(nu + "_w", 1)] if c == DC - 1 else [])
                        if ft % 4 == 3 and nh == 1 and c == DC - 1:
                            bld.mark(("wur", sG), nu + "_w")
                            if ft < 8:
                                emit_wgu_dma(ft // 4 + 2)
                    add("scalar", lambda s, e=e, ft=ft, pgb=pgb: s.activation(
                        silu_sb[:, :], pgb[:, :], AF.Silu,
                        bias=bgT_sb[:, e, ft:ft + 1]),
                        waits=wall(ng + "_w", "s_h1", "s_c2"),
                        incs=[("s_silu", 1), (ng + "_r", 1)])
                    add("scalar", lambda s, e=e, ft=ft, pub=pub: s.activation(
                        u_sb[:, :], pub[:, :], AF.Identity,
                        bias=buT_sb[:, e, ft:ft + 1]),
                        waits=wall(nu + "_w", "s_h1"),
                        incs=[("s_usb", 1), (nu + "_r", 1)])
                    add("vector", lambda v: v.tensor_mul(
                        h1_sb[:, :], silu_sb[:, :], u_sb[:, :]),
                        waits=bw("vector", wall("s_silu", "s_usb")),
                        incs=[("s_h1", 1)])
                    add("vector", lambda v, ep=ep, ft=ft, nh=nh: v.tensor_mul(
                        h_sb[:, ft, nh * 512:(nh + 1) * 512], h1_sb[:, :],
                        pb_sb[:, ep, nh * 512:(nh + 1) * 512]),
                        waits=wall("s_pb", "s_h1") +
                              bld.mwait(("wdr", seq - 1)),
                        incs=[("s_h", 1)])
            for dtile in range(DC):
                for nh in range(2):
                    pyp = nh
                    pyb = py[pyp]
                    nyp = "s_py%d" % pyp
                    if e == 0:
                        add("tensor", lambda t, dtile=dtile, nh=nh, pyb=pyb: t.matmul(
                            pyb[:, :], bd_sb[:, dtile, :],
                            prT[:, nh * 512:(nh + 1) * 512],
                            start=True, stop=False),
                            waits=wall(nyp + "_r", "s_prT", "s_c2g"))
                    for fc in range(FT):
                        w = []
                        if fc == 0:
                            if e != 0:
                                w = wall(nyp + "_r")
                            if dtile == 0 and nh == 0:
                                w += wall("s_h") + [("s_wd", 16 * (seq + 1))]
                        add("tensor", lambda t, fc=fc, dtile=dtile,
                            nh=nh, pyb=pyb, e=e: t.matmul(
                            pyb[:, :], wd_sb[:, fc, dtile, :],
                            h_sb[:, fc, nh * 512:(nh + 1) * 512],
                            start=(fc == 0 and e != 0), stop=(fc == FT - 1)),
                            waits=w,
                            incs=[(nyp + "_w", 1)] if fc == FT - 1 else [])
                        if fc == FT - 1 and dtile == DC - 1 and nh == 1:
                            bld.mark(("wdr", seq), nyp + "_w")
                    if e == 0:
                        add("vector", lambda v, dtile=dtile, nh=nh, pyb=pyb:
                            v.tensor_copy(
                            yffn[:, dtile, nh * 512:(nh + 1) * 512], pyb[:, :]),
                            waits=wall(nyp + "_w") +
                                  bld.mwait(("finT", g - 1)),
                            incs=[(nyp + "_r", 1), ("s_yf", 1)])
                    else:
                        prevyf = g * 128 + (e - 1) * 16 + dtile * 2 + nh + 1
                        add("vector", lambda v, dtile=dtile, nh=nh, pyb=pyb:
                            v.scalar_tensor_tensor(
                            yffn[:, dtile, nh * 512:(nh + 1) * 512],
                            pyb[:, :], 1.0,
                            yffn[:, dtile, nh * 512:(nh + 1) * 512],
                            ALU.mult, ALU.add),
                            waits=wall(nyp + "_w") + [("s_yf", prevyf)],
                            incs=[(nyp + "_r", 1), ("s_yf", 1)])

        for jl in range(8):
            jt = g * 8 + jl
            tp = jt % 2
            yn = (cur("s_y0") + cur("s_y1")) // 16
            add("sync", lambda s, jt=jt, tp=tp: s.dma_start(
                y_rot[:, tp, :], yspill[:, jt, :]),
                waits=[("s_y_r", yn - 1)], incs=[("s_y%d" % tp, 16)])
            for pr in range(4):
                for d2 in range(2):
                    dtile = 2 * pr + d2
                    w = []
                    if d2 == 0:
                        w = wall("s_pxq_r")
                        if jl == 0 and pr == 0:
                            w += wall("s_yf")
                    add("tensor", lambda t, dtile=dtile, d2=d2, jl=jl: t.matmul(
                        pxq[:, d2 * 512:d2 * 512 + 128],
                        yffn[:, dtile, jl * 128:(jl + 1) * 128], idf[:, :],
                        start=True, stop=True),
                        waits=w, incs=[("s_pxq_w", 1)])
                    if jl == 7 and pr == 3 and d2 == 1:
                        bld.mark(("finT", g), "s_pxq_w")
                add("vector", lambda v, tp=tp, pr=pr:
                    v.scalar_tensor_tensor(
                    fin_sb[:, tp, pr * 256:(pr + 1) * 256].rearrange(
                        "p (c f) -> p c f", f=128),
                    pxv[:, :, 0:128], 1.0,
                    y_rot[:, tp, pr * 256:(pr + 1) * 256].rearrange(
                        "p (c f) -> p c f", f=128),
                    ALU.mult, ALU.add),
                    waits=wall("s_pxq_w", "s_y%d" % tp) +
                          ([("s_out%d" % tp, 16 * (jt // 2))] if (jt >= 2 and pr == 0) else []),
                    incs=[("s_pxq_r", 1), ("s_fin", 1)] +
                         ([("s_y_r", 1)] if pr == 3 else []))
            add("sync", lambda s, jt=jt, tp=tp: s.dma_start(
                outp[jt], fin_sb[:, tp, :]),
                waits=wall("s_fin"), incs=[("s_out%d" % tp, 16)])

    if stop is not None:
        add("gpsimd", lambda g: g.dma_start(
            dbgF, ln2T.rearrange("p a b -> p (a b)")),
            waits=wall("s_yf"), incs=[("s_dbg2", 16)])
        add("gpsimd", lambda g: g.dma_start(
            dbgY, yffn.rearrange("p a b -> p (a b)")),
            incs=[("s_dbg2", 16)])
        add("gpsimd", lambda g: g.dma_start(
            dbgP[:, 0:2048], pb_sb.rearrange("p a b -> p (a b)")),
            incs=[("s_dbg2", 16)])
        add("gpsimd", lambda g: g.dma_start(
            dbgP[0:8, 2048:3072], prT[:, :]), incs=[("s_dbg2", 16)])
        add("gpsimd", lambda g: g.dma_start(
            dbgP[:, 3072:3136], pf_sb.rearrange("p a b -> p (a b)")),
            incs=[("s_dbg2", 16)])
        add("gpsimd", lambda g: g.dma_start(
            dbgH, h_sb.rearrange("p a b -> p (a b)")),
            incs=[("s_dbg2", 16)])
        add("sync", lambda s: s.wait_ge(bld.sem("s_dbg2"), cur("s_dbg2")))
    add("sync", lambda s: s.wait_ge(bld.sem("s_out0"), cur("s_out0")))
    add("sync", lambda s: s.wait_ge(bld.sem("s_out1"), cur("s_out1")))

    bld.emit_block()
    p2.close()
    ctx.close()
    return nc



def _pe_table():
    pos = np.arange(T, dtype=np.float32)[:, None]
    inv = np.exp(-np.float32(math.log(10000.0)) *
                 np.arange(0, HD, 2, dtype=np.float32) / np.float32(HD))
    ang = pos * inv[None, :].astype(np.float32)
    pe = np.zeros((T, HD), dtype=np.float32)
    pe[:, 0::2] = np.sin(ang)
    pe[:, 1::2] = np.cos(ang)
    return pe


def stage_inputs(L0_attn_x, L0_attn_Wq, L0_attn_bq, L0_attn_Wkv, L0_attn_bkv,
                 L0_attn_Wo, L0_ffn_router_w, L0_ffn_gate_w, L0_ffn_gate_b,
                 L0_ffn_up_w, L0_ffn_up_b, L0_ffn_down_w, L0_ffn_down_b):
    f32 = lambda a: np.ascontiguousarray(np.asarray(a), dtype=np.float32)
    x = f32(L0_attn_x)
    Wq, bq, Wkv, bkv, Wo = map(f32, (L0_attn_Wq, L0_attn_bq, L0_attn_Wkv,
                                     L0_attn_bkv, L0_attn_Wo))
    rwv, gw, gb, uw, ub, dw, db = map(f32, (
        L0_ffn_router_w, L0_ffn_gate_w, L0_ffn_gate_b, L0_ffn_up_w,
        L0_ffn_up_b, L0_ffn_down_w, L0_ffn_down_b))

    C = np.ascontiguousarray
    shared = {
        "wq": C(Wq.reshape(DC, 128, H, 128).transpose(1, 2, 0, 3)),
        "wkv": C(Wkv.reshape(DC, 128, 2, 128).transpose(1, 2, 0, 3)),
        "wo": C(Wo.reshape(H, 128, DC, 128).transpose(1, 0, 2, 3)),
        "wg": C(gw.reshape(NE, DC, 128, FT, 128).transpose(0, 2, 3, 1, 4)),
        "wu": C(uw.reshape(NE, DC, 128, FT, 128).transpose(0, 2, 3, 1, 4)),
        "wd": C(dw.reshape(NE, FT, 128, DC, 128).transpose(0, 2, 1, 3, 4)),
        "rw": C(rwv.reshape(NE, DC, 128).transpose(2, 1, 0)),
        "bqr": C(bq.reshape(1, H, 128)),
        "bkvr": C(bkv.reshape(1, 2, 128)),
        "bgT": C(gb.reshape(NE, FT, 128).transpose(2, 0, 1)),
        "buT": C(ub.reshape(NE, FT, 128).transpose(2, 0, 1)),
        "bdr": C(db.reshape(NE, DC, 128)),
        "idbin": np.eye(128, dtype=np.float32),
        "idfin": np.eye(128, dtype=np.float32),
        "onesin": np.ones((1, 128), dtype=np.float32),
        "eselin": C(np.eye(NE, dtype=np.float32)[:, :, None]
                    * np.ones((1, 1, 128), np.float32)),
    }
    pe = _pe_table()
    shared["pek"] = C(pe.T)
    pes = (pe * np.float32(SCALE)).astype(np.float32)

    in_maps = []
    for c in range(8):
        g, r = c // 4, c % 4
        xg = x[g].reshape(NT, 128, D)
        qi = np.arange(128, dtype=np.float32)[:, None]
        kb = np.arange(512, dtype=np.float32)[None, :]
        maskv = np.where(kb <= r * 128 + qi, 0.0, NEG).astype(np.float32)
        own = [4 * j + r for j in range(NJ)]
        peov = C(np.stack([pes[t * 128:(t + 1) * 128, :].T for t in own]
                          ).transpose(1, 0, 2))
        m = dict(shared)
        m["xk"] = C(xg)
        m["xq"] = C(xg[own])
        m["peo"] = peov
        m["maskin"] = maskv
        in_maps.append(m)
    return in_maps


def kernel(**inputs):
    in_maps = stage_inputs(**inputs)
    nc = build_nc()
    res = run_bass_kernel_spmd(nc, in_maps, core_ids=list(range(8)))
    out = np.empty((B, T, D), dtype=np.float32)
    for c in range(8):
        g, r = c // 4, c % 4
        out[g].reshape(NT, 128, D)[r::4] = res.results[c]["out"]
    return out


if __name__ == "__main__":
    import reference
    inputs = reference.setup_inputs()
    got = kernel(**{k: np.asarray(v) for k, v in inputs.items()})
    want = np.asarray(reference.reference(**inputs))
    err = np.sqrt(((got - want) ** 2).mean() / (want ** 2).mean())
    print("resvar-rel-l2", err)


# version 11 (final)
# speedup vs baseline: 1.1073x; 1.0123x over previous
import math
from contextlib import ExitStack

import ml_dtypes
import numpy as np

import concourse.bass as bass
import concourse.mybir as mybir
from concourse.bass_utils import run_bass_kernel_spmd

F32 = mybir.dt.float32
BF16 = mybir.dt.bfloat16
AF = mybir.ActivationFunctionType
ALU = mybir.AluOpType
AX = mybir.AxisListType.X

B, T, D, H, HD, NE, FF = 2, 8192, 1024, 8, 128, 8, 2048
NJ = 16
NT = 64
DC = D // 128
FT = FF // 128
SCALE = 1.0 / math.sqrt(HD)
EPS = 1e-5
NEG = -1.0e30
ENGINES = ("sync", "tensor", "vector", "scalar", "gpsimd")


class Builder:

    def __init__(self, nc, ctx):
        self.nc = nc
        self.ctx = ctx
        self.ops = {e: [] for e in ENGINES}
        self.h = {}
        self.cnt = {}

    sem_name = {}

    def sem(self, name):
        if name not in self.h:
            self.h[name] = self.ctx.enter_context(self.nc.semaphore(name))
            self.cnt[name] = 0
            self.sem_name[id(self.h[name])] = name
        return self.h[name]

    def cur(self, name):
        self.sem(name)
        return self.cnt[name]

    def add(self, engine, make, waits=(), incs=()):
        wl = [(self.sem(n), v) for n, v in waits if v > 0]
        il = []
        for n, k in incs:
            self.sem(n)
            self.cnt[n] += k
            il.append((self.h[n], k))
        self.ops[engine].append((wl, make, il))

    def wall(self, *names):
        return [(n, self.cur(n)) for n in names]

    marks = {}

    def mark(self, key, name):
        self.marks[key] = (name, self.cur(name))

    def mwait(self, key):
        return [self.marks[key]] if key in self.marks else []

    def emit_block(self):
        ops = self.ops
        self.ops = {e: [] for e in ENGINES}

        def replay(engine, eng):
            RELEASE = {"s_ptq_r", "s_pxq_r", "s_qkr", "s_qkr0",
                       "s_qkr1", "s_atr0", "s_atr1", "s_kvk_r", "s_kvv_r",
                       "s_xkv_r", "s_pek_r", "s_y_r", "s_pg0_r", "s_pg1_r",
                       "s_pu0_r", "s_pu1_r", "s_py0_r", "s_py1_r"}
            via_drain = engine in ("vector", "scalar")
            for wl, make, il in ops[engine]:
                for s, v in wl:
                    eng.wait_ge(s, v)
                inst = make(eng)
                if engine == "tensor":
                    assert len(il) <= 1, "tensor ops carry at most one inc"
                if not via_drain:
                    for i, (s, k) in enumerate(il):
                        if i == 0:
                            inst.then_inc(s, k)
                        else:
                            eng.nop().then_inc(s, k)
                    continue
                rel = [x for x in il
                       if self.sem_name.get(id(x[0]), "") in RELEASE]
                dat = [x for x in il
                       if self.sem_name.get(id(x[0]), "") not in RELEASE]
                for i, (s, k) in enumerate(rel):
                    if i == 0:
                        inst.then_inc(s, k)
                    else:
                        eng.nop().then_inc(s, k)
                for i, (s, k) in enumerate(dat):
                    if i == 0:
                        eng.drain().then_inc(s, k)
                    else:
                        eng.nop().then_inc(s, k)

        with self.nc.Block() as block:
            @block.sync
            def _(eng):
                replay("sync", eng)

            @block.tensor
            def _(eng):
                replay("tensor", eng)

            @block.vector
            def _(eng):
                replay("vector", eng)

            @block.scalar
            def _(eng):
                replay("scalar", eng)

            @block.gpsimd
            def _(eng):
                replay("gpsimd", eng)


def build_nc(stop=None):
    nc = bass.Bass("TRN2", target_bir_lowering=False,
                   detect_race_conditions=False)
    dt = nc.dram_tensor
    if stop is not None:
        dbgK = dt("dbgK", [128, T], F32, kind="ExternalOutput").ap()
        dbgV = dt("dbgV", [128, NT * 128], F32, kind="ExternalOutput").ap()
        dbgL = dt("dbgL", [128, 2048], F32, kind="ExternalOutput").ap()
        dbgI = dt("dbgI", [128, 256], F32, kind="ExternalOutput").ap()
        dbgF = dt("dbgF", [128, 8192], F32, kind="ExternalOutput").ap()
        dbgY = dt("dbgY", [128, 8192], F32, kind="ExternalOutput").ap()
        dbgP = dt("dbgP", [128, 4096], F32, kind="ExternalOutput").ap()
        dbgH = dt("dbgH", [128, FT * 1024], F32, kind="ExternalOutput").ap()

    xq = dt("xq", [NJ, 128, D], F32, kind="ExternalInput").ap()
    xk = dt("xk", [NT, 128, D], F32, kind="ExternalInput").ap()
    wq = dt("wq", [128, H, DC, 128], F32, kind="ExternalInput").ap()
    wkv = dt("wkv", [128, 2, DC, 128], F32, kind="ExternalInput").ap()
    wo = dt("wo", [128, H, DC, 128], F32, kind="ExternalInput").ap()
    wg = dt("wg", [NE, 128, FT, DC, 128], F32, kind="ExternalInput").ap()
    wu = dt("wu", [NE, 128, FT, DC, 128], F32, kind="ExternalInput").ap()
    wd = dt("wd", [NE, 128, FT, DC, 128], F32, kind="ExternalInput").ap()
    rw = dt("rw", [128, DC, NE], F32, kind="ExternalInput").ap()
    bqr = dt("bqr", [1, H, 128], F32, kind="ExternalInput").ap()
    bkvr = dt("bkvr", [1, 2, 128], F32, kind="ExternalInput").ap()
    bgT = dt("bgT", [128, NE, FT], F32, kind="ExternalInput").ap()
    buT = dt("buT", [128, NE, FT], F32, kind="ExternalInput").ap()
    bdr = dt("bdr", [8, DC, 128], F32, kind="ExternalInput").ap()
    pek = dt("pek", [128, T], F32, kind="ExternalInput").ap()
    peo = dt("peo", [128, NJ, 128], BF16, kind="ExternalInput").ap()
    maskin = dt("maskin", [128, 512], F32, kind="ExternalInput").ap()
    idbin = dt("idbin", [128, 128], F32, kind="ExternalInput").ap()
    idfin = dt("idfin", [128, 128], F32, kind="ExternalInput").ap()
    onesin = dt("onesin", [1, 128], F32, kind="ExternalInput").ap()
    eselin = dt("eselin", [8, NE, 128], F32, kind="ExternalInput").ap()
    outp = dt("out", [NJ, 128, D], F32, kind="ExternalOutput").ap()
    yspill = dt("yspill", [128, NJ, D], F32, kind="Internal").ap()

    ctx = ExitStack()
    bld = Builder(nc, ctx)
    add, cur, wall = bld.add, bld.cur, bld.wall

    def sbt(stack, name, shape, dtp):
        return stack.enter_context(nc.sbuf_tensor(name, shape, dtp)).ap()

    def pst(stack, name):
        return stack.enter_context(nc.psum_tensor(name, [128, 512], F32)).ap()

    idb = sbt(ctx, "idb", [128, 128], BF16)
    idf = sbt(ctx, "idf", [128, 128], F32)
    ones_b = sbt(ctx, "ones_b", [1, 128], BF16)
    mask_sb = sbt(ctx, "mask_sb", [128, 512], F32)
    sqscr = sbt(ctx, "sqscr", [128, 1024], BF16)
    lnsum = sbt(ctx, "lnsum", [128, 1], F32)
    lnssq = sbt(ctx, "lnssq", [128, 1], F32)
    lnmean = sbt(ctx, "lnmean", [128, 1], F32)
    lnmsq = sbt(ctx, "lnmsq", [128, 1], F32)
    veps = sbt(ctx, "veps", [128, 1], F32)
    lnt = sbt(ctx, "lnt", [128, 1], F32)
    rstd = sbt(ctx, "rstd", [128, 1], F32)
    nmr = sbt(ctx, "nmr", [128, 1], F32)
    nm = sbt(ctx, "nm", [128, 1], F32)

    lncount = [0]

    def vch(make, extra=(), incs=()):
        add("vector", make, waits=list(extra) + wall("s_vch"),
            incs=[("s_vch", 1)] + list(incs))

    def sch(make, extra=(), incs=()):
        add("scalar", make, waits=list(extra) + wall("s_sch"),
            incs=[("s_sch", 1)] + list(incs))

    def emit_ln(x_ap, out_ap, dep_waits, apply_waits=(), apply_incs=()):
        lncount[0] += 1
        w0 = list(dep_waits) + wall("s_lnapply")
        vch(lambda v, x=x_ap: v.reduce_sum(
            lnsum[:, :], x[:, :], axis=AX), extra=w0)
        sch(lambda s, x=x_ap: s.activation(
            sqscr[:, :], x[:, :], AF.Square, accum_out=lnssq[:, 0:1]),
            extra=list(dep_waits) + wall("s_lnnmr"),
            incs=[("s_lnsq", 1)])
        vch(lambda v: v.tensor_scalar(
            lnmean[:, :], lnsum[:, :], 1.0 / D, None, ALU.mult))
        vch(lambda v: v.tensor_mul(lnmsq[:, :], lnmean[:, :], lnmean[:, :]))
        vch(lambda v: v.scalar_tensor_tensor(
            veps[:, :], lnssq[:, :], 1.0 / D, lnmsq[:, :],
            ALU.mult, ALU.subtract),
            extra=wall("s_lnsq"))
        vch(lambda v: v.tensor_scalar(
            veps[:, :], veps[:, :], EPS, None, ALU.add),
            incs=[("s_lnstat", 1)])
        sch(lambda s: s.activation(lnt[:, :], veps[:, :], AF.Ln),
            extra=wall("s_lnstat"))
        sch(lambda s: s.activation(rstd[:, :], lnt[:, :], AF.Exp, scale=-0.5),
            incs=[("s_lnrstd", 1)])
        vch(lambda v: v.scalar_tensor_tensor(
            nmr[:, :], lnmean[:, :], -1.0, rstd[:, :], ALU.mult, ALU.mult),
            extra=wall("s_lnrstd"), incs=[("s_lnnmr", 1)])
        sch(lambda s, x=x_ap, o=out_ap: s.activation(
            o, x, AF.Identity, bias=nmr[:, 0:1], scale=rstd[:, 0:1]),
            extra=list(apply_waits) + wall("s_lnnmr"),
            incs=[("s_lnapply", 1)] + list(apply_incs))

    p1c = ExitStack()
    x_own = sbt(p1c, "x_own", [128, NJ, D], F32)
    wq_sb = sbt(p1c, "wq_sb", [128, H, DC, 128], BF16)
    wo_sb = sbt(p1c, "wo_sb", [128, H, DC, 128], BF16)
    wkv_sb = sbt(p1c, "wkv_sb", [128, 2, DC, 128], BF16)
    bq_sb = sbt(p1c, "bq_sb", [1, H, 128], BF16)
    bkv_sb = sbt(p1c, "bkv_sb", [1, 2, 128], BF16)
    K_sb = sbt(p1c, "K_sb", [128, T], BF16)
    V_sb = sbt(p1c, "V_sb", [128, NT, 128], BF16)

    p1a = ExitStack()
    x_kv = sbt(p1a, "x_kv", [128, 2, D], F32)
    pek_sb = sbt(p1a, "pek_sb", [128, 2, 512], F32)
    ln_kv = sbt(p1a, "ln_kv", [128, 2, D], BF16)
    lnT_kv = sbt(p1a, "lnT_kv", [128, 2, DC, 128], BF16)
    vt_tmp = sbt(p1a, "vt_tmp", [128, 2, 128], BF16)
    ptr = p1a.enter_context(nc.psum_tensor("ptr", [128, 2048], F32)).ap()
    pkv = p1a.enter_context(nc.psum_tensor("pkv", [128, 1024], F32)).ap()

    add("sync", lambda s: s.dma_start(idf[:, :], idfin), incs=[("s_c", 16)])
    add("sync", lambda s: s.dma_start(mask_sb[:, :], maskin), incs=[("s_c", 16)])
    add("gpsimd", lambda g: g.dma_start(idb[:, :], idbin), incs=[("s_cg", 16)])
    add("gpsimd", lambda g: g.dma_start(ones_b[:, :], onesin), incs=[("s_cg", 16)])
    add("gpsimd", lambda g: g.dma_start(wq_sb[:, :, :, :], wq), incs=[("s_w", 16)])
    add("gpsimd", lambda g: g.dma_start(wkv_sb[:, :, :, :], wkv), incs=[("s_w", 16)])
    add("gpsimd", lambda g: g.dma_start(wo_sb[:, :, :, :], wo), incs=[("s_w", 16)])
    add("gpsimd", lambda g: g.dma_start(bq_sb[:, :, :], bqr), incs=[("s_w", 16)])
    add("gpsimd", lambda g: g.dma_start(bkv_sb[:, :, :], bkvr), incs=[("s_w", 16)])
    for j in range(NJ):
        add("sync", lambda s, j=j: s.dma_start(x_own[:, j, :], xq[j]),
            incs=[("s_xq", 16)])

    for j in range(NJ):
        jp = j % 2
        add("sync", lambda s, j=j, jp=jp: s.dma_start(
            pek_sb[:, jp, :], pek[:, 512 * j:512 * (j + 1)]),
            waits=[("s_pek_r", j - 1)] if j >= 2 else (),
            incs=[("s_pek%d" % jp, 16)])
        for ti in range(4):
            tt = 4 * j + ti
            tp = tt % 2
            add("sync", lambda s, tt=tt, tp=tp: s.dma_start(
                x_kv[:, tp, :], xk[tt]),
                waits=[("s_xkv_r", tt - 1)] if tt >= 2 else (),
                incs=[("s_xkv%d" % tp, 16)])
            emit_ln(x_kv[:, tp, :], ln_kv[:, tp, :],
                    wall("s_xkv%d" % tp),
                    apply_waits=bld.mwait(("lnkv", tt - 2)),
                    apply_incs=[("s_xkv_r", 1)])
            for half in range(2):
                for c4 in range(4):
                    c = 4 * half + c4
                    w = []
                    if c4 == 0:
                        w = wall("s_ptr_r")
                        if tt == 0 and half == 0:
                            w += wall("s_c", "s_cg")
                        if half == 0:
                            w += wall("s_lnapply")
                    add("tensor", lambda t, tp=tp, c=c, c4=c4: t.matmul(
                        ptr[:, c4 * 512:c4 * 512 + 128],
                        ln_kv[:, tp, c * 128:(c + 1) * 128], idb[:, :],
                        start=True, stop=True),
                        waits=w, incs=[("s_ptr_w", 1)])
                    if half == 1 and c4 == 3:
                        bld.mark(("lnkv", tt), "s_ptr_w")
                add("scalar", lambda s, tp=tp, half=half: s.copy(
                    lnT_kv[:, tp, 4 * half:4 * half + 4, :],
                    ptr.rearrange("p (b w) -> p b w", w=512)[:, :, 0:128]),
                    waits=wall("s_ptr_w"), incs=[("s_ptr_r", 1)])
            for f in range(2):
                for c in range(DC):
                    w = []
                    if f == 0 and c == 0:
                        w = wall("s_ptr_r", "s_w") + \
                            ([("s_kvk_r", tt)] if tt >= 1 else [])
                    if f == 1 and c == 0:
                        w = [("s_kvv_r", tt)] if tt >= 1 else []
                    add("tensor", lambda t, f=f, c=c, tp=tp: t.matmul(
                        pkv[:, f * 512:f * 512 + 128],
                        wkv_sb[:, f, c, :], lnT_kv[:, tp, c, :],
                        start=(c == 0), stop=False), waits=w)
                add("tensor", lambda t, f=f: t.matmul(
                    pkv[:, f * 512:f * 512 + 128],
                    bkv_sb[:, f, :], ones_b[:, :], start=False, stop=True),
                    incs=[("s_kvk_w" if f == 0 else "s_kvv_w", 1)])
            add("vector", lambda v, tt=tt, jp=jp, ti=ti: v.scalar_tensor_tensor(
                K_sb[:, tt * 128:(tt + 1) * 128], pkv[:, 0:128], 1.0,
                pek_sb[:, jp, ti * 128:(ti + 1) * 128], ALU.mult, ALU.add),
                waits=wall("s_kvk_w", "s_pek%d" % jp),
                incs=[("s_kv", 1), ("s_kvk_r", 1)] +
                     ([("s_pek_r", 1)] if ti == 3 else []))
            add("scalar", lambda s, tp=tp: s.copy(
                vt_tmp[:, tp, :], pkv[:, 512:640]),
                waits=wall("s_kvv_w"),
                incs=[("s_vt", 1), ("s_kvv_r", 1)])
            add("tensor", lambda t, tp=tp: t.matmul(
                ptr[:, 0:128], vt_tmp[:, tp, :], idb[:, :],
                start=True, stop=True),
                waits=wall("s_vt", "s_ptr_r"),
                incs=[("s_ptr_w", 1)])
            add("vector", lambda v, tt=tt: v.tensor_copy(
                V_sb[:, tt, :], ptr[:, 0:128]),
                waits=wall("s_ptr_w"),
                incs=[("s_kv", 1), ("s_ptr_r", 1)])

    for e in ENGINES:
        add(e, lambda eng: eng.nop(), incs=[("s_p1a", 1)])

    if stop is not None:
        add("gpsimd", lambda g: g.dma_start(dbgK, K_sb[:, :]),
            waits=[("s_p1a", 5)], incs=[("s_dbg_st", 16)])
        add("gpsimd", lambda g: g.dma_start(
            dbgV, V_sb.rearrange("p a b -> p (a b)")),
            incs=[("s_dbg_st", 16)])
        add("gpsimd", lambda g: g.dma_start(
            dbgL[:, 0:1024], ln_kv[:, 1, :]), incs=[("s_dbg_st", 16)])
        add("gpsimd", lambda g: g.dma_start(
            dbgL[:, 1024:2048],
            lnT_kv[:, 1, :, :].rearrange("p c f -> p (c f)")),
            incs=[("s_dbg_st", 16)])
        add("gpsimd", lambda g: g.dma_start(dbgI[:, 0:128], idb[:, :]),
            incs=[("s_dbg_st", 16)])
        add("gpsimd", lambda g: g.dma_start(dbgI[:, 128:256], vt_tmp[:, 1, :]),
            incs=[("s_dbg_st", 16)])
        if stop == "kv":
            for j in range(NJ):
                add("sync", lambda s, j=j: s.dma_start(
                    outp[j], x_own[:, j, :]),
                    waits=[("s_xq", 16 * NJ)], incs=[("s_out", 16)])
            add("sync", lambda s: s.wait_ge(bld.sem("s_out"), cur("s_out")))
            add("sync", lambda s: s.wait_ge(bld.sem("s_dbg_st"), cur("s_dbg_st")))
            bld.emit_block()
            p1a.close()
            p1c.close()
            ctx.close()
            return nc

    bld.emit_block()
    p1a.close()

    bar1 = wall("s_p1a")
    p1b = ExitStack()
    peo_sb = sbt(p1b, "peo_sb", [128, NJ, 128], BF16)
    scores = sbt(p1b, "scores", [128, T], BF16)
    probs = sbt(p1b, "probs", [128, 2, T], BF16)
    ln_own = sbt(p1b, "ln_own", [128, D], BF16)
    ln1T = sbt(p1b, "ln1T", [128, 2, DC, 128], BF16)
    qT_sb = sbt(p1b, "qT_sb", [128, 2, 128], BF16)
    pT_sb = sbt(p1b, "pT_sb", [128, 2, 512], BF16)
    attnT = sbt(p1b, "attnT", [128, 2, H, 128], BF16)
    nmb = sbt(p1b, "nmb", [128, 16], F32)
    ssum = sbt(p1b, "ssum", [128, 2], F32)
    rs = sbt(p1b, "rs", [128, 1], F32)
    diag = sbt(p1b, "diag", [128, 128], BF16)
    tmpf = sbt(p1b, "tmpf", [128, 512], F32)

    pqk = [pst(p1b, "pqk0"), pst(p1b, "pqk1")]
    pat = [p1b.enter_context(nc.psum_tensor("pat0", [128, 128], F32)).ap(),
           p1b.enter_context(nc.psum_tensor("pat1", [128, 128], F32)).ap()]
    ptq = p1b.enter_context(nc.psum_tensor("ptq", [128, 2048], F32)).ap()

    add("sync", lambda s: s.dma_start(peo_sb[:, :, :], peo),
        waits=bar1, incs=[("s_peo", 16)])

    first1 = {"tensor": True, "vector": True}

    def bw1(eng, extra=()):
        w = list(extra)
        if first1.get(eng):
            w += bar1
            first1[eng] = False
        return w

    _G = globals()
    for j in range(_G.get("ATTN_J", NJ)):
        jp = j % 2
        kext = 512 * (j + 1)
        nkb = j + 1
        emit_ln(x_own[:, j, :], ln_own[:, :], bw1("vector", wall("s_xq")),
                apply_waits=bld.mwait(("lnown", j - 1)))
        for half in range(2):
            for c4 in range(4):
                c = 4 * half + c4
                w = []
                if c4 == 0:
                    w = bw1("tensor", wall("s_ptq_r"))
                    if half == 0:
                        w += wall("s_lnapply")
                add("tensor", lambda t, c=c, c4=c4: t.matmul(
                    ptq[:, c4 * 512:c4 * 512 + 128],
                    ln_own[:, c * 128:(c + 1) * 128], idb[:, :],
                    start=True, stop=True),
                    waits=w, incs=[("s_ptq_w", 1)])
                if half == 1 and c4 == 3:
                    bld.mark(("lnown", j), "s_ptq_w")
            add("scalar", lambda s, jp=jp, half=half: s.copy(
                ln1T[:, jp, 4 * half:4 * half + 4, :],
                ptq.rearrange("p (b w) -> p b w", w=512)[:, :, 0:128]),
                waits=wall("s_ptq_w"), incs=[("s_ptq_r", 1), ("s_ln1T", 1)])

        STG = _G.get("ATTN_STAGE", 5)
        def emit_front(h):
            hp = h % 2
            seq = 8 * j + h
            for c in range(DC):
                w = wall("s_ptq_r", "s_ln1T", "s_w") if c == 0 else []
                add("tensor", lambda t, h=h, c=c, jp=jp: t.matmul(
                    ptq[:, 0:128], wq_sb[:, h, c, :], ln1T[:, jp, c, :],
                    start=(c == 0), stop=False), waits=w)
            add("tensor", lambda t, h=h: t.matmul(
                ptq[:, 0:128], bq_sb[:, h, :], ones_b[:, :],
                start=False, stop=True), incs=[("s_ptq_w", 1)])
            add("vector", lambda v, hp=hp, j=j: v.scalar_tensor_tensor(
                qT_sb[:, hp, :], ptq[:, 0:128], SCALE, peo_sb[:, j, :],
                ALU.mult, ALU.add),
                waits=wall("s_ptq_w", "s_peo") + bld.mwait(("qTr", seq - 2)),
                incs=[("s_qT", 1), ("s_ptq_r", 1)])
            if STG < 3:
                return
            for kb in range(nkb):
                bp = kb % 2
                w = wall("s_qT") + [("s_qkr%d" % bp, cur("s_qkr%d" % bp))]
                if seq == 0 and kb == 0:
                    w += [("s_kv", 2 * NT)]
                add("tensor", lambda t, hp=hp, kb=kb, bp=bp: t.matmul(
                    pqk[bp][:, :], qT_sb[:, hp, :],
                    K_sb[:, kb * 512:(kb + 1) * 512],
                    start=True, stop=True),
                    waits=w, incs=[("s_qkw%d" % bp, 1)])
                if kb == nkb - 1:
                    bld.mark(("qTr", seq), "s_qkw%d" % bp)
                if kb < nkb - 1:
                    add("vector", lambda v, kb=kb, bp=bp: v.tensor_scalar(
                        scores[:, kb * 512:(kb + 1) * 512], pqk[bp][:, :],
                        1.0, None, ALU.mult, op1=ALU.max,
                        accum_out=nmb[:, kb:kb + 1]),
                        waits=wall("s_qkw%d" % bp, "s_exp"),
                        incs=[("s_qkr%d" % bp, 1)])
                else:
                    add("vector", lambda v, kb=kb, bp=bp: v.scalar_tensor_tensor(
                        scores[:, kb * 512:(kb + 1) * 512], pqk[bp][:, :],
                        1.0, mask_sb[:, :], ALU.mult, ALU.add),
                        waits=wall("s_qkw%d" % bp, "s_exp", "s_c"),
                        incs=[("s_qkr%d" % bp, 1)])
                    add("vector", lambda v, kb=kb: v.reduce_max(
                        nmb[:, kb:kb + 1],
                        scores[:, kb * 512:(kb + 1) * 512], axis=AX),
                        waits=[("s_qkr%d" % bp, cur("s_qkr%d" % bp))])
            add("vector", lambda v, nkb=nkb: v.reduce_max(
                nm[:, :], nmb[:, 0:nkb], axis=AX, negate=True),
                waits=wall("s_qkr0", "s_qkr1"),
                incs=[("s_nm", 1)])

        def emit_exp(h):
            hp = h % 2
            seq = 8 * j + h
            add("scalar", lambda s, kext=kext, hp=hp: s.activation(
                probs[:, hp, 0:kext], scores[:, 0:kext], AF.Exp,
                bias=nm[:, 0:1], accum_out=ssum[:, hp:hp + 1]),
                waits=[("s_nm", seq + 1)] + bld.mwait(("T", seq - 2)),
                incs=[("s_exp", 1)])

        def emit_tail(h):
            hp = h % 2
            seq = 8 * j + h
            add("vector", lambda v, hp=hp: v.reciprocal(
                rs[:, :], ssum[:, hp:hp + 1]),
                waits=[("s_exp", seq + 1)], incs=[("s_rs", 1)])
            add("vector", lambda v: v.tensor_scalar(
                diag[:, :], idb[:, :], rs[:, 0:1], None, ALU.mult),
                waits=[("s_rs", seq + 1)] + bld.mwait(("T", seq - 1)) +
                      (wall("s_cg") if seq == 0 else []),
                incs=[("s_diag", 1)])
            if STG < 4:
                return
            SUB = _G.get("ATTN_SUB", 3)
            for kb in range(nkb):
                tp = kb % 2
                for c4 in range(4):
                    w = []
                    if c4 == 0:
                        w = [("s_diag", seq + 1), ("s_exp", seq + 1)] + \
                            wall("s_ptq_r")
                    add("tensor", lambda t, kb=kb, c4=c4, hp=hp: t.matmul(
                        ptq[:, c4 * 512:c4 * 512 + 128],
                        probs[:, hp, kb * 512 + c4 * 128:kb * 512 + (c4 + 1) * 128],
                        diag[:, :], start=True, stop=True),
                        waits=w, incs=[("s_ptq_w", 1)])
                    if kb == nkb - 1 and c4 == 3:
                        bld.mark(("T", seq), "s_ptq_w")
                add("vector", lambda v, tp=tp: v.tensor_copy(
                    pT_sb[:, tp, :],
                    ptq.rearrange("p (b w) -> p b w", w=512)[:, :, 0:128]),
                    waits=wall("s_ptq_w") +
                          [("s_pta%d" % tp, cur("s_pta%d" % tp))],
                    incs=[("s_ptq_r", 1), ("s_pte%d" % tp, 1)])
                if SUB < 2:
                    continue
                for c4 in range(4):
                    w = []
                    if c4 == 0:
                        w = wall("s_pte%d" % tp)
                        if kb == 0:
                            w += [("s_atr%d" % hp, cur("s_atr%d" % hp))]
                    add("tensor", lambda t, kb=kb, c4=c4, tp=tp, hp=hp,
                        last=(kb == nkb - 1): t.matmul(
                        pat[hp][:, :], V_sb[:, 4 * kb + c4, :],
                        pT_sb[:, tp, c4 * 128:(c4 + 1) * 128],
                        start=(kb == 0 and c4 == 0),
                        stop=(last and c4 == 3)),
                        waits=w,
                        incs=[("s_pta%d" % tp, 1)] if c4 == 3 else [])
            if SUB >= 3:
                add("vector", lambda v, jp=jp, h=h, hp=hp: v.tensor_copy(
                    attnT[:, jp, h, :], pat[hp][:, :]),
                    waits=wall("s_pta0", "s_pta1"),
                    incs=[("s_atr%d" % hp, 1), ("s_at", 1)])

        STG = _G.get("ATTN_STAGE", 5)
        for h in range(H if STG >= 2 else 0):
            emit_front(h)
            if STG >= 3 and h >= 1:
                emit_tail(h - 1)
            if STG >= 3:
                emit_exp(h)
        if STG >= 3:
            emit_tail(H - 1)

        for half in range(2 if STG >= 5 else 0):
            for d4 in range(4):
                dtile = 4 * half + d4
                for h in range(H):
                    w = []
                    if d4 == 0 and h == 0:
                        w = wall("s_at", "s_ptq_r")
                    add("tensor", lambda t, h=h, dtile=dtile, d4=d4, jp=jp: t.matmul(
                        ptq[:, d4 * 512:d4 * 512 + 128],
                        wo_sb[:, h, dtile, :], attnT[:, jp, h, :],
                        start=(h == 0), stop=(h == H - 1)),
                        waits=w,
                        incs=[("s_ptq_w", 1)] if (d4 == 3 and h == H - 1) else ())
            add("scalar", lambda s: s.copy(
                tmpf[:, :],
                ptq.rearrange("p (b w) -> p b w", w=512)[:, :, 0:128]),
                waits=wall("s_ptq_w"),
                incs=[("s_ptq_r", 1), ("s_tmpf", 1)])
            for d4 in range(4):
                w = []
                if d4 == 0:
                    w = wall("s_tmpf", "s_ptq_r")
                add("tensor", lambda t, d4=d4: t.matmul(
                    ptq[:, d4 * 512:d4 * 512 + 128],
                    tmpf[:, d4 * 128:(d4 + 1) * 128], idf[:, :],
                    start=True, stop=True),
                    waits=w, incs=[("s_ptq_w", 1)])
            add("vector", lambda v, j=j, half=half: v.scalar_tensor_tensor(
                x_own[:, j, half * 512:(half + 1) * 512].rearrange(
                    "p (c f) -> p c f", f=128),
                ptq.rearrange("p (b w) -> p b w", w=512)[:, :, 0:128], 1.0,
                x_own[:, j, half * 512:(half + 1) * 512].rearrange(
                    "p (c f) -> p c f", f=128),
                ALU.mult, ALU.add),
                waits=wall("s_ptq_w"),
                incs=[("s_ptq_r", 1), ("s_res", 1)])

    if stop == "attn":
        for j in range(NJ):
            add("sync", lambda s, j=j: s.dma_start(outp[j], x_own[:, j, :]),
                waits=wall("s_res") + [("s_xq", 16 * NJ)], incs=[("s_out", 16)])
        add("sync", lambda s: s.wait_ge(bld.sem("s_out"), cur("s_out")))
        add("sync", lambda s: s.wait_ge(bld.sem("s_dbg_st"), cur("s_dbg_st")))
        bld.emit_block()
        p1b.close()
        p1c.close()
        ctx.close()
        return nc

    add("sync", lambda s: s.dma_start(yspill, x_own[:, :, :]),
        waits=wall("s_res"), incs=[("s_yspill", 16)])
    for e in ENGINES:
        add(e, lambda eng: eng.nop(), incs=[("s_p1", 1)])

    bld.emit_block()
    p1b.close()
    p1c.close()

    bar = wall("s_p1", "s_yspill")
    p2 = ExitStack()
    ln2T = sbt(p2, "ln2T", [128, DC, 1024], BF16)
    h_sb = sbt(p2, "h_sb", [128, FT, 1024], BF16)
    yffn = sbt(p2, "yffn", [128, DC, 1024], F32)
    wg_sb = sbt(p2, "wg_sb", [128, 2, 4, DC, 128], BF16)
    wu_sb = sbt(p2, "wu_sb", [128, 2, 4, DC, 128], BF16)
    wd_sb = sbt(p2, "wd_sb", [128, FT, DC, 128], BF16)
    pb_sb = sbt(p2, "pb_sb", [128, 2, 1024], BF16)
    pf_sb = sbt(p2, "pf_sb", [128, 8, NE], BF16)
    prT = sbt(p2, "prT", [8, 1024], BF16)
    y_rot = sbt(p2, "y_rot", [128, 2, D], F32)
    ln2_tmp = sbt(p2, "ln2_tmp", [128, D], BF16)
    silu_sb = sbt(p2, "silu_sb", [128, 512], BF16)
    u_sb = sbt(p2, "u_sb", [128, 512], BF16)
    h1_sb = sbt(p2, "h1_sb", [128, 512], BF16)
    fin_sb = sbt(p2, "fin_sb", [128, 2, D], F32)
    rw_sb = sbt(p2, "rw_sb", [128, DC, NE], BF16)
    esel_sb = sbt(p2, "esel_sb", [8, NE, 128], BF16)
    bd_sb = sbt(p2, "bd_sb", [8, DC, 128], BF16)
    bgT_sb = sbt(p2, "bgT_sb", [128, NE, FT], F32)
    buT_sb = sbt(p2, "buT_sb", [128, NE, FT], F32)
    l_sb = sbt(p2, "l_sb", [128, NE], F32)
    E_sb = sbt(p2, "E_sb", [128, NE], F32)
    t8_sb = sbt(p2, "t8_sb", [128, 8], F32)
    keep_sb = sbt(p2, "keep_sb", [128, NE], F32)
    pm_sb = sbt(p2, "pm_sb", [128, NE], F32)
    Z_sb = sbt(p2, "Z_sb", [128, 1], F32)
    S2_sb = sbt(p2, "S2_sb", [128, 1], F32)
    den_sb = sbt(p2, "den_sb", [128, 1], F32)
    rden_sb = sbt(p2, "rden_sb", [128, 1], F32)

    pg = [pst(p2, "pg0"), pst(p2, "pg1")]
    pu = [pst(p2, "pu0"), pst(p2, "pu1")]
    py = [pst(p2, "py0"), pst(p2, "py1")]
    pxq = p2.enter_context(nc.psum_tensor("pxq", [128, 1024], F32)).ap()
    pxv = pxq.rearrange("p (b w) -> p b w", w=512)

    add("sync", lambda s: s.dma_start(bgT_sb[:, :, :], bgT),
        waits=bar, incs=[("s_c2", 16)])
    add("sync", lambda s: s.dma_start(buT_sb[:, :, :], buT), incs=[("s_c2", 16)])
    add("gpsimd", lambda g: g.dma_start(rw_sb[:, :, :], rw),
        waits=bar, incs=[("s_c2g", 16)])
    add("gpsimd", lambda g: g.dma_start(esel_sb[:, :, :], eselin),
        incs=[("s_c2g", 16)])
    add("gpsimd", lambda g: g.dma_start(bd_sb[:, :, :], bdr),
        incs=[("s_c2g", 16)])

    first = {"tensor": True, "vector": True}

    def bw(eng, extra=()):
        w = list(extra)
        if first.get(eng):
            w += bar
            first[eng] = False
        return w

    for g in range(2):
        for jl in range(8):
            jt = g * 8 + jl
            tp = jt % 2
            yn = (cur("s_y0") + cur("s_y1")) // 16
            w = ([("s_y_r", yn - 1)] if yn >= 2 else []) + \
                (wall("s_yspill") if (g == 0 and jl == 0) else [])
            add("sync", lambda s, jt=jt, tp=tp: s.dma_start(
                y_rot[:, tp, :], yspill[:, jt, :]), waits=w,
                incs=[("s_y%d" % tp, 16)])
            emit_ln(y_rot[:, tp, :], ln2_tmp[:, :],
                    bw("vector", wall("s_y%d" % tp)),
                    apply_waits=bld.mwait(("ln2tmp", jt - 1)),
                    apply_incs=[("s_y_r", 1)])
            for pr in range(4):
                for c2 in range(2):
                    c = 2 * pr + c2
                    w = []
                    if c2 == 0:
                        w = bw("tensor", wall("s_pxq_r"))
                        if pr == 0:
                            w += wall("s_lnapply")
                    add("tensor", lambda t, c=c, c2=c2: t.matmul(
                        pxq[:, c2 * 512:c2 * 512 + 128],
                        ln2_tmp[:, c * 128:(c + 1) * 128], idb[:, :],
                        start=True, stop=True),
                        waits=w, incs=[("s_pxq_w", 1)])
                    if pr == 3 and c2 == 1:
                        bld.mark(("ln2tmp", jt), "s_pxq_w")
                add("scalar", lambda s, jl=jl, pr=pr: s.copy(
                    ln2T[:, 2 * pr:2 * pr + 2, jl * 128:(jl + 1) * 128],
                    pxv[:, :, 0:128]),
                    waits=wall("s_pxq_w"),
                    incs=[("s_pxq_r", 1), ("s_ln2T", 1)])

        for jl in range(8):
            for c in range(DC):
                w = wall("s_pxq_r", "s_c2g") if c == 0 else []
                add("tensor", lambda t, jl=jl, c=c: t.matmul(
                    pxq[:, 0:NE], ln2T[:, c, jl * 128:(jl + 1) * 128],
                    rw_sb[:, c, :], start=(c == 0), stop=(c == DC - 1)),
                    waits=w, incs=[("s_pxq_w", 1)] if c == DC - 1 else ())
            vch(lambda v: v.tensor_copy(l_sb[:, :], pxq[:, 0:NE]),
                extra=wall("s_pxq_w", "s_rexp"), incs=[("s_pxq_r", 1)])
            vch(lambda v: v.reduce_max(nm[:, :], l_sb[:, :],
                                       axis=AX, negate=True),
                incs=[("s_rnm", 1)])
            sch(lambda s: s.activation(
                E_sb[:, :], l_sb[:, :], AF.Exp, bias=nm[:, 0:1],
                accum_out=Z_sb[:, 0:1]),
                extra=wall("s_rnm"), incs=[("s_rexp", 1)])
            vch(lambda v: v.max(t8_sb[:, :], E_sb[:, :]),
                extra=wall("s_rexp"))
            vch(lambda v: v.tensor_scalar(
                keep_sb[:, :], E_sb[:, :], t8_sb[:, 1:2], None, ALU.is_ge))
            vch(lambda v: v.tensor_mul(pm_sb[:, :], E_sb[:, :], keep_sb[:, :]))
            vch(lambda v: v.reduce_sum(S2_sb[:, :], pm_sb[:, :], axis=AX))
            vch(lambda v: v.scalar_tensor_tensor(
                den_sb[:, :], Z_sb[:, :], 1e-9, S2_sb[:, :],
                ALU.mult, ALU.add))
            vch(lambda v: v.reciprocal(rden_sb[:, :], den_sb[:, :]))
            vch(lambda v, jl=jl: v.tensor_scalar(
                pf_sb[:, jl, :], pm_sb[:, :], rden_sb[:, 0:1], None, ALU.mult),
                incs=[("s_rout", 1)])
        for st in range(4):
            for q2 in range(2):
                jl = 2 * st + q2
                w = wall("s_rout", "s_pxq_r") if q2 == 0 else []
                add("tensor", lambda t, jl=jl, q2=q2: t.matmul(
                    pxq[0:8, q2 * 512:q2 * 512 + 128],
                    pf_sb[:, jl, :], idb[:, :], start=True, stop=True),
                    waits=w, incs=[("s_pxq_w", 1)])
            add("scalar", lambda s, st=st: s.copy(
                prT[:, st * 256:(st + 1) * 256],
                pxv[0:8, :, 0:128]),
                waits=wall("s_pxq_w"),
                incs=[("s_pxq_r", 1), ("s_prT", 1)])

        for e in range(NE):
            ep = e % 2
            seq = g * NE + e
            def emit_wgu_dma(q, e=e, seq=seq):
                sG = 4 * seq + q
                sp = sG % 2
                assert sG < 2 or ("wgr", sG - 2) in bld.marks
                add("gpsimd", lambda gg, e=e, q=q, sp=sp: gg.dma_start(
                    wg_sb[:, sp, :, :, :], wg[e][:, 4 * q:4 * q + 4, :, :]),
                    waits=bld.mwait(("wgr", sG - 2)) +
                          (bar if sG == 0 else []),
                    incs=[("s_wg%d" % sp, 16)])
                add("gpsimd", lambda gg, e=e, q=q, sp=sp: gg.dma_start(
                    wu_sb[:, sp, :, :, :], wu[e][:, 4 * q:4 * q + 4, :, :]),
                    waits=bld.mwait(("wur", sG - 2)),
                    incs=[("s_wu%d" % sp, 16)])
            emit_wgu_dma(0)
            emit_wgu_dma(1)
            add("gpsimd", lambda gg, e=e: gg.dma_start(
                wd_sb[:, :, :, :], wd[e]),
                waits=bld.mwait(("wdr", seq - 1)),
                incs=[("s_wd", 16)])
            for half in range(2):
                w = wall("s_prT", "s_c2g")
                if half == 0:
                    w += wall("s_pxq_r")
                add("tensor", lambda t, e=e, half=half: t.matmul(
                    pxq[:, half * 512:(half + 1) * 512], esel_sb[:, e, :],
                    prT[:, half * 512:(half + 1) * 512],
                    start=True, stop=True),
                    waits=w,
                    incs=[("s_pxq_w", 1)])
                add("scalar", lambda s, ep=ep, half=half: s.copy(
                    pb_sb[:, ep, half * 512:(half + 1) * 512],
                    pxq[:, half * 512:(half + 1) * 512]),
                    waits=wall("s_pxq_w"),
                    incs=[("s_pxq_r", 1), ("s_pb", 1)])
            for ft in range(FT):
                q = ft // 4
                sG = 4 * seq + q
                sp = sG % 2
                for nh in range(2):
                    pgb, pub = pg[nh], pu[nh]
                    ng, nu = "s_pg%d" % nh, "s_pu%d" % nh
                    for c in range(DC):
                        w = []
                        if c == 0:
                            w = wall(ng + "_r")
                            if ft % 4 == 0 and nh == 0:
                                w += [("s_wg%d" % sp, 16 * (sG // 2 + 1))]
                            if ft == 0 and nh == 0:
                                w += wall("s_ln2T")
                        add("tensor", lambda t, sp=sp, ft=ft, c=c, nh=nh,
                            pgb=pgb: t.matmul(
                            pgb[:, :], wg_sb[:, sp, ft % 4, c, :],
                            ln2T[:, c, nh * 512:(nh + 1) * 512],
                            start=(c == 0), stop=(c == DC - 1)),
                            waits=w,
                            incs=[(ng + "_w", 1)] if c == DC - 1 else [])
                        if ft % 4 == 3 and nh == 1 and c == DC - 1:
                            bld.mark(("wgr", sG), ng + "_w")
                    for c in range(DC):
                        w = []
                        if c == 0:
                            w = wall(nu + "_r")
                            if ft % 4 == 0 and nh == 0:
                                w += [("s_wu%d" % sp, 16 * (sG // 2 + 1))]
                        add("tensor", lambda t, sp=sp, ft=ft, c=c, nh=nh,
                            pub=pub: t.matmul(
                            pub[:, :], wu_sb[:, sp, ft % 4, c, :],
                            ln2T[:, c, nh * 512:(nh + 1) * 512],
                            start=(c == 0), stop=(c == DC - 1)),
                            waits=w,
                            incs=[(nu + "_w", 1)] if c == DC - 1 else [])
                        if ft % 4 == 3 and nh == 1 and c == DC - 1:
                            bld.mark(("wur", sG), nu + "_w")
                            if ft < 8:
                                emit_wgu_dma(ft // 4 + 2)
                    add("scalar", lambda s, e=e, ft=ft, pgb=pgb: s.activation(
                        silu_sb[:, :], pgb[:, :], AF.Silu,
                        bias=bgT_sb[:, e, ft:ft + 1]),
                        waits=wall(ng + "_w", "s_h1", "s_c2"),
                        incs=[("s_silu", 1), (ng + "_r", 1)])
                    add("scalar", lambda s, e=e, ft=ft, pub=pub: s.activation(
                        u_sb[:, :], pub[:, :], AF.Identity,
                        bias=buT_sb[:, e, ft:ft + 1]),
                        waits=wall(nu + "_w", "s_h1"),
                        incs=[("s_usb", 1), (nu + "_r", 1)])
                    add("vector", lambda v: v.tensor_mul(
                        h1_sb[:, :], silu_sb[:, :], u_sb[:, :]),
                        waits=bw("vector", wall("s_silu", "s_usb")),
                        incs=[("s_h1", 1)])
                    add("vector", lambda v, ep=ep, ft=ft, nh=nh: v.tensor_mul(
                        h_sb[:, ft, nh * 512:(nh + 1) * 512], h1_sb[:, :],
                        pb_sb[:, ep, nh * 512:(nh + 1) * 512]),
                        waits=wall("s_pb", "s_h1") +
                              bld.mwait(("wdr", seq - 1)),
                        incs=[("s_h", 1)])
            for dtile in range(DC):
                for nh in range(2):
                    pyp = nh
                    pyb = py[pyp]
                    nyp = "s_py%d" % pyp
                    if e == 0:
                        add("tensor", lambda t, dtile=dtile, nh=nh, pyb=pyb: t.matmul(
                            pyb[:, :], bd_sb[:, dtile, :],
                            prT[:, nh * 512:(nh + 1) * 512],
                            start=True, stop=False),
                            waits=wall(nyp + "_r", "s_prT", "s_c2g"))
                    for fc in range(FT):
                        w = []
                        if fc == 0:
                            if e != 0:
                                w = wall(nyp + "_r")
                            if dtile == 0 and nh == 0:
                                w += wall("s_h") + [("s_wd", 16 * (seq + 1))]
                        add("tensor", lambda t, fc=fc, dtile=dtile,
                            nh=nh, pyb=pyb, e=e: t.matmul(
                            pyb[:, :], wd_sb[:, fc, dtile, :],
                            h_sb[:, fc, nh * 512:(nh + 1) * 512],
                            start=(fc == 0 and e != 0), stop=(fc == FT - 1)),
                            waits=w,
                            incs=[(nyp + "_w", 1)] if fc == FT - 1 else [])
                        if fc == FT - 1 and dtile == DC - 1 and nh == 1:
                            bld.mark(("wdr", seq), nyp + "_w")
                    if e == 0:
                        add("vector", lambda v, dtile=dtile, nh=nh, pyb=pyb:
                            v.tensor_copy(
                            yffn[:, dtile, nh * 512:(nh + 1) * 512], pyb[:, :]),
                            waits=wall(nyp + "_w") +
                                  bld.mwait(("finT", g - 1)),
                            incs=[(nyp + "_r", 1), ("s_yf", 1)])
                    else:
                        prevyf = g * 128 + (e - 1) * 16 + dtile * 2 + nh + 1
                        add("vector", lambda v, dtile=dtile, nh=nh, pyb=pyb:
                            v.scalar_tensor_tensor(
                            yffn[:, dtile, nh * 512:(nh + 1) * 512],
                            pyb[:, :], 1.0,
                            yffn[:, dtile, nh * 512:(nh + 1) * 512],
                            ALU.mult, ALU.add),
                            waits=wall(nyp + "_w") + [("s_yf", prevyf)],
                            incs=[(nyp + "_r", 1), ("s_yf", 1)])

        for jl in range(8):
            jt = g * 8 + jl
            tp = jt % 2
            yn = (cur("s_y0") + cur("s_y1")) // 16
            add("sync", lambda s, jt=jt, tp=tp: s.dma_start(
                y_rot[:, tp, :], yspill[:, jt, :]),
                waits=[("s_y_r", yn - 1)], incs=[("s_y%d" % tp, 16)])
            for pr in range(4):
                for d2 in range(2):
                    dtile = 2 * pr + d2
                    w = []
                    if d2 == 0:
                        w = wall("s_pxq_r")
                        if jl == 0 and pr == 0:
                            w += wall("s_yf")
                    add("tensor", lambda t, dtile=dtile, d2=d2, jl=jl: t.matmul(
                        pxq[:, d2 * 512:d2 * 512 + 128],
                        yffn[:, dtile, jl * 128:(jl + 1) * 128], idf[:, :],
                        start=True, stop=True),
                        waits=w, incs=[("s_pxq_w", 1)])
                    if jl == 7 and pr == 3 and d2 == 1:
                        bld.mark(("finT", g), "s_pxq_w")
                add("vector", lambda v, tp=tp, pr=pr:
                    v.scalar_tensor_tensor(
                    fin_sb[:, tp, pr * 256:(pr + 1) * 256].rearrange(
                        "p (c f) -> p c f", f=128),
                    pxv[:, :, 0:128], 1.0,
                    y_rot[:, tp, pr * 256:(pr + 1) * 256].rearrange(
                        "p (c f) -> p c f", f=128),
                    ALU.mult, ALU.add),
                    waits=wall("s_pxq_w", "s_y%d" % tp) +
                          ([("s_out%d" % tp, 16 * (jt // 2))] if (jt >= 2 and pr == 0) else []),
                    incs=[("s_pxq_r", 1), ("s_fin", 1)] +
                         ([("s_y_r", 1)] if pr == 3 else []))
            add("sync", lambda s, jt=jt, tp=tp: s.dma_start(
                outp[jt], fin_sb[:, tp, :]),
                waits=wall("s_fin"), incs=[("s_out%d" % tp, 16)])

    if stop is not None:
        add("gpsimd", lambda g: g.dma_start(
            dbgF, ln2T.rearrange("p a b -> p (a b)")),
            waits=wall("s_yf"), incs=[("s_dbg2", 16)])
        add("gpsimd", lambda g: g.dma_start(
            dbgY, yffn.rearrange("p a b -> p (a b)")),
            incs=[("s_dbg2", 16)])
        add("gpsimd", lambda g: g.dma_start(
            dbgP[:, 0:2048], pb_sb.rearrange("p a b -> p (a b)")),
            incs=[("s_dbg2", 16)])
        add("gpsimd", lambda g: g.dma_start(
            dbgP[0:8, 2048:3072], prT[:, :]), incs=[("s_dbg2", 16)])
        add("gpsimd", lambda g: g.dma_start(
            dbgP[:, 3072:3136], pf_sb.rearrange("p a b -> p (a b)")),
            incs=[("s_dbg2", 16)])
        add("gpsimd", lambda g: g.dma_start(
            dbgH, h_sb.rearrange("p a b -> p (a b)")),
            incs=[("s_dbg2", 16)])
        add("sync", lambda s: s.wait_ge(bld.sem("s_dbg2"), cur("s_dbg2")))
    add("sync", lambda s: s.wait_ge(bld.sem("s_out0"), cur("s_out0")))
    add("sync", lambda s: s.wait_ge(bld.sem("s_out1"), cur("s_out1")))

    bld.emit_block()
    p2.close()
    ctx.close()
    return nc



def _pe_table():
    pos = np.arange(T, dtype=np.float32)[:, None]
    inv = np.exp(-np.float32(math.log(10000.0)) *
                 np.arange(0, HD, 2, dtype=np.float32) / np.float32(HD))
    ang = pos * inv[None, :].astype(np.float32)
    pe = np.zeros((T, HD), dtype=np.float32)
    pe[:, 0::2] = np.sin(ang)
    pe[:, 1::2] = np.cos(ang)
    return pe


def stage_inputs(L0_attn_x, L0_attn_Wq, L0_attn_bq, L0_attn_Wkv, L0_attn_bkv,
                 L0_attn_Wo, L0_ffn_router_w, L0_ffn_gate_w, L0_ffn_gate_b,
                 L0_ffn_up_w, L0_ffn_up_b, L0_ffn_down_w, L0_ffn_down_b):
    f32 = lambda a: np.ascontiguousarray(np.asarray(a), dtype=np.float32)
    x = f32(L0_attn_x)
    Wq, bq, Wkv, bkv, Wo = map(f32, (L0_attn_Wq, L0_attn_bq, L0_attn_Wkv,
                                     L0_attn_bkv, L0_attn_Wo))
    rwv, gw, gb, uw, ub, dw, db = map(f32, (
        L0_ffn_router_w, L0_ffn_gate_w, L0_ffn_gate_b, L0_ffn_up_w,
        L0_ffn_up_b, L0_ffn_down_w, L0_ffn_down_b))

    C = np.ascontiguousarray
    shared = {
        "wq": C(Wq.reshape(DC, 128, H, 128).transpose(1, 2, 0, 3)),
        "wkv": C(Wkv.reshape(DC, 128, 2, 128).transpose(1, 2, 0, 3)),
        "wo": C(Wo.reshape(H, 128, DC, 128).transpose(1, 0, 2, 3)),
        "wg": C(gw.reshape(NE, DC, 128, FT, 128).transpose(0, 2, 3, 1, 4)),
        "wu": C(uw.reshape(NE, DC, 128, FT, 128).transpose(0, 2, 3, 1, 4)),
        "wd": C(dw.reshape(NE, FT, 128, DC, 128).transpose(0, 2, 1, 3, 4)),
        "rw": C(rwv.reshape(NE, DC, 128).transpose(2, 1, 0)),
        "bqr": C(bq.reshape(1, H, 128)),
        "bkvr": C(bkv.reshape(1, 2, 128)),
        "bgT": C(gb.reshape(NE, FT, 128).transpose(2, 0, 1)),
        "buT": C(ub.reshape(NE, FT, 128).transpose(2, 0, 1)),
        "bdr": C(db.reshape(NE, DC, 128)),
        "idbin": np.eye(128, dtype=np.float32),
        "idfin": np.eye(128, dtype=np.float32),
        "onesin": np.ones((1, 128), dtype=np.float32),
        "eselin": C(np.eye(NE, dtype=np.float32)[:, :, None]
                    * np.ones((1, 1, 128), np.float32)),
    }
    pe = _pe_table()
    shared["pek"] = C(pe.T)
    pes = (pe * np.float32(SCALE)).astype(np.float32)

    in_maps = []
    for c in range(8):
        g, r = c // 4, c % 4
        xg = x[g].reshape(NT, 128, D)
        qi = np.arange(128, dtype=np.float32)[:, None]
        kb = np.arange(512, dtype=np.float32)[None, :]
        maskv = np.where(kb <= r * 128 + qi, 0.0, NEG).astype(np.float32)
        own = [4 * j + r for j in range(NJ)]
        peov = C(np.stack([pes[t * 128:(t + 1) * 128, :].T for t in own]
                          ).transpose(1, 0, 2))
        m = dict(shared)
        m["xk"] = C(xg)
        m["xq"] = C(xg[own])
        m["peo"] = peov.astype(ml_dtypes.bfloat16)
        m["maskin"] = maskv
        in_maps.append(m)
    return in_maps


def kernel(**inputs):
    in_maps = stage_inputs(**inputs)
    nc = build_nc()
    res = run_bass_kernel_spmd(nc, in_maps, core_ids=list(range(8)))
    out = np.empty((B, T, D), dtype=np.float32)
    for c in range(8):
        g, r = c // 4, c % 4
        out[g].reshape(NT, 128, D)[r::4] = res.results[c]["out"]
    return out


if __name__ == "__main__":
    import reference
    inputs = reference.setup_inputs()
    got = kernel(**{k: np.asarray(v) for k, v in inputs.items()})
    want = np.asarray(reference.reference(**inputs))
    err = np.sqrt(((got - want) ** 2).mean() / (want ** 2).mean())
    print("resvar-rel-l2", err)
